# version 05
# baseline (speedup 1.0000x reference)
import numpy as np
import concourse.bass as bass
import concourse.mybir as mybir
from concourse.bass_utils import run_bass_kernel_spmd

F32 = mybir.dt.float32
BF16 = mybir.dt.bfloat16
AF = mybir.ActivationFunctionType
ALU = mybir.AluOpType
AX = mybir.AxisListType

T = 4096
C = 1024
NCH = C // 128
NTB = T // 512
NTC = T // 128

_DEBUG_TAPS = ()

COMPUTE_ENGINES = ("tensor", "scalar", "vector", "gpsimd")
DMA_QUEUES = ("sync", "gpsimd", "scalar")


_SLOTTED = ("pbig", "WA", "WB", "P0", "P1")


def _slot_keys(name, x):
    pref = "pb" if name == "pbig" else name
    row = 1
    for d in x.tensor.shape[1:]:
        row *= d
    off = x.offset % row
    width = 1
    for d in x.shape[1:]:
        width *= d
    return [f"{pref}{i}" for i in range(off // 512, (off + width - 1) // 512 + 1)]


def _keys(x):
    if isinstance(x, tuple):
        ap = x[0]
        t = getattr(ap, "tensor", None)
        if t is not None and t.name in _SLOTTED:
            return _slot_keys(t.name, ap)
        return [x[1]]
    t = getattr(x, "tensor", None)
    if t is None:
        if x.name in _SLOTTED:
            return _slot_keys(x.name, x[tuple(slice(None) for _ in x.shape)])
        return [x.name]
    if t.name in _SLOTTED:
        return _slot_keys(t.name, x)
    return [t.name]


def _ap(x):
    if isinstance(x, tuple):
        x = x[0]
    if getattr(x, "tensor", None) is None and hasattr(x, "shape"):
        return x[tuple(slice(None) for _ in x.shape)]
    return x


class Op:
    __slots__ = ("eng", "fn", "reads", "writes", "dma", "qidx", "eidx",
                 "waits", "marked", "cnt")

    def __init__(self, eng, fn, reads, writes, dma):
        self.eng = eng
        self.fn = fn
        self.reads = reads
        self.writes = writes
        self.dma = dma
        self.waits = []
        self.marked = False
        self.cnt = 0


class TR:

    def __init__(self, nc):
        self.nc = nc
        self.ops = []
        self.uid = 0

    def name(self, base):
        self.uid += 1
        return f"{base}_{self.uid}"

    def arena(self, name, shape, dtype):
        if not hasattr(self, "_arenas"):
            self._arenas = {}
        if name not in self._arenas:
            self._arenas[name] = self.nc.alloc_sbuf_tensor(name, shape, dtype)
        return self._arenas[name]

    def constcol(self, val):
        if not hasattr(self, "_consts"):
            self._consts = {}
        if val not in self._consts:
            t = self.nc.alloc_sbuf_tensor(self.name(f"cc_{val}"), [128, 1], F32)
            self.memset(t, float(val))
            self._consts[val] = t
        return self._consts[val]

    def _rec(self, eng, fn, reads, writes, dma=False, xreads=(), xwrites=()):
        rk = [k for r in reads for k in _keys(r)] + list(xreads)
        wk = [k for w in writes for k in _keys(w)] + list(xwrites)
        self.ops.append(Op(eng, fn, rk, wk, dma))

    def mm(self, out, lhsT, rhs, start=True, stop=True):
        o, l, r = _ap(out), _ap(lhsT), _ap(rhs)
        self._rec("tensor", lambda e: e.matmul(o, l, r, start=start, stop=stop),
                  [lhsT, rhs], [out])

    def transpose(self, out, in_, ident):
        o, i, d = _ap(out), _ap(in_), _ap(ident)
        self._rec("tensor", lambda e: e.transpose(o, i, d), [in_, ident], [out])

    def act(self, out, in_, func, bias=0.0, scale=1.0, accum_out=None,
            xreads=(), xwrites=()):
        if isinstance(bias, (int, float)) and float(bias) != 0.0:
            p = _ap(out).shape[0]
            bias = self.constcol(float(bias))[0:p, 0:1]
        o, i = _ap(out), _ap(in_)
        b = _ap(bias) if not isinstance(bias, (int, float)) else float(bias)
        s = _ap(scale) if not isinstance(scale, (int, float)) else float(scale)
        ao = _ap(accum_out) if accum_out is not None else None
        reads = [in_] + [x for x in (bias, scale) if not isinstance(x, (int, float))]
        writes = [out] + ([accum_out] if accum_out is not None else [])
        self._rec("scalar",
                  lambda e: e.activation(o, i, func, bias=b, scale=s, accum_out=ao),
                  reads, writes, xreads=xreads, xwrites=xwrites)

    def s_copy(self, out, in_):
        o, i = _ap(out), _ap(in_)
        self._rec("scalar", lambda e: e.copy(o, i), [in_], [out])

    def square(self, out, in_):
        o, i = _ap(out), _ap(in_)
        self._rec("scalar", lambda e: e.square(o, i), [in_], [out])

    def tt(self, out, in0, in1, op, eng="vector"):
        o, a, b = _ap(out), _ap(in0), _ap(in1)
        self._rec(eng, lambda e: e.tensor_tensor(o, a, b, op), [in0, in1], [out])

    def ts(self, out, in0, s1, s2, op0, op1=ALU.bypass, eng="vector"):
        o, a = _ap(out), _ap(in0)
        v1 = _ap(s1) if not isinstance(s1, (int, float)) else float(s1)
        v2 = (_ap(s2) if not isinstance(s2, (int, float)) else
              (float(s2) if s2 is not None else None))
        reads = [in0] + [x for x in (s1, s2) if x is not None and not isinstance(x, (int, float))]
        self._rec(eng, lambda e: e.tensor_scalar(o, a, v1, v2, op0, op1), reads, [out])

    def stt(self, out, in0, scalar, in1, op0, op1, eng="vector"):
        o, a, b = _ap(out), _ap(in0), _ap(in1)
        s = _ap(scalar) if not isinstance(scalar, (int, float)) else float(scalar)
        reads = [in0, in1] + ([scalar] if not isinstance(scalar, (int, float)) else [])
        self._rec(eng, lambda e: e.scalar_tensor_tensor(o, a, s, b, op0, op1),
                  reads, [out])

    def reduce(self, out, in_, op, negate=False, eng="vector", xreads=()):
        o, i = _ap(out), _ap(in_)
        self._rec(eng, lambda e: e.tensor_reduce(o, i, AX.X, op, negate=negate),
                  [in_], [out], xreads=xreads)

    def recip(self, out, in_):
        o, i = _ap(out), _ap(in_)
        self._rec("vector", lambda e: e.reciprocal(o, i), [in_], [out])

    def select(self, out, mask, on_true, on_false):
        o, m, a, b = _ap(out), _ap(mask), _ap(on_true), _ap(on_false)
        self._rec("vector", lambda e: e.select(o, m, a, b), [mask, on_true, on_false], [out])

    def copy(self, out, in_, eng="vector"):
        o, i = _ap(out), _ap(in_)
        self._rec(eng, lambda e: e.tensor_copy(o, i), [in_], [out])

    def memset(self, out, val, eng="gpsimd"):
        o = _ap(out)
        self._rec(eng, lambda e: e.memset(o, val), [], [out])

    def iota(self, out, pattern, base=0, chmul=0):
        o = _ap(out)
        self._rec("gpsimd",
                  lambda e: e.iota(o, pattern=pattern, base=base, channel_multiplier=chmul,
                                   allow_small_or_imprecise_dtypes=True), [], [out])

    def affine(self, out, in_, compare_op, fill, pattern, chmul, base=0):
        o, i = _ap(out), _ap(in_)
        self._rec("gpsimd",
                  lambda e: e.affine_select(out=o, in_=i, compare_op=compare_op,
                                            fill=fill, base=base, pattern=pattern,
                                            channel_multiplier=chmul), [in_], [out])

    def dma(self, out, in_, queue="sync", accum_op=None, slow_ok=False):
        o, i = _ap(out), _ap(in_)
        kw = {}
        if accum_op is not None:
            kw["accum_op"] = accum_op
        if slow_ok:
            kw["allow_slow_non_contiguous"] = True
        reads = [in_] + ([out] if accum_op is not None else [])
        self._rec(queue, lambda e: e.dma_start(out=o, in_=i, **kw), reads, [out],
                  dma=True)

    NLANES = 16

    def finalize_and_emit(self):
        nc = self.nc
        isem = {e: nc.alloc_semaphore(f"si_{e}") for e in COMPUTE_ENGINES}
        dsem = {q: [nc.alloc_semaphore(f"sd_{q}{j}") for j in range(self.NLANES)]
                for q in DMA_QUEUES}
        eidx = {}
        qidx = {}
        for op in self.ops:
            if op.dma:
                op.qidx = qidx.get(op.eng, 0)
                qidx[op.eng] = op.qidx + 1
            else:
                op.eidx = eidx.get(op.eng, 0)
                eidx[op.eng] = op.eidx + 1

        ordn = {}
        cnte = {}
        for n, op in enumerate(self.ops):
            ordn[id(op)] = cnte.get(op.eng, 0)
            cnte[op.eng] = ordn[id(op)] + 1
        last_w = {}
        readers = {}
        deps = [[] for _ in self.ops]
        for n, op in enumerate(self.ops):
            need = []
            for r in op.reads:
                w = last_w.get(r)
                if w is not None:
                    need.append(w)
            for w_ in op.writes:
                lw = last_w.get(w_)
                if lw is not None:
                    need.append(lw)
                need.extend(readers.get(w_, ()))
            for p in need:
                if p is op:
                    continue
                if p.dma or p.eng != op.eng or op.eng in ("gpsimd", "vector", "scalar"):
                    deps[n].append(p)
                    p.marked = True
            for r in op.reads:
                readers.setdefault(r, []).append(op)
            for w_ in op.writes:
                last_w[w_] = op
                readers[w_] = []
        ecnt = {e: 0 for e in COMPUTE_ENGINES}
        for op in self.ops:
            if op.dma:
                op.cnt = (op.qidx // self.NLANES + 1) * 16
            else:
                if op.marked:
                    ecnt[op.eng] += 1
                op.cnt = ecnt[op.eng]
        known = {e: {} for e in COMPUTE_ENGINES + ("sync",)}
        for n, op in enumerate(self.ops):
            kn = known[op.eng]
            for p in deps[n]:
                sem = (dsem[p.eng][p.qidx % self.NLANES] if p.dma
                       else isem[p.eng])
                if kn.get(sem, 0) < p.cnt:
                    kn[sem] = p.cnt
                    op.waits.append((sem, p.cnt))

        ops_by_eng = {}
        for op in self.ops:
            ops_by_eng.setdefault(op.eng, []).append(op)

        with nc.Block() as blk:
            def body(eng_name):
                def _(e):
                    for op in ops_by_eng.get(eng_name, ()):
                        if op.dma and op.qidx >= self.NLANES:
                            e.wait_ge(dsem[op.eng][op.qidx % self.NLANES],
                                      (op.qidx // self.NLANES) * 16)
                        for sem, cnt in op.waits:
                            e.wait_ge(sem, cnt)
                        instr = op.fn(e)
                        if op.dma:
                            instr.then_inc(dsem[op.eng][op.qidx % self.NLANES], 16)
                        elif op.marked:
                            instr.then_inc(isem[op.eng], 1)
                return _

            for e in ("sync", "tensor", "scalar", "vector", "gpsimd"):
                getattr(blk, e)(body(e))

            @blk.sync
            def _(sync):
                for q in DMA_QUEUES:
                    n = qidx.get(q, 0)
                    for j in range(min(n, self.NLANES)):
                        tot = (n - j + self.NLANES - 1) // self.NLANES
                        sync.wait_ge(dsem[q][j], tot * 16)


class Consts:
    def __init__(self, tr):
        nc = tr.nc
        self.ident128 = nc.alloc_sbuf_tensor("ident128", [128, 128], F32)
        self.ident64 = nc.alloc_sbuf_tensor("ident64", [128, 64], F32)
        self.mask64_low = nc.alloc_sbuf_tensor("mask64_low", [128, 64], F32)
        self.mask64_up = nc.alloc_sbuf_tensor("mask64_up", [128, 64], F32)
        self.ones_col_f32 = nc.alloc_sbuf_tensor("ones_col_f32", [128, 1], F32)
        self.ones_row_f32 = nc.alloc_sbuf_tensor("ones_row_f32", [1, 128], F32)
        self.iota16 = nc.alloc_sbuf_tensor("iota16", [128, 16], F32)
        self.iota8 = nc.alloc_sbuf_tensor("iota8", [128, 8], F32)
        self.iota4 = nc.alloc_sbuf_tensor("iota4", [128, 4], F32)
        tr.memset(self.ident128, 0.0)
        tr.affine(self.ident128, self.ident128, ALU.not_equal, 1.0, [[-1, 128]], 1)
        tr.memset(self.ident64, 0.0)
        tr.memset(self.mask64_low, 1.0)
        tr.memset(self.mask64_up, 1.0)
        for hrows in (slice(0, 64), slice(64, 128)):
            iv = (self.ident64[hrows, :], "ident64")
            lv = (self.mask64_low[hrows, :], "mask64_low")
            uv = (self.mask64_up[hrows, :], "mask64_up")
            tr.affine(iv, iv, ALU.not_equal, 1.0, [[-1, 64]], 1)
            tr.affine(lv, lv, ALU.is_gt, 0.0, [[-1, 64]], 1)
            tr.affine(uv, uv, ALU.is_gt, 0.0, [[1, 64]], -1)
        tr.memset(self.ones_col_f32, 1.0)
        tr.memset(self.ones_row_f32, 1.0)
        tr.iota(self.iota16, [[1, 16]])
        tr.iota(self.iota8, [[1, 8]])
        tr.iota(self.iota4, [[1, 4]])


class SH:

    def __init__(self, tr):
        nc = tr.nc
        self.A0 = tr.arena("A0", [128, 8192], F32)
        self.A1 = tr.arena("A1", [128, 8192], F32)
        self.A2 = tr.arena("A2", [128, 2048], F32)
        self.P0 = tr.arena("P0", [128, 8192], BF16)
        self.P1 = tr.arena("P1", [128, 8192], BF16)
        self.WA = tr.arena("WA", [128, 8192], BF16)
        self.WB = tr.arena("WB", [128, 8192], BF16)
        self.xb = [nc.alloc_sbuf_tensor(f"sh_xb{i}", [128, 512], F32) for i in range(8)]
        self.zb = [nc.alloc_sbuf_tensor(f"sh_zb{i}", [128, 512], BF16) for i in range(8)]
        self.g = [nc.alloc_sbuf_tensor(f"sh_g{i}", [128, 512], F32) for i in range(4)]
        self.b16 = [nc.alloc_sbuf_tensor(f"sh_b{i}", [128, 512], BF16) for i in range(4)]
        self.row = nc.alloc_sbuf_tensor("sh_row", [1, 1024], F32)
        self.stat = nc.alloc_sbuf_tensor("sh_stat", [1, 2048], F32)
        self.col = nc.alloc_sbuf_tensor("sh_col", [128, 64], F32)
        self.bcol = nc.alloc_sbuf_tensor("sh_bcol", [128, 48], F32)
        self.moeA = nc.alloc_sbuf_tensor("sh_moeA", [128, 256], F32)
        self.moeB = nc.alloc_sbuf_tensor("sh_moeB", [128, 256], F32)
        self.moeT = nc.alloc_sbuf_tensor("sh_moeT", [128, 64], F32)
        self.g8 = nc.alloc_sbuf_tensor("sh_g8", [16, 128], F32)
        self.bstage = nc.alloc_sbuf_tensor("sh_bstage", [32, 128], F32)
        self.lrch = nc.alloc_sbuf_tensor("sh_lrch", [128, 32], F32)
        self.kf = [nc.alloc_sbuf_tensor("sh_kf0", [128, 1024], F32)] * 2
        self.kt = [nc.alloc_sbuf_tensor("sh_kt0", [128, 1024], BF16)] * 2
        self.vt = [nc.alloc_sbuf_tensor("sh_vt0", [128, 1024], BF16)] * 2
        self.u16 = nc.alloc_sbuf_tensor("sh_u16", [128, 1024], BF16)
        self.ulr16 = nc.alloc_sbuf_tensor("sh_ulr16", [128, 1024], BF16)
        self.r16 = nc.alloc_sbuf_tensor("sh_r16", [128, 1024], BF16)
        ch = {}
        for nm in ("L", "LT", "AT", "P", "PT"):
            ch[nm] = nc.alloc_sbuf_tensor(f"ch_{nm}", [128, 64], F32)
        self.ch = ch
        self.dnt = nc.alloc_sbuf_tensor("ch_dnt", [128, 64], BF16)
        self.l10t = nc.alloc_sbuf_tensor("ch_l10t", [64, 64], BF16)


def load_bias_cols(tr, cs, dram_vec, n, psum_bank, sh, slot):
    nch = (n + 127) // 128
    out = sh.bcol[:, slot:slot + nch]
    tr.dma(sh.bstage[0:nch, :], dram_vec.rearrange("(a b) -> a b", b=128))
    tr.transpose(psum_bank[:, 0:nch], sh.bstage[0:nch, :], cs.ident128[0:nch, 0:nch])
    tr.s_copy(out, psum_bank[:, 0:nch])
    return out


PRECAST = [
    ("L0_attn_gates_w", [C, 3 * C]), ("L0_attn_out_w", [C, C]),
    ("L0_attn_w1", [C, C]), ("L0_attn_w2", [C, C]),
    ("L0_ffn_Wq", [C, C]), ("L0_ffn_Wk", [C, C]), ("L0_ffn_Wv", [C, C]),
    ("L0_ffn_Wo", [C, C]),
    ("L0_ffn_We1", [8 * C, 4096]), ("L0_ffn_We2", [8 * 4096, C]),
    ("L1_attn_Wq", [C, C]), ("L1_attn_Wk", [C, C]), ("L1_attn_Wv", [C, C]),
    ("L1_attn_Wo", [C, C]),
    ("L1_ffn_We1", [16 * C, 512]), ("L1_ffn_We2", [16 * 512, C]),
    ("L1_ffn_We3", [16 * C, 512]),
    ("L1_ffn_ws1", [C, C]), ("L1_ffn_ws2", [C, C]), ("L1_ffn_ws3", [C, C]),
]


def emit_precast(tr, sh, w, wb):
    slabs = [(sh.WA, "WA"), (sh.WB, "WB"), (sh.P0, "P0"), (sh.P1, "P1")]
    i = 0
    for name, (rows, cols) in [(n, s) for n, s in PRECAST]:
        src2d = w[name]
        if len(src2d.shape) == 3:
            src2d = src2d.rearrange("e a b -> (e a) b")
        step = min(8192, cols)
        for r in range(rows // 128):
            for c0 in range(0, cols, step):
                sl, key = slabs[i % 4]
                i += 1
                n = min(step, cols - c0)
                tr.dma((sl[:, 0:n], key), src2d[r * 128:(r + 1) * 128, c0:c0 + n],
                       queue="gpsimd")
                tr.dma(wb[name][r * 128:(r + 1) * 128, c0:c0 + n], (sl[:, 0:n], key))


def emit_entry_transpose(tr, cs, sh, pb, src_tm, dst_fm):
    for tt in range(NTC):
        tin = (sh.A0[:, (tt % 2) * 1024:(tt % 2 + 1) * 1024], "A0")
        tr.dma(tin, src_tm[tt * 128:(tt + 1) * 128, :])
        for c in range(NCH):
            i = tt * NCH + c
            ps = pb[4 + i % 4]
            o = sh.g[i % 4]
            tr.transpose(ps[:, 0:128], (tin[0][:, c * 128:(c + 1) * 128], "A0"),
                         cs.ident128[:])
            tr.s_copy(o[:, 0:128], ps[:, 0:128])
            tr.dma(dst_fm[c * 128:(c + 1) * 128, tt * 128:(tt + 1) * 128], o[:, 0:128],
                   queue="gpsimd")


def emit_exit_transpose(tr, cs, sh, pb, src_fm, dst_tm):
    tiles = [(sh.A0[:, 0:2048], "A0"), (sh.A1[:, 0:2048], "A1")]
    seq = [(c, h) for c in range(NCH) for h in range(2)]
    for i, (c, h) in enumerate(seq):
        tin = tiles[i % 2]
        tr.dma(tin, src_fm[c * 128:(c + 1) * 128, h * 2048:(h + 1) * 2048])
        for j in range(16):
            n = i * 16 + j
            tt = h * 16 + j
            ps = pb[4 + n % 4]
            o = sh.g[n % 4]
            tr.transpose(ps[:, 0:128], (tin[0][:, j * 128:(j + 1) * 128], tin[1]),
                         cs.ident128[:])
            tr.s_copy(o[:, 0:128], ps[:, 0:128])
            tr.dma(dst_tm[tt * 128:(tt + 1) * 128, c * 128:(c + 1) * 128], o[:, 0:128],
                   queue="gpsimd")


def emit_layernorm(tr, cs, sh, pb, src_fm, dsts, eps, gamma=None, beta=None,
                   res_fm=None, res_dst=None):
    xb = sh.xb
    sq = [sh.g[0], sh.g[1]]
    mean_r = (sh.stat[0:1, 0:512], "ln_mean")
    m2_r = (sh.stat[0:1, 512:1024], "ln_m2")
    a_r = (sh.stat[0:1, 1024:1536], "ln_a")
    b_r = (sh.stat[0:1, 1536:2048], "ln_b")
    yf = [sh.g[2], sh.g[3]]
    yb16 = [sh.b16[0], sh.b16[1]]
    gcol = bcol = None
    if gamma is not None:
        gcol = load_bias_cols(tr, cs, gamma, C, pb[6], sh, 0)
        bcol = load_bias_cols(tr, cs, beta, C, pb[7], sh, 8)

    sum_ps = (pb[0][0:1, :], "pb0")
    sq_ps = (pb[1][0:1, :], "pb1")
    for tb in range(NTB):
        cols = slice(tb * 512, (tb + 1) * 512)
        for c in range(NCH):
            tr.dma(xb[c][:], src_fm[c * 128:(c + 1) * 128, cols])
            tr.square(sq[c % 2][:], xb[c][:])
            tr.mm(sum_ps, cs.ones_col_f32[:], xb[c][:], start=(c == 0), stop=(c == 7))
            tr.mm(sq_ps, cs.ones_col_f32[:], sq[c % 2][:], start=(c == 0), stop=(c == 7))
        tr.ts(mean_r, sum_ps, 1.0 / C, None, ALU.mult)
        tr.tt(m2_r, mean_r, mean_r, ALU.mult)
        tr.stt(m2_r, sq_ps, 1.0 / C, m2_r, ALU.mult, ALU.subtract)
        tr.act(a_r, m2_r, AF.Ln, bias=float(eps))
        tr.act(a_r, a_r, AF.Exp, scale=-0.5)
        tr.tt(b_r, mean_r, a_r, ALU.mult)
        tr.ts(b_r, b_r, -1.0, None, ALU.mult)
        tr.mm((pb[2][:], "pb2"), cs.ones_row_f32[:], a_r)
        tr.mm((pb[3][:], "pb3"), cs.ones_row_f32[:], b_r)
        for c in range(NCH):
            y = yf[c % 2]
            tr.tt(y[:], xb[c][:], (pb[2][:], "pb2"), ALU.mult)
            tr.tt(y[:], y[:], (pb[3][:], "pb3"), ALU.add)
            if gamma is not None:
                tr.act(y[:], y[:], AF.Identity, bias=bcol[:, c:c + 1], scale=gcol[:, c:c + 1])
            wrote16 = False
            for dst, dt in dsts:
                if dt == F32:
                    tr.dma(dst[c * 128:(c + 1) * 128, cols], y[:])
                else:
                    if not wrote16:
                        tr.copy(yb16[c % 2][:], y[:], eng="gpsimd")
                        wrote16 = True
                    tr.dma(dst[c * 128:(c + 1) * 128, cols], yb16[c % 2][:],
                           queue="scalar")
            if res_fm is not None:
                rbuf = (sh.A2[:, (c % 2) * 512:(c % 2 + 1) * 512], "A2")
                tr.dma(rbuf, res_fm[c * 128:(c + 1) * 128, cols])
                tr.tt(y[:], y[:], rbuf, ALU.add)
                tr.dma(res_dst[c * 128:(c + 1) * 128, cols], y[:], queue="gpsimd")


def emit_l0a_gates(tr, cs, sh, pb, w, wb, zbf, qfm_d, kfm_d, ktm_d, vtm_d):
    gwq = sh.WA
    for c in range(NCH):
        tr.dma(gwq[:, c * 1024:(c + 1) * 1024],
               wb["L0_attn_gates_w"][c * 128:(c + 1) * 128, 0:1024], queue="scalar")
        dst = sh.P0 if c < 4 else sh.P1
        tr.dma(dst[:, (c % 4) * 2048:((c % 4) + 1) * 2048],
               wb["L0_attn_gates_w"][c * 128:(c + 1) * 128, 1024:3072], queue="scalar")
    bias_q = load_bias_cols(tr, cs, w["L0_attn_gates_b"][0:1024], C, pb[6], sh, 16)
    biasg = lambda j: (sh.A1[:, j * 512:(j + 1) * 512], "A1")
    for j in range(4):
        tr.dma(biasg(j),
               w["L0_attn_gates_b"][1024 + j * 512:1024 + (j + 1) * 512]
               .unsqueeze(0).broadcast_to([128, 512]))
    lrw_f = load_bias_cols(tr, cs, w["L0_attn_lr_w"].rearrange("a o -> (a o)"), C,
                           pb[7], sh, 24)
    nc = tr.nc
    lrw = nc.alloc_sbuf_tensor("lrw_bf", [128, 8], BF16)
    tr.copy(lrw[:], lrw_f)
    lrb = nc.alloc_sbuf_tensor("lrb_pp", [128, 1], F32)
    tr.dma(lrb[:], w["L0_attn_lr_b"].unsqueeze(0).broadcast_to([128, 1]))
    identb = nc.alloc_sbuf_tensor("ident128_bf", [128, 128], BF16)
    tr.memset(identb, 0.0)
    tr.affine(identb, identb, ALU.not_equal, 1.0, [[-1, 128]], 1)

    ktmp = (sh.A0[:, 0:1024], "A0")
    ksq = (sh.A0[:, 1024:2048], "A0")
    krstd = (sh.col[:, 0:1], "sh_col")
    psb = (pb[5][:].bitcast(BF16), "pb5")
    for tb in range(NTB):
        cols = slice(tb * 512, (tb + 1) * 512)
        for c in range(NCH):
            tr.dma(sh.zb[c][:], zbf[c * 128:(c + 1) * 128, cols])
        for dc in range(NCH):
            qp = (pb[dc % 2][:], f"pb{dc % 2}")
            for c in range(NCH):
                tr.mm(qp, gwq[:, c * 1024 + dc * 128:c * 1024 + (dc + 1) * 128],
                      sh.zb[c][:], start=(c == 0), stop=(c == 7))
            qt = sh.b16[dc % 2]
            tr.act(qt[:], qp, AF.Identity, bias=bias_q[:, dc:dc + 1])
            tr.dma(qfm_d[dc * 128:(dc + 1) * 128, cols], qt[:])
        for tch in range(4):
            tcs = slice(tch * 128, (tch + 1) * 128)
            trow = tb * 4 + tch
            for c in range(NCH):
                zt = sh.zb[c][:, tcs]
                src = sh.P0 if c < 4 else sh.P1
                for j in range(4):
                    tr.mm((pb[j][:], f"pb{j}"), zt,
                          src[:, (c % 4) * 2048 + j * 512:(c % 4) * 2048 + (j + 1) * 512],
                          start=(c == 0), stop=(c == 7))
                tr.mm((pb[4][:, 0:1], "pb4"), zt, lrw[:, c:c + 1],
                      start=(c == 0), stop=(c == 7))
            tr.act((sh.lrch[:, trow:trow + 1], "lrch"), (pb[4][:, 0:1], "pb4"),
                   AF.Identity, bias=lrb[:, 0:1])
            tr.tt((ktmp[0][:, 0:512], "A0"), (pb[0][:], "pb0"), biasg(0), ALU.add)
            tr.tt((ktmp[0][:, 512:1024], "A0"), (pb[1][:], "pb1"), biasg(1), ALU.add)
            tr.square(ksq, ktmp)
            tr.reduce(krstd, ksq, ALU.add)
            tr.act(krstd, krstd, AF.Ln)
            tr.act(krstd, krstd, AF.Exp, scale=-0.5)
            kn = (sh.A2[:].bitcast(BF16)[:, 0:1024], "A2")
            tr.ts(kn, ktmp, krstd, None, ALU.mult)
            tr.dma(ktm_d[trow * 128:(trow + 1) * 128, :], kn)
            for c2 in range(NCH):
                tr.transpose((psb[0][:, 0:128], "pb5"),
                             (kn[0][:, c2 * 128:(c2 + 1) * 128], "A2"), identb[:])
                tr.s_copy(sh.b16[2][:, 0:128], (psb[0][:, 0:128], "pb5"))
                tr.dma(kfm_d[c2 * 128:(c2 + 1) * 128, trow * 128:(trow + 1) * 128],
                       sh.b16[2][:, 0:128], queue="scalar")
            vn = (sh.A2[:].bitcast(BF16)[:, 1024:2048], "A2")
            tr.tt((vn[0][:, 0:512], "A2"), (pb[2][:], "pb2"), biasg(2), ALU.add)
            tr.tt((vn[0][:, 512:1024], "A2"), (pb[3][:], "pb3"), biasg(3), ALU.add)
            tr.dma(vtm_d[trow * 128:(trow + 1) * 128, :], vn)


def emit_l0a_recurrence(tr, cs, sh, pb, kfm_d, ktm_d, vtm_d, rpart=9):
    S = (sh.A0, "A0")
    W = (sh.A1, "A1")
    for j in range(16):
        tr.memset((sh.A0[:, j * 512:(j + 1) * 512], "A0"), 0.0)
        tr.memset((sh.A1[:, j * 512:(j + 1) * 512], "A1"), 0.0)

    for i in range(NTC):
        kf, kt, vt = sh.kf[i % 2], sh.kt[i % 2], sh.vt[i % 2]
        for c in range(NCH):
            tr.dma(kf[:, c * 128:(c + 1) * 128],
                   kfm_d[c * 128:(c + 1) * 128, i * 128:(i + 1) * 128],
                   queue="gpsimd")
        tr.dma(kt[:], ktm_d[i * 128:(i + 1) * 128, :])
        tr.dma(vt[:], vtm_d[i * 128:(i + 1) * 128, :])
        if rpart < 1:
            continue
        kfA = [kf[:, c * 128:c * 128 + 64] for c in range(NCH)]
        kfB = [kf[:, c * 128 + 64:(c + 1) * 128] for c in range(NCH)]
        KA = (pb[6][0:64, 0:64], "pb6")
        KB = (pb[7][64:128, 0:64], "pb7")
        L10T = (pb[5][0:64, 0:64], "pb5")
        for c in range(NCH):
            tr.mm(KA, kfA[c], kfA[c], start=(c == 0), stop=(c == 7))
        for c in range(NCH):
            tr.mm(KB, kfB[c], kfB[c], start=(c == 0), stop=(c == 7))
        for c in range(NCH):
            tr.mm(L10T, kfA[c], kfB[c], start=(c == 0), stop=(c == 7))
        tr.s_copy(sh.l10t[:], L10T)
        if rpart < 2:
            continue
        ch = sh.ch
        for blk, Kps in ((0, KA), (1, KB)):
            rows = slice(0, 64) if blk == 0 else slice(64, 128)
            L = (ch["L"][rows, :], "ch_L")
            LT = (ch["LT"][rows, :], "ch_LT")
            AT = (ch["AT"][rows, :], "ch_AT")
            P = (ch["P"][rows, :], "ch_P")
            PT = (ch["PT"][rows, :], "ch_PT")
            ml = (cs.mask64_low[rows, :], "mask64_low")
            mu = (cs.mask64_up[rows, :], "mask64_up")
            iv = (cs.ident64[rows, :], "ident64")
            tr.tt(L, Kps, ml, ALU.mult)
            tr.tt(LT, Kps, mu, ALU.mult)
            tr.tt(AT, iv, LT, ALU.subtract)
            tr.copy(P, L)
            tr.copy(PT, LT)
            c1 = (pb[6][rows, 64:128], "pb6")
            c2 = (pb[7][rows, 64:128], "pb7")
            c3 = (pb[5][rows, 64:128], "pb5")
            for step in range(5):
                tr.mm(c1, PT, P)
                tr.mm(c2, P, PT)
                tr.s_copy(P, c1)
                tr.s_copy(PT, c2)
                tr.mm(c3, P, AT)
                tr.tt(AT, AT, c3, ALU.add)
            tr.copy((sh.dnt[rows, :], "ch_dnt"), AT)
        if rpart < 3:
            continue
        rps = [(pb[0][:], "pb0"), (pb[1][:], "pb1")]
        for cc in range(NCH):
            for h in range(2):
                tr.mm(rps[h], kf[:, cc * 128:(cc + 1) * 128],
                      (sh.A0[:, cc * 1024 + h * 512:cc * 1024 + (h + 1) * 512], "A0"),
                      start=(cc == 0), stop=(cc == 7))
        ups = [(pb[2][:], "pb2"), (pb[3][:], "pb3")]
        rsub = rpart - 3 if rpart < 9 else 9
        for h in range(2):
            tr.tt((sh.r16[0:64, h * 512:(h + 1) * 512], "r16"),
                  vt[0:64, h * 512:(h + 1) * 512],
                  (pb[h][0:64, :], f"pb{h}"), ALU.subtract)
            if rsub < 1:
                continue
            tr.mm((ups[h][0][0:64, :], f"pb{2 + h}"), (sh.dnt[0:64, :], "ch_dnt"),
                  (sh.r16[0:64, h * 512:(h + 1) * 512], "r16"))
            tr.s_copy((sh.u16[0:64, h * 512:(h + 1) * 512], "u16"),
                      (ups[h][0][0:64, :], f"pb{2 + h}"))
            if rsub < 2:
                continue
            tr.mm((pb[4 + h][64:128, :], f"pb{4 + h}"), sh.l10t[:],
                  (sh.u16[0:64, h * 512:(h + 1) * 512], "u16"))
            tr.tt((sh.r16[64:128, h * 512:(h + 1) * 512], "r16"),
                  vt[64:128, h * 512:(h + 1) * 512],
                  (pb[h][64:128, :], f"pb{h}"), ALU.subtract)
            tr.tt((sh.r16[64:128, h * 512:(h + 1) * 512], "r16"),
                  (sh.r16[64:128, h * 512:(h + 1) * 512], "r16"),
                  (pb[4 + h][64:128, :], f"pb{4 + h}"), ALU.subtract)
            if rsub < 3:
                continue
            tr.mm((ups[h][0][64:128, :], f"pb{2 + h}"), (sh.dnt[64:128, :], "ch_dnt"),
                  (sh.r16[64:128, h * 512:(h + 1) * 512], "r16"))
            tr.s_copy((sh.u16[64:128, h * 512:(h + 1) * 512], "u16"),
                      (ups[h][0][64:128, :], f"pb{2 + h}"))
            if rsub < 4:
                continue
            tr.ts((sh.ulr16[:, h * 512:(h + 1) * 512], "ulr16"),
                  (sh.u16[:, h * 512:(h + 1) * 512], "u16"),
                  (sh.lrch[:, i:i + 1], "lrch"), None, ALU.mult)
        if rpart < 4:
            continue
        for cc in range(NCH):
            ktc = kt[:, cc * 128:(cc + 1) * 128]
            for h in range(2):
                up = (pb[4 + h][:], f"pb{4 + h}")
                tr.mm(up, ktc, (sh.u16[:, h * 512:(h + 1) * 512], "u16"))
                dst = (sh.A0[:, cc * 1024 + h * 512:cc * 1024 + (h + 1) * 512], "A0")
                tr.tt(dst, dst, up, ALU.add)
                tr.mm(up, ktc, (sh.ulr16[:, h * 512:(h + 1) * 512], "ulr16"))
                dst = (sh.A1[:, cc * 1024 + h * 512:cc * 1024 + (h + 1) * 512], "A1")
                tr.tt(dst, dst, up, ALU.add)


def emit_l0a_output(tr, cs, sh, pb, w, wb, qfm_d, xln_d, x1a_d):
    nc = tr.nc
    for j in range(16):
        tr.s_copy((sh.WA[:, j * 512:(j + 1) * 512], "WA"),
                  (sh.A1[:, j * 512:(j + 1) * 512], "A1"))
    for c in range(NCH):
        tr.dma(sh.WB[:, c * 1024:(c + 1) * 1024],
               wb["L0_attn_out_w"][c * 128:(c + 1) * 128, :], queue="scalar")
    outb = load_bias_cols(tr, cs, w["L0_attn_out_b"], C, pb[6], sh, 0)
    for tb in range(NTB):
        cols = slice(tb * 512, (tb + 1) * 512)
        for c in range(NCH):
            tr.dma(sh.zb[c][:], qfm_d[c * 128:(c + 1) * 128, cols])
        for cp in range(NCH):
            yp = (pb[cp % 2][:], f"pb{cp % 2}")
            for c in range(NCH):
                tr.mm(yp, (sh.WA[:, c * 1024 + cp * 128:c * 1024 + (cp + 1) * 128], "WA"),
                      sh.zb[c][:], start=(c == 0), stop=(c == 7))
            tr.s_copy((sh.P0[:, cp * 512:(cp + 1) * 512], "P0"), yp)
        for dc in range(NCH):
            op_ = (pb[2 + dc % 2][:], f"pb{2 + dc % 2}")
            for cp in range(NCH):
                tr.mm(op_, sh.WB[:, cp * 1024 + dc * 128:cp * 1024 + (dc + 1) * 128],
                      (sh.P0[:, cp * 512:(cp + 1) * 512], "P0"),
                      start=(cp == 0), stop=(cp == 7))
            g0 = sh.g[dc % 2]
            tr.act(g0[:], op_, AF.Identity, bias=outb[:, dc:dc + 1])
            g1 = sh.g[2 + dc % 2]
            tr.dma(g1[:], xln_d[dc * 128:(dc + 1) * 128, cols])
            tr.tt(g1[:], g1[:], g0[:], ALU.add)
            tr.dma(x1a_d[dc * 128:(dc + 1) * 128, cols], g1[:])


def emit_l0a_mlp(tr, cs, sh, pb, w, wb, z2bf, x1a_d, xfm0, xfm1):
    for c in range(NCH):
        tr.dma(sh.WA[:, c * 1024:(c + 1) * 1024],
               wb["L0_attn_w1"][c * 128:(c + 1) * 128, :], queue="scalar")
        tr.dma(sh.WB[:, c * 1024:(c + 1) * 1024],
               wb["L0_attn_w2"][c * 128:(c + 1) * 128, :], queue="scalar")
    b1c = load_bias_cols(tr, cs, w["L0_attn_b1"], C, pb[6], sh, 0)
    b2c = load_bias_cols(tr, cs, w["L0_attn_b2"], C, pb[7], sh, 8)
    for tb in range(NTB):
        cols = slice(tb * 512, (tb + 1) * 512)
        for c in range(NCH):
            tr.dma(sh.zb[c][:], z2bf[c * 128:(c + 1) * 128, cols])
        for f in range(NCH):
            hp = (pb[f % 2][:], f"pb{f % 2}")
            for c in range(NCH):
                tr.mm(hp, (sh.WA[:, c * 1024 + f * 128:c * 1024 + (f + 1) * 128], "WA"),
                      sh.zb[c][:], start=(c == 0), stop=(c == 7))
            tr.act((sh.P0[:, f * 512:(f + 1) * 512], "P0"), hp, AF.Relu,
                   bias=b1c[:, f:f + 1])
        for dc in range(NCH):
            op_ = (pb[2 + dc % 2][:], f"pb{2 + dc % 2}")
            for f in range(NCH):
                tr.mm(op_, (sh.WB[:, f * 1024 + dc * 128:f * 1024 + (dc + 1) * 128], "WB"),
                      (sh.P0[:, f * 512:(f + 1) * 512], "P0"),
                      start=(f == 0), stop=(f == 7))
            g0 = sh.g[dc % 2]
            tr.act(g0[:], op_, AF.Identity, bias=b2c[:, dc:dc + 1])
            g1 = sh.g[2 + dc % 2]
            tr.dma(g1[:], x1a_d[dc * 128:(dc + 1) * 128, cols])
            tr.tt(g1[:], g1[:], g0[:], ALU.add)
            g2 = (sh.A0[:, (dc % 2) * 512:(dc % 2 + 1) * 512], "A0")
            tr.dma(g2, xfm0[dc * 128:(dc + 1) * 128, cols])
            tr.tt(g1[:], g1[:], g2, ALU.add)
            tr.dma(xfm1[dc * 128:(dc + 1) * 128, cols], g1[:])



def vaug_tile(sh, kt):
    if kt < 15:
        return (sh.A0[:].bitcast(BF16)[:, kt * 1040:(kt + 1) * 1040], "A0")
    if kt < 30:
        j = kt - 15
        return (sh.A1[:].bitcast(BF16)[:, j * 1040:(j + 1) * 1040], "A1")
    j = kt - 30
    return (sh.A2[:].bitcast(BF16)[:, j * 1040:(j + 1) * 1040], "A2")


def pt_tile(sh, kt):
    if kt < 16:
        return (sh.P0[:, kt * 512:(kt + 1) * 512], "P0")
    return (sh.P1[:, (kt - 16) * 512:(kt - 15) * 512], "P1")


def emit_fm_proj(tr, cs, sh, pb, w_dram, bias_vec, src_bf, dst, res_fm=None,
                 func=AF.Identity, dst_dtype=F32, wslab=None, bslot=0):
    wsl, wkey = wslab if wslab is not None else (sh.WA, "WA")
    for c in range(NCH):
        tr.dma((wsl[:, c * 1024:(c + 1) * 1024], wkey),
               w_dram[c * 128:(c + 1) * 128, :], queue="scalar")
    bcol = load_bias_cols(tr, cs, bias_vec, C, pb[6], sh, bslot) if bias_vec is not None else None
    for tb in range(NTB):
        cols = slice(tb * 512, (tb + 1) * 512)
        for c in range(NCH):
            tr.dma(sh.zb[c][:], src_bf[c * 128:(c + 1) * 128, cols])
        for dc in range(NCH):
            op_ = (pb[dc % 2][:], f"pb{dc % 2}")
            for c in range(NCH):
                tr.mm(op_, (wsl[:, c * 1024 + dc * 128:c * 1024 + (dc + 1) * 128], wkey),
                      sh.zb[c][:], start=(c == 0), stop=(c == 7))
            if dst_dtype == BF16:
                o = sh.b16[dc % 2]
                if bcol is not None:
                    tr.act(o[:], op_, func, bias=bcol[:, dc:dc + 1])
                else:
                    tr.s_copy(o[:], op_)
                tr.dma(dst[dc * 128:(dc + 1) * 128, cols], o[:])
            else:
                g0 = sh.g[dc % 2]
                if bcol is not None:
                    tr.act(g0[:], op_, func, bias=bcol[:, dc:dc + 1])
                else:
                    tr.s_copy(g0[:], op_)
                if res_fm is not None:
                    g1 = sh.g[2 + dc % 2]
                    tr.dma(g1[:], res_fm[dc * 128:(dc + 1) * 128, cols])
                    tr.tt(g0[:], g0[:], g1[:], ALU.add)
                tr.dma(dst[dc * 128:(dc + 1) * 128, cols], g0[:])


def emit_l0f_qkv(tr, cs, sh, pb, w, wb, zsrc, qfm2_d, kfm2_d):
    for c in range(NCH):
        tr.dma((sh.WA[:, c * 1024:(c + 1) * 1024], "WA"),
               wb["L0_ffn_Wq"][c * 128:(c + 1) * 128, :], queue="scalar")
        tr.dma((sh.WB[:, c * 1024:(c + 1) * 1024], "WB"),
               wb["L0_ffn_Wk"][c * 128:(c + 1) * 128, :], queue="scalar")
        tr.dma((sh.P0[:, c * 1024:(c + 1) * 1024], "P0"),
               wb["L0_ffn_Wv"][c * 128:(c + 1) * 128, :], queue="scalar")
    for tb in range(NTB):
        cols = slice(tb * 512, (tb + 1) * 512)
        for c in range(NCH):
            tr.dma(sh.zb[c][:], zsrc[c * 128:(c + 1) * 128, cols])
        for dc in range(NCH):
            qp = (pb[dc % 2][:], f"pb{dc % 2}")
            kp = (pb[2 + dc % 2][:], f"pb{2 + dc % 2}")
            for c in range(NCH):
                tr.mm(qp, (sh.WA[:, c * 1024 + dc * 128:c * 1024 + (dc + 1) * 128], "WA"),
                      sh.zb[c][:], start=(c == 0), stop=(c == 7))
            for c in range(NCH):
                tr.mm(kp, (sh.WB[:, c * 1024 + dc * 128:c * 1024 + (dc + 1) * 128], "WB"),
                      sh.zb[c][:], start=(c == 0), stop=(c == 7))
            tr.s_copy(sh.b16[0][:], qp)
            tr.dma(qfm2_d[dc * 128:(dc + 1) * 128, cols], sh.b16[0][:])
            tr.s_copy(sh.b16[1][:], kp)
            tr.dma(kfm2_d[dc * 128:(dc + 1) * 128, cols], sh.b16[1][:], queue="scalar")
        for tch in range(4):
            trow = tb * 4 + tch
            tcs = slice(tch * 128, (tch + 1) * 128)
            for c in range(NCH):
                zt = sh.zb[c][:, tcs]
                tr.mm((pb[4][:], "pb4"), zt, (sh.P0[:, c * 1024:c * 1024 + 512], "P0"),
                      start=(c == 0), stop=(c == 7))
                tr.mm((pb[5][:], "pb5"), zt, (sh.P0[:, c * 1024 + 512:(c + 1) * 1024], "P0"),
                      start=(c == 0), stop=(c == 7))
            vt = vaug_tile(sh, trow)
            tr.memset(vt, 1.0)
            for hh in range(8):
                tr.s_copy((vt[0][:, hh * 65:hh * 65 + 64], vt[1]),
                          (pb[4][:, hh * 64:(hh + 1) * 64], "pb4"))
            for hh in range(8):
                tr.s_copy((vt[0][:, (8 + hh) * 65:(8 + hh) * 65 + 64], vt[1]),
                          (pb[5][:, hh * 64:(hh + 1) * 64], "pb5"))


def emit_l0f_attn(tr, cs, sh, pb, qfm2_d, kfm2_d, ctx_d):
    pbig = _ap(pb[0]).tensor
    kslab = (sh.WA[:, 0:4096], "WA")
    qslab = (sh.WB[:, 0:4096], "WB")
    for hp in range(8):
        tr.dma(kslab, kfm2_d[hp * 128:(hp + 1) * 128, :])
        tr.dma(qslab, qfm2_d[hp * 128:(hp + 1) * 128, :])
        for h2 in range(2):
            base = 64 * h2
            rows = slice(base, base + 64)
            for which, slab in ((0, qslab), (1, kslab)):
                for j in range(NTB):
                    sq = (sh.xb[j][rows, :], f"sh_xb{j}")
                    tr.square(sq, (slab[0][rows, j * 512:(j + 1) * 512], slab[1]))
                    tr.mm((pb[6][0:1, :], "pb6"), cs.ones_col_f32[rows, 0:1], sq)
                    tr.reduce((sh.stat[0:1, which * 8 + j:which * 8 + j + 1], "ln_mean"),
                              (pb[6][0:1, :], "pb6"), ALU.max)
            tr.reduce((sh.stat[0:1, 16:17], "ln_mean"), (sh.stat[0:1, 0:8], "ln_mean"), ALU.max)
            tr.reduce((sh.stat[0:1, 17:18], "ln_mean"), (sh.stat[0:1, 8:16], "ln_mean"), ALU.max)
            tr.tt((sh.stat[0:1, 18:19], "ln_mean"), (sh.stat[0:1, 16:17], "ln_mean"),
                  (sh.stat[0:1, 17:18], "ln_mean"), ALU.mult)
            tr.act((sh.stat[0:1, 18:19], "ln_mean"), (sh.stat[0:1, 18:19], "ln_mean"), AF.Ln)
            tr.act((sh.stat[0:1, 18:19], "ln_mean"), (sh.stat[0:1, 18:19], "ln_mean"),
                   AF.Exp, scale=0.5)
            tr.ts((sh.stat[0:1, 18:19], "ln_mean"), (sh.stat[0:1, 18:19], "ln_mean"),
                  -0.125, None, ALU.mult)
            tr.mm((pb[7][:, 0:1], "pb7"), cs.ones_row_f32[:],
                  (sh.stat[0:1, 18:19], "ln_mean"))
            negb = (sh.col[:, 4:5], "sh_col")
            tr.s_copy(negb, (pb[7][:, 0:1], "pb7"))
            h = hp * 2 + h2
            for qb in range(NTB):
                qcs = slice(qb * 512, (qb + 1) * 512)
                for kt0 in range(0, NTC, 4):
                    for j in range(4):
                        kt = kt0 + j
                        tr.mm((pb[j][:], f"pb{j}"),
                              (kslab[0][rows, kt * 128:(kt + 1) * 128], kslab[1]),
                              (qslab[0][rows, qcs], qslab[1]))
                    pt4 = ((sh.P0 if kt0 < 16 else sh.P1)
                           [:, (kt0 % 16) * 512:((kt0 % 16) + 4) * 512],
                           "P0" if kt0 < 16 else "P1")
                    tr.act(pt4, (pbig[:, 0:2048], "pb0"), AF.Exp, bias=negb,
                           scale=0.125, xreads=("pb1", "pb2", "pb3"))
                cp = (pb[4][0:65, :], "pb4")
                for kt in range(NTC):
                    vt = vaug_tile(sh, kt)
                    tr.mm(cp, (vt[0][:, h * 65:(h + 1) * 65], vt[1]), pt_tile(sh, kt),
                          start=(kt == 0), stop=(kt == 31))
                tr.recip((sh.row[0:1, 0:512], "sh_row"), (pb[4][64:65, :], "pb4"))
                tr.mm((pb[5][0:64, :], "pb5"), cs.ones_row_f32[0:1, 0:64],
                      (sh.row[0:1, 0:512], "sh_row"))
                tr.s_copy((sh.g[0][0:64, :], "sh_g0"), (pb[5][0:64, :], "pb5"))
                ctx = sh.xb[4 * h2 + qb // 2][:].bitcast(BF16)
                tr.tt((ctx[0:64, (qb % 2) * 512:(qb % 2 + 1) * 512], f"sh_xb{4 * h2 + qb // 2}"),
                      (pb[4][0:64, :], "pb4"), (sh.g[0][0:64, :], "sh_g0"), ALU.mult)
        for h2 in range(2):
            for j in range(4):
                ctx = sh.xb[4 * h2 + j][:].bitcast(BF16)
                tr.dma(ctx_d[hp * 128 + h2 * 64:hp * 128 + h2 * 64 + 64,
                             j * 1024:(j + 1) * 1024],
                       (ctx[0:64, :], f"sh_xb{4 * h2 + j}"), queue="gpsimd")


def emit_l0f_router(tr, cs, sh, pb, w, out1b_d, g_d):
    nc = tr.nc
    wr = nc.alloc_sbuf_tensor("wr_bf", [128, 64], BF16)
    for c in range(NCH):
        tr.dma(wr[:, c * 8:(c + 1) * 8], w["L0_ffn_Wr"][c * 128:(c + 1) * 128, :],
               queue="gpsimd")
    big8 = nc.alloc_sbuf_tensor("big8", [128, 8], F32)
    tr.memset(big8, 1e4)
    tr.tt(big8[:], big8[:], cs.iota8[:], ALU.add)
    for tb in range(NTB):
        cols = slice(tb * 512, (tb + 1) * 512)
        for c in range(NCH):
            tr.dma(sh.zb[c][:], out1b_d[c * 128:(c + 1) * 128, cols])
        for tch in range(4):
            ti = tb * 4 + tch
            tcs = slice(tch * 128, (tch + 1) * 128)
            lp = (pb[tch % 2][:, 0:8], f"pb{tch % 2}")
            for c in range(NCH):
                tr.mm(lp, sh.zb[c][:, tcs], wr[:, c * 8:(c + 1) * 8],
                      start=(c == 0), stop=(c == 7))
            lg = (sh.moeA[:, 0:8], "moeA")
            tr.s_copy(lg, lp)
            nm = (sh.moeT[:, 0:1], "moeT")
            tr.reduce(nm, lg, ALU.max, negate=True)
            ex = (sh.moeT[:, 8:16], "moeT")
            tr.act(ex, lg, AF.Exp, bias=nm)
            sm = (sh.moeT[:, 1:2], "moeT")
            tr.reduce(sm, ex, ALU.add)
            rs = (sh.moeT[:, 2:3], "moeT")
            tr.recip(rs, sm)
            pr = (sh.moeB[:, 0:8], "moeB")
            tr.ts(pr, ex, rs, None, ALU.mult)
            m1 = (sh.moeT[:, 3:4], "moeT")
            tr.reduce(m1, pr, ALU.max)
            t0 = (sh.moeT[:, 16:24], "moeT")
            tr.ts(t0, pr, m1, None, ALU.is_ge)
            t1 = (sh.moeT[:, 24:32], "moeT")
            tr.stt(t1, t0, -1e4, big8[:], ALU.mult, ALU.add)
            i1 = (sh.moeT[:, 4:5], "moeT")
            tr.reduce(i1, t1, ALU.min)
            oh1 = (sh.moeT[:, 32:40], "moeT")
            tr.ts(oh1, cs.iota8[:], i1, None, ALU.is_equal)
            p2m = (sh.moeT[:, 40:48], "moeT")
            tr.stt(p2m, oh1, -1e30, pr, ALU.mult, ALU.add)
            m2 = (sh.moeT[:, 5:6], "moeT")
            tr.reduce(m2, p2m, ALU.max)
            tr.ts(t0, p2m, m2, None, ALU.is_ge)
            tr.stt(t1, t0, -1e4, big8[:], ALU.mult, ALU.add)
            i2 = (sh.moeT[:, 6:7], "moeT")
            tr.reduce(i2, t1, ALU.min)
            oh2 = (sh.moeT[:, 48:56], "moeT")
            tr.ts(oh2, cs.iota8[:], i2, None, ALU.is_equal)
            e1 = (sh.moeT[:, 7:8], "moeT")
            tr.act(e1, m1, AF.Exp)
            e2 = (sh.moeT[:, 56:57], "moeT")
            tr.act(e2, m2, AF.Exp)
            den = (sh.moeT[:, 57:58], "moeT")
            tr.tt(den, e1, e2, ALU.add)
            rden = (sh.moeT[:, 58:59], "moeT")
            tr.recip(rden, den)
            w1 = (sh.moeT[:, 59:60], "moeT")
            tr.tt(w1, e1, rden, ALU.mult)
            w2 = (sh.moeT[:, 60:61], "moeT")
            tr.tt(w2, e2, rden, ALU.mult)
            Gc = (sh.moeA[:, 8:16], "moeA")
            tr.ts(Gc, oh1, w1, None, ALU.mult)
            Gc2 = (sh.moeA[:, 16:24], "moeA")
            tr.ts(Gc2, oh2, w2, None, ALU.mult)
            tr.tt(Gc, Gc, Gc2, ALU.add)
            tr.transpose((pb[7][0:8, 0:128], "pb7"), Gc, cs.ident128[:])
            tr.s_copy((sh.g8[0:8, :], "sh_g8"), (pb[7][0:8, 0:128], "pb7"))
            tr.dma(g_d[:, ti * 128:(ti + 1) * 128], (sh.g8[0:8, :], "sh_g8"),
                   queue="scalar")


def emit_l0f_moe(tr, cs, sh, pb, w, wb, out1b_d, g_d, acc_d):
    for e in range(8):
        be1c = load_bias_cols(tr, cs, w["L0_ffn_be1"][e, :], 4096, pb[6], sh, 0)
        be2c = load_bias_cols(tr, cs, w["L0_ffn_be2"][e, :], C, pb[7], sh, 32)
        for tb in range(NTB):
            cols = slice(tb * 512, (tb + 1) * 512)
            for c in range(NCH):
                tr.dma(sh.zb[c][:], out1b_d[c * 128:(c + 1) * 128, cols])
            gate = sh.g[2]
            tr.dma(gate[:], g_d[e:e + 1, cols].broadcast_to([128, 512]))
            for fg in range(8):
                hh = (fg % 2) * 512
                for c in range(NCH):
                    tr.dma(sh.WA[:, c * 1024 + hh:c * 1024 + hh + 512],
                           wb["L0_ffn_We1"][e * 1024 + c * 128:e * 1024 + (c + 1) * 128,
                                            fg * 512:(fg + 1) * 512],
                           queue="scalar")
                for fi in range(4):
                    f = fg * 4 + fi
                    hp_ = (pb[f % 2][:], f"pb{f % 2}")
                    for c in range(NCH):
                        tr.mm(hp_, sh.WA[:, c * 1024 + hh + fi * 128:
                                         c * 1024 + hh + (fi + 1) * 128],
                              sh.zb[c][:], start=(c == 0), stop=(c == 7))
                    tr.act(pt_tile(sh, f), hp_, AF.Gelu, bias=be1c[:, f:f + 1])
            for f in range(32):
                tr.dma((sh.WB[:, (f % 8) * 1024:(f % 8 + 1) * 1024], "WB"),
                       wb["L0_ffn_We2"][e * 4096 + f * 128:e * 4096 + (f + 1) * 128, :],
                       queue="scalar")
                for dc in range(NCH):
                    tr.mm((pb[dc][:], f"pb{dc}"),
                          (sh.WB[:, (f % 8) * 1024 + dc * 128:(f % 8) * 1024 + (dc + 1) * 128],
                           "WB"), pt_tile(sh, f), start=(f == 0), stop=(f == 31))
            for dc in range(NCH):
                g0 = sh.g[dc % 2]
                tr.stt(g0[:], (pb[dc][:], f"pb{dc}"), be2c[:, dc:dc + 1], gate[:],
                       ALU.add, ALU.mult)
                tr.dma(acc_d[dc * 128:(dc + 1) * 128, cols], g0[:], queue="gpsimd",
                       accum_op=(ALU.add if e > 0 else None))


def emit_add_fm(tr, cs, sh, pb, a_d, b_d, dst_d):
    for c in range(NCH):
        for tb in range(NTB):
            cols = slice(tb * 512, (tb + 1) * 512)
            g0, g1 = sh.g[tb % 2], sh.g[2 + tb % 2]
            tr.dma(g0[:], a_d[c * 128:(c + 1) * 128, cols])
            tr.dma(g1[:], b_d[c * 128:(c + 1) * 128, cols], queue="scalar")
            tr.tt(g0[:], g0[:], g1[:], ALU.add)
            tr.dma(dst_d[c * 128:(c + 1) * 128, cols], g0[:], queue="gpsimd")


def emit_l1a_qkv_v(tr, cs, sh, pb, w, wb, zsrc):
    for c in range(NCH):
        tr.dma((sh.P0[:, c * 1024:(c + 1) * 1024], "P0"),
               wb["L1_attn_Wv"][c * 128:(c + 1) * 128, :], queue="scalar")
    tr.dma(sh.g[2][:], w["L1_attn_bv"][0:512].unsqueeze(0).broadcast_to([128, 512]))
    tr.dma(sh.g[3][:], w["L1_attn_bv"][512:1024].unsqueeze(0).broadcast_to([128, 512]))
    for tb in range(NTB):
        cols = slice(tb * 512, (tb + 1) * 512)
        for c in range(NCH):
            tr.dma(sh.zb[c][:], zsrc[c * 128:(c + 1) * 128, cols])
        for tch in range(4):
            trow = tb * 4 + tch
            tcs = slice(tch * 128, (tch + 1) * 128)
            for c in range(NCH):
                zt = sh.zb[c][:, tcs]
                tr.mm((pb[4][:], "pb4"), zt, (sh.P0[:, c * 1024:c * 1024 + 512], "P0"),
                      start=(c == 0), stop=(c == 7))
                tr.mm((pb[5][:], "pb5"), zt, (sh.P0[:, c * 1024 + 512:(c + 1) * 1024], "P0"),
                      start=(c == 0), stop=(c == 7))
            h0 = sh.g[0]
            tr.tt(h0[:], (pb[4][:], "pb4"), sh.g[2][:], ALU.add)
            h1 = sh.g[1]
            tr.tt(h1[:], (pb[5][:], "pb5"), sh.g[3][:], ALU.add)
            vt = vaug_tile(sh, trow)
            tr.memset(vt, 1.0)
            for hh in range(8):
                tr.s_copy((vt[0][:, hh * 65:hh * 65 + 64], vt[1]),
                          (h0[:, hh * 64:(hh + 1) * 64], "sh_g0"))
            for hh in range(8):
                tr.s_copy((vt[0][:, (8 + hh) * 65:(8 + hh) * 65 + 64], vt[1]),
                          (h1[:, hh * 64:(hh + 1) * 64], "sh_g1"))


def emit_l1a_stats(tr, cs, sh, pb, qfm3_d, kfm3_d, statall, negbs):
    nc = tr.nc
    kslab = (sh.WA[:, 0:4096], "WA")
    qslab = (sh.WB[:, 0:4096], "WB")
    ks16 = sh.col[:].bitcast(BF16)
    for hp in range(8):
        tr.dma(kslab, kfm3_d[hp * 128:(hp + 1) * 128, :])
        tr.dma(qslab, qfm3_d[hp * 128:(hp + 1) * 128, :])
        for h2 in range(2):
            h = 2 * hp + h2
            base = 64 * h2
            rows = slice(base, base + 64)
            for which, slab in ((0, qslab), (1, kslab)):
                for j in range(NTB):
                    sq = (sh.xb[j][rows, :], f"sh_xb{j}")
                    tr.square(sq, (slab[0][rows, j * 512:(j + 1) * 512], slab[1]))
                    tr.mm((pb[6][0:1, :], "pb6"), cs.ones_col_f32[rows, 0:1], sq)
                    tr.reduce((sh.stat[0:1, which * 8 + j:which * 8 + j + 1], "ln_mean"),
                              (pb[6][0:1, :], "pb6"), ALU.max)
            tr.reduce((sh.stat[0:1, 16:17], "ln_mean"), (sh.stat[0:1, 0:8], "ln_mean"), ALU.max)
            tr.reduce((sh.stat[0:1, 17:18], "ln_mean"), (sh.stat[0:1, 8:16], "ln_mean"), ALU.max)
            tr.tt((sh.stat[0:1, 18:19], "ln_mean"), (sh.stat[0:1, 16:17], "ln_mean"),
                  (sh.stat[0:1, 17:18], "ln_mean"), ALU.mult)
            tr.act((sh.stat[0:1, 18:19], "ln_mean"), (sh.stat[0:1, 18:19], "ln_mean"), AF.Ln)
            tr.act((sh.stat[0:1, 18:19], "ln_mean"), (sh.stat[0:1, 18:19], "ln_mean"),
                   AF.Exp, scale=0.5)
            tr.s_copy((sh.stat[0:1, 32 + h:33 + h], "ln_bound"),
                      (sh.stat[0:1, 18:19], "ln_mean"))
            tr.ts((sh.stat[0:1, 18:19], "ln_mean"), (sh.stat[0:1, 18:19], "ln_mean"),
                  -0.125, None, ALU.mult)
            tr.mm((pb[7][:, 0:1], "pb7"), cs.ones_row_f32[:], (sh.stat[0:1, 18:19], "ln_mean"))
            tr.s_copy((negbs[:, h:h + 1], "sh_col"), (pb[7][:, 0:1], "pb7"))
            tr.reduce((sh.col[rows, 32:33], "sh_col"), (kslab[0][rows, :], kslab[1]),
                      ALU.add)
            tr.copy((ks16[rows, 100:101], "sh_col"), (sh.col[rows, 32:33], "sh_col"))
            pbig_s = _ap(pb[0]).tensor
            for qt in range(NTC):
                qv = (qslab[0][rows, qt * 128:(qt + 1) * 128], qslab[1])
                for half in range(2):
                    for j in range(4):
                        kb = half * 4 + j
                        tr.mm((pb[j][:], f"pb{j}"), qv,
                              (kslab[0][rows, kb * 512:(kb + 1) * 512], kslab[1]))
                    tr.reduce((sh.moeT[:, half:half + 1], "moeT"),
                              (pbig_s[:, 0:2048], "pb0"), ALU.max,
                              xreads=("pb1", "pb2", "pb3"))
                tr.reduce((sh.moeT[:, 8:9], "moeT"), (sh.moeT[:, 0:2], "moeT"), ALU.max)
                tr.mm((pb[4][:, 0:1], "pb4"), qv, (ks16[rows, 100:101], "sh_col"))
                tr.ts((sh.moeT[:, 9:10], "moeT"), (pb[4][:, 0:1], "pb4"),
                      1.0 / T, None, ALU.mult)
                tr.tt((statall[:, h * 32 + qt:h * 32 + qt + 1], "A2"),
                      (sh.moeT[:, 8:9], "moeT"), (sh.moeT[:, 9:10], "moeT"), ALU.subtract)


def emit_l1a_select(tr, cs, sh, pb, statall, mask_td):
    U = 1228.0
    lo = (sh.stat[0:1, 64:80], "ln_bis")
    hi = (sh.stat[0:1, 80:96], "ln_bis")
    mid = (sh.stat[0:1, 96:112], "ln_bis")
    cnt = (sh.stat[0:1, 112:128], "ln_bis")
    tmp = (sh.stat[0:1, 128:144], "ln_bis")
    tr.memset(lo, 0.0)
    tr.ts(hi, (sh.stat[0:1, 32:48], "ln_bound"), 2.0, 1.0, ALU.mult, ALU.add)
    msk = sh.xb[0]
    red = (sh.xb[1][:, 0:16], "sh_xb1")
    for it in range(60):
        tr.tt(mid, lo, hi, ALU.add)
        tr.ts(mid, mid, 0.5, None, ALU.mult)
        tr.mm((pb[3][:], "pb3"), cs.ones_row_f32[:],
              (_ap(mid).unsqueeze(2).broadcast_to([1, 16, 32]), "ln_bis"))
        tr.tt((msk[:], "sh_xb0"), (statall[:], "A2"), (pb[3][:], "pb3"), ALU.is_ge)
        tr.reduce(red, (msk[:].rearrange("p (h w) -> p h w", h=16), "sh_xb0"), ALU.add)
        tr.mm((pb[4][0:1, 0:16], "pb4"), cs.ones_col_f32[:], red)
        tr.ts(cnt, (pb[4][0:1, 0:16], "pb4"), U, None, ALU.is_ge)
        tr.tt(tmp, mid, lo, ALU.subtract)
        tr.tt(tmp, cnt, tmp, ALU.mult)
        tr.tt(lo, lo, tmp, ALU.add)
        tr.tt(tmp, hi, mid, ALU.subtract)
        tr.tt(tmp, cnt, tmp, ALU.mult)
        tr.tt(hi, mid, tmp, ALU.add)
    for h in range(16):
        tr.mm((pb[5][:, 0:1], "pb5"), cs.ones_row_f32[:],
              (sh.stat[0:1, 64 + h:65 + h], "ln_bis"))
        tr.s_copy((sh.col[:, 33:34], "sh_col"), (pb[5][:, 0:1], "pb5"))
        tr.ts((msk[:, h * 32:(h + 1) * 32], "sh_xb0"),
              (statall[:, h * 32:(h + 1) * 32], "A2"),
              (sh.col[:, 33:34], "sh_col"), None, ALU.is_ge)
    for cb in range(4):
        tr.transpose((pb[6][:, 0:128], "pb6"),
                     (msk[:, cb * 128:(cb + 1) * 128], "sh_xb0"), cs.ident128[:])
        tr.s_copy((sh.g[0][:, 0:128], "sh_g0"), (pb[6][:, 0:128], "pb6"))
        tr.dma(mask_td[cb * 128:(cb + 1) * 128, :], sh.g[0][:, 0:128], queue="gpsimd")


def emit_l1a_attn(tr, cs, sh, pb, qfm3_d, kfm3_d, mask_td, negbs, ctx_d):
    pbig = _ap(pb[0]).tensor
    kslab = (sh.WA[:, 0:4096], "WA")
    qslab = (sh.WB[:, 0:4096], "WB")
    for hp in range(8):
        tr.dma(kslab, kfm3_d[hp * 128:(hp + 1) * 128, :])
        tr.dma(qslab, qfm3_d[hp * 128:(hp + 1) * 128, :])
        for h2 in range(2):
            h = hp * 2 + h2
            base = 64 * h2
            rows = slice(base, base + 64)
            negb = (negbs[:, h:h + 1], "sh_col")
            for qb in range(NTB):
                qcs = slice(qb * 512, (qb + 1) * 512)
                for kt0 in range(0, NTC, 4):
                    for j in range(4):
                        kt = kt0 + j
                        tr.mm((pb[j][:], f"pb{j}"),
                              (kslab[0][rows, kt * 128:(kt + 1) * 128], kslab[1]),
                              (qslab[0][rows, qcs], qslab[1]))
                    pt4 = ((sh.P0 if kt0 < 16 else sh.P1)
                           [:, (kt0 % 16) * 512:((kt0 % 16) + 4) * 512],
                           "P0" if kt0 < 16 else "P1")
                    tr.act(pt4, (pbig[:, 0:2048], "pb0"), AF.Exp, bias=negb,
                           scale=0.125, xreads=("pb1", "pb2", "pb3"))
                cp = (pb[4][0:65, :], "pb4")
                for kt in range(NTC):
                    vt = vaug_tile(sh, kt)
                    tr.mm(cp, (vt[0][:, h * 65:(h + 1) * 65], vt[1]), pt_tile(sh, kt),
                          start=(kt == 0), stop=(kt == 31))
                tr.recip((sh.row[0:1, 0:512], "sh_row"), (pb[4][64:65, :], "pb4"))
                mr = (sh.row[0:1, 512:1024], "sh_row")
                tr.dma(mr, mask_td[h * 32 + qb * 4:h * 32 + qb * 4 + 4, :]
                       .rearrange("a b -> (a b)").unsqueeze(0))
                tr.tt((sh.row[0:1, 0:512], "sh_row"), (sh.row[0:1, 0:512], "sh_row"),
                      mr, ALU.mult)
                tr.mm((pb[5][0:64, :], "pb5"), cs.ones_row_f32[0:1, 0:64],
                      (sh.row[0:1, 0:512], "sh_row"))
                tr.s_copy((sh.g[0][0:64, :], "sh_g0"), (pb[5][0:64, :], "pb5"))
                ctx = sh.xb[4 * h2 + qb // 2][:].bitcast(BF16)
                tr.tt((ctx[0:64, (qb % 2) * 512:(qb % 2 + 1) * 512],
                       f"sh_xb{4 * h2 + qb // 2}"),
                      (pb[4][0:64, :], "pb4"), (sh.g[0][0:64, :], "sh_g0"), ALU.mult)
        for h2 in range(2):
            for j in range(4):
                ctx = sh.xb[4 * h2 + j][:].bitcast(BF16)
                tr.dma(ctx_d[hp * 128 + h2 * 64:hp * 128 + h2 * 64 + 64,
                             j * 1024:(j + 1) * 1024],
                       (ctx[0:64, :], f"sh_xb{4 * h2 + j}"), queue="gpsimd")


def emit_l1f_gate(tr, cs, sh, pb, w, zsrc, g1_d):
    nc = tr.nc
    gwT_ap = sh.A2[:].bitcast(BF16)
    gwT = lambda a, b: (gwT_ap[:, 3072 + a:3072 + b], "A2")
    gwstage = (sh.A2[0:16, 0:1024], "A2")
    tr.dma(gwstage, w["L1_ffn_gate_w"])
    for c in range(NCH):
        tr.transpose((pb[6][0:128, 0:16], "pb6"),
                     (gwstage[0][:, c * 128:(c + 1) * 128], "A2"),
                     cs.ident128[0:16, 0:16])
        tr.s_copy(gwT(c * 16, (c + 1) * 16), (pb[6][0:128, 0:16], "pb6"))
    gb = (sh.moeB[:, 148:164], "moeB")
    tr.dma(gb, w["L1_ffn_gate_b"].unsqueeze(0).broadcast_to([128, 16]))
    big16 = (sh.moeB[:, 128:144], "moeB")
    tr.memset(big16, 1e4)
    tr.tt(big16, big16, cs.iota16[:], ALU.add)
    big4 = (sh.moeB[:, 144:148], "moeB")
    tr.memset(big4, 1e4)
    tr.tt(big4, big4, cs.iota4[:], ALU.add)
    mt = lambda a, b: (sh.moeT[:, a:b], "moeT")
    ma = lambda a, b: (sh.moeA[:, a:b], "moeA")
    mb = lambda a, b: (sh.moeB[:, a:b], "moeB")
    for tb in range(NTB):
        cols = slice(tb * 512, (tb + 1) * 512)
        for c in range(NCH):
            tr.dma(sh.zb[c][:], zsrc[c * 128:(c + 1) * 128, cols])
        for tch in range(4):
            ti = tb * 4 + tch
            tcs = slice(tch * 128, (tch + 1) * 128)
            lp = (pb[tch % 2][:, 0:16], f"pb{tch % 2}")
            for c in range(NCH):
                tr.mm(lp, sh.zb[c][:, tcs], gwT(c * 16, (c + 1) * 16),
                      start=(c == 0), stop=(c == 7))
            lg = ma(0, 16)
            tr.s_copy(lg, lp)
            nm = mt(0, 1)
            tr.reduce(nm, lg, ALU.max, negate=True)
            ex = ma(16, 32)
            tr.act(ex, lg, AF.Exp, bias=nm)
            sm = mt(1, 2)
            tr.reduce(sm, ex, ALU.add)
            rs = mt(2, 3)
            tr.recip(rs, sm)
            orig = mb(0, 16)
            tr.ts(orig, ex, rs, None, ALU.mult)
            scores = ma(32, 48)
            tr.tt(scores, orig, gb, ALU.add)
            for g in range(4):
                sg = ma(32 + g * 4, 36 + g * 4)
                m1g = mt(3, 4)
                tr.reduce(m1g, sg, ALU.max)
                mkg = mt(16, 20)
                tr.ts(mkg, sg, m1g, None, ALU.is_ge)
                p2 = mt(20, 24)
                tr.stt(p2, mkg, -1e30, sg, ALU.mult, ALU.add)
                m2g = mt(4, 5)
                tr.reduce(m2g, p2, ALU.max)
                tr.tt((sh.moeT[:, 8 + g:9 + g], "moeT"), m1g, m2g, ALU.add)
            gs = mt(8, 12)
            mg1 = mt(5, 6)
            tr.reduce(mg1, gs, ALU.max)
            mkg = mt(16, 20)
            tr.ts(mkg, gs, mg1, None, ALU.is_ge)
            t4 = mt(20, 24)
            tr.stt(t4, mkg, -1e4, big4, ALU.mult, ALU.add)
            ig1 = mt(6, 7)
            tr.reduce(ig1, t4, ALU.min)
            ohg1 = mt(24, 28)
            tr.ts(ohg1, cs.iota4[:], ig1, None, ALU.is_equal)
            gs2 = mt(28, 32)
            tr.stt(gs2, ohg1, -1e30, gs, ALU.mult, ALU.add)
            mg2 = mt(5, 6)
            tr.reduce(mg2, gs2, ALU.max)
            tr.ts(mkg, gs2, mg2, None, ALU.is_ge)
            tr.stt(t4, mkg, -1e4, big4, ALU.mult, ALU.add)
            ig2 = mt(7, 8)
            tr.reduce(ig2, t4, ALU.min)
            ohg2 = mt(32, 36)
            tr.ts(ohg2, cs.iota4[:], ig2, None, ALU.is_equal)
            keep = mt(36, 40)
            tr.tt(keep, ohg1, ohg2, ALU.add)
            for g in range(4):
                kadj = mt(12, 13)
                tr.ts(kadj, (sh.moeT[:, 36 + g:37 + g], "moeT"), -1.0, 1e30,
                      ALU.add, ALU.mult)
                tr.ts(ma(48 + g * 4, 52 + g * 4), ma(32 + g * 4, 36 + g * 4),
                      kadj, None, ALU.add)
            sp16 = ma(48, 64)
            m1 = mt(13, 14)
            tr.reduce(m1, sp16, ALU.max)
            mk16 = mb(16, 32)
            tr.ts(mk16, sp16, m1, None, ALU.is_ge)
            t16_ = mb(32, 48)
            tr.stt(t16_, mk16, -1e4, big16, ALU.mult, ALU.add)
            i1 = mt(14, 15)
            tr.reduce(i1, t16_, ALU.min)
            oh1 = mb(48, 64)
            tr.ts(oh1, cs.iota16[:], i1, None, ALU.is_equal)
            sp2 = mb(64, 80)
            tr.stt(sp2, oh1, -1e30, sp16, ALU.mult, ALU.add)
            m2 = mt(13, 14)
            tr.reduce(m2, sp2, ALU.max)
            tr.ts(mk16, sp2, m2, None, ALU.is_ge)
            tr.stt(t16_, mk16, -1e4, big16, ALU.mult, ALU.add)
            i2 = mt(15, 16)
            tr.reduce(i2, t16_, ALU.min)
            oh2 = mb(80, 96)
            tr.ts(oh2, cs.iota16[:], i2, None, ALU.is_equal)
            wtmp = mb(96, 112)
            tr.tt(wtmp, orig, oh1, ALU.mult)
            w1 = mt(13, 14)
            tr.reduce(w1, wtmp, ALU.add)
            tr.tt(wtmp, orig, oh2, ALU.mult)
            w2 = mt(14, 15)
            tr.reduce(w2, wtmp, ALU.add)
            Gc = mb(112, 128)
            tr.ts(Gc, oh1, w1, None, ALU.mult)
            Gc2 = mb(96, 112)
            tr.ts(Gc2, oh2, w2, None, ALU.mult)
            tr.tt(Gc, Gc, Gc2, ALU.add)
            tr.transpose((pb[7][0:16, 0:128], "pb7"), Gc, cs.ident128[:])
            tr.s_copy((sh.g8[:], "sh_g8"), (pb[7][0:16, 0:128], "pb7"))
            tr.dma(g1_d[:, ti * 128:(ti + 1) * 128], (sh.g8[:], "sh_g8"),
                   queue="scalar")


def emit_l1f_experts(tr, cs, sh, pb, w, wb, zsrc, g1_d, yacc_d):
    for e in range(16):
        be1c = load_bias_cols(tr, cs, w["L1_ffn_be1"][e, :], 512, pb[6], sh, 0)
        be3c = load_bias_cols(tr, cs, w["L1_ffn_be3"][e, :], 512, pb[6], sh, 4)
        be2c = load_bias_cols(tr, cs, w["L1_ffn_be2"][e, :], C, pb[7], sh, 8)
        for c in range(NCH):
            tr.dma((sh.WA[:, c * 512:(c + 1) * 512], "WA"),
                   wb["L1_ffn_We1"][e * 1024 + c * 128:e * 1024 + (c + 1) * 128, :],
                   queue="scalar")
            tr.dma((sh.WA[:, 4096 + c * 512:4096 + (c + 1) * 512], "WA"),
                   wb["L1_ffn_We3"][e * 1024 + c * 128:e * 1024 + (c + 1) * 128, :],
                   queue="scalar")
        for ic in range(4):
            tr.dma((sh.WB[:, ic * 1024:(ic + 1) * 1024], "WB"),
                   wb["L1_ffn_We2"][e * 512 + ic * 128:e * 512 + (ic + 1) * 128, :],
                   queue="scalar")
        for tb in range(NTB):
            cols = slice(tb * 512, (tb + 1) * 512)
            for c in range(NCH):
                tr.dma(sh.zb[c][:], zsrc[c * 128:(c + 1) * 128, cols])
            gate = sh.g[2]
            tr.dma(gate[:], g1_d[e:e + 1, cols].broadcast_to([128, 512]))
            for ic in range(4):
                h1p = (pb[0][:], "pb0")
                h3p = (pb[1][:], "pb1")
                for c in range(NCH):
                    tr.mm(h1p, (sh.WA[:, c * 512 + ic * 128:c * 512 + (ic + 1) * 128], "WA"),
                          sh.zb[c][:], start=(c == 0), stop=(c == 7))
                for c in range(NCH):
                    tr.mm(h3p, (sh.WA[:, 4096 + c * 512 + ic * 128:4096 + c * 512 + (ic + 1) * 128],
                                "WA"), sh.zb[c][:], start=(c == 0), stop=(c == 7))
                tr.act(sh.b16[0][:], h1p, AF.Silu, bias=be1c[:, ic:ic + 1])
                tr.act(sh.b16[1][:], h3p, AF.Identity, bias=be3c[:, ic:ic + 1])
                tr.tt(pt_tile(sh, ic), sh.b16[0][:], sh.b16[1][:], ALU.mult)
            for dc in range(NCH):
                op_ = (pb[2 + dc % 2][:], f"pb{2 + dc % 2}")
                for ic in range(4):
                    tr.mm(op_, (sh.WB[:, ic * 1024 + dc * 128:ic * 1024 + (dc + 1) * 128], "WB"),
                          pt_tile(sh, ic), start=(ic == 0), stop=(ic == 3))
                g0 = sh.g[dc % 2]
                tr.stt(g0[:], op_, be2c[:, dc:dc + 1], gate[:], ALU.add, ALU.mult)
                tr.dma(yacc_d[dc * 128:(dc + 1) * 128, cols], g0[:], queue="gpsimd",
                       accum_op=(ALU.add if e > 0 else None))


def emit_l1f_shared_ab(tr, cs, sh, pb, w, wb, zsrc, ab_d):
    for c in range(NCH):
        tr.dma((sh.WA[:, c * 1024:(c + 1) * 1024], "WA"),
               wb["L1_ffn_ws1"][c * 128:(c + 1) * 128, :], queue="scalar")
        tr.dma((sh.WB[:, c * 1024:(c + 1) * 1024], "WB"),
               wb["L1_ffn_ws3"][c * 128:(c + 1) * 128, :], queue="scalar")
    bs1c = load_bias_cols(tr, cs, w["L1_ffn_bs1"], C, pb[6], sh, 0)
    bs3c = load_bias_cols(tr, cs, w["L1_ffn_bs3"], C, pb[7], sh, 8)
    for tb in range(NTB):
        cols = slice(tb * 512, (tb + 1) * 512)
        for c in range(NCH):
            tr.dma(sh.zb[c][:], zsrc[c * 128:(c + 1) * 128, cols])
        for dc in range(NCH):
            ap_ = (pb[dc % 2][:], f"pb{dc % 2}")
            bp_ = (pb[2 + dc % 2][:], f"pb{2 + dc % 2}")
            for c in range(NCH):
                tr.mm(ap_, (sh.WA[:, c * 1024 + dc * 128:c * 1024 + (dc + 1) * 128], "WA"),
                      sh.zb[c][:], start=(c == 0), stop=(c == 7))
            for c in range(NCH):
                tr.mm(bp_, (sh.WB[:, c * 1024 + dc * 128:c * 1024 + (dc + 1) * 128], "WB"),
                      sh.zb[c][:], start=(c == 0), stop=(c == 7))
            tr.act(sh.b16[0][:], ap_, AF.Silu, bias=bs1c[:, dc:dc + 1])
            tr.act(sh.b16[1][:], bp_, AF.Identity, bias=bs3c[:, dc:dc + 1])
            tr.tt(sh.b16[2][:], sh.b16[0][:], sh.b16[1][:], ALU.mult)
            tr.dma(ab_d[dc * 128:(dc + 1) * 128, cols], sh.b16[2][:])


def emit_l1f_final(tr, cs, sh, pb, w, wb, ab_d, yacc_d, xfm3, xfm4):
    for c in range(NCH):
        tr.dma((sh.WA[:, c * 1024:(c + 1) * 1024], "WA"),
               wb["L1_ffn_ws2"][c * 128:(c + 1) * 128, :], queue="scalar")
    bs2c = load_bias_cols(tr, cs, w["L1_ffn_bs2"], C, pb[6], sh, 0)
    for tb in range(NTB):
        cols = slice(tb * 512, (tb + 1) * 512)
        for c in range(NCH):
            tr.dma(sh.zb[c][:], ab_d[c * 128:(c + 1) * 128, cols])
        for dc in range(NCH):
            op_ = (pb[dc % 2][:], f"pb{dc % 2}")
            for c in range(NCH):
                tr.mm(op_, (sh.WA[:, c * 1024 + dc * 128:c * 1024 + (dc + 1) * 128], "WA"),
                      sh.zb[c][:], start=(c == 0), stop=(c == 7))
            g0 = sh.g[dc % 2]
            tr.act(g0[:], op_, AF.Identity, bias=bs2c[:, dc:dc + 1])
            g1 = sh.g[2 + dc % 2]
            tr.dma(g1[:], yacc_d[dc * 128:(dc + 1) * 128, cols])
            tr.tt(g0[:], g0[:], g1[:], ALU.add)
            tr.dma(g1[:], xfm3[dc * 128:(dc + 1) * 128, cols], queue="scalar")
            tr.tt(g0[:], g0[:], g1[:], ALU.add)
            tr.dma(xfm4[dc * 128:(dc + 1) * 128, cols], g0[:], queue="gpsimd")


WEIGHT_SPECS = {
    "L0_attn_dn_g": [C], "L0_attn_dn_b": [C],
    "L0_attn_gates_w": [C, 3 * C], "L0_attn_gates_b": [3 * C],
    "L0_attn_lr_w": [C, 1], "L0_attn_lr_b": [1],
    "L0_attn_out_w": [C, C], "L0_attn_out_b": [C],
    "L0_attn_mn_g": [C], "L0_attn_mn_b": [C],
    "L0_attn_w1": [C, C], "L0_attn_b1": [C],
    "L0_attn_w2": [C, C], "L0_attn_b2": [C],
    "L0_ffn_Wq": [C, C], "L0_ffn_Wk": [C, C], "L0_ffn_Wv": [C, C],
    "L0_ffn_Wo": [C, C], "L0_ffn_bo": [C], "L0_ffn_Wr": [C, 8],
    "L0_ffn_We1": [8, C, 4096], "L0_ffn_be1": [8, 4096],
    "L0_ffn_We2": [8, 4096, C], "L0_ffn_be2": [8, C],
    "L0_ffn_ln1_g": [C], "L0_ffn_ln1_b": [C],
    "L0_ffn_ln2_g": [C], "L0_ffn_ln2_b": [C],
    "L1_attn_Wq": [C, C], "L1_attn_bq": [C], "L1_attn_Wk": [C, C],
    "L1_attn_bk": [C], "L1_attn_Wv": [C, C], "L1_attn_bv": [C],
    "L1_attn_Wo": [C, C], "L1_attn_bo": [C],
    "L1_ffn_gate_w": [16, C], "L1_ffn_gate_b": [16],
    "L1_ffn_We1": [16, C, 512], "L1_ffn_be1": [16, 512],
    "L1_ffn_We2": [16, 512, C], "L1_ffn_be2": [16, C],
    "L1_ffn_We3": [16, C, 512], "L1_ffn_be3": [16, 512],
    "L1_ffn_ws1": [C, C], "L1_ffn_bs1": [C],
    "L1_ffn_ws2": [C, C], "L1_ffn_bs2": [C],
    "L1_ffn_ws3": [C, C], "L1_ffn_bs3": [C],
}

N_STAGES = 1


def build_program(taps=(), stages=99):
    nc = bass.Bass("TRN2", target_bir_lowering=False)
    tr = TR(nc)
    pbig = nc.alloc_psum_tensor("pbig", [128, 4096], F32)
    pb = [pbig[:, i * 512:(i + 1) * 512] for i in range(8)]
    cs = Consts(tr)
    sh = SH(tr)

    class LazyW(dict):
        def __missing__(self, name):
            ap = nc.dram_tensor(name, list(WEIGHT_SPECS[name]), F32,
                                kind="ExternalInput").ap()
            self[name] = ap
            return ap

    w = LazyW()
    w["x"] = nc.dram_tensor("x", [T, C], F32, kind="ExternalInput").ap()
    wb = {}
    for name, (rows, cols) in [(n, (s[0], s[1])) for n, s in PRECAST]:
        full = WEIGHT_SPECS[name]
        wb[name] = nc.dram_tensor("b16_" + name, [rows, cols], BF16,
                                  kind="Internal").ap()

    out = nc.dram_tensor("out", [T, C], F32, kind="ExternalOutput").ap()

    scratch = {}

    def dscr(name, shape, dtype=F32):
        ap = nc.dram_tensor(name, list(shape), dtype, kind="Internal").ap()
        scratch[name] = ap
        return ap

    xfm0 = dscr("xfm0", [C, T])
    xln = dscr("xln", [C, T])
    zbf = dscr("zbf", [C, T], BF16)
    qfm_d = dscr("qfm_d", [C, T], BF16)
    kfm_d = dscr("kfm_d", [C, T], BF16)
    ktm_d = dscr("ktm_d", [T, C], BF16)
    vtm_d = dscr("vtm_d", [T, C], BF16)
    x1a_d = dscr("x1a_d", [C, T])
    z2bf = dscr("z2bf", [C, T], BF16)
    xfm1 = dscr("xfm1", [C, T])

    emit_precast(tr, sh, w, wb)
    emit_entry_transpose(tr, cs, sh, pb, w["x"], xfm0)
    final = xfm0
    if stages >= 2:
        emit_layernorm(tr, cs, sh, pb, xfm0, [(xln, F32)], 1e-5)
        emit_layernorm(tr, cs, sh, pb, xln, [(zbf, BF16)], 1e-5,
                       gamma=w["L0_attn_dn_g"], beta=w["L0_attn_dn_b"])
        final = xln
    if stages >= 3:
        emit_l0a_gates(tr, cs, sh, pb, w, wb, zbf, qfm_d, kfm_d, ktm_d, vtm_d)
    if stages >= 4:
        emit_l0a_recurrence(tr, cs, sh, pb, kfm_d, ktm_d, vtm_d,
                            rpart=globals().get("_rpart", 9))
    if stages >= 5:
        emit_l0a_output(tr, cs, sh, pb, w, wb, qfm_d, xln, x1a_d)
        final = x1a_d
    if stages >= 6:
        emit_layernorm(tr, cs, sh, pb, x1a_d, [(z2bf, BF16)], 1e-5,
                       gamma=w["L0_attn_mn_g"], beta=w["L0_attn_mn_b"])
        emit_l0a_mlp(tr, cs, sh, pb, w, wb, z2bf, x1a_d, xfm0, xfm1)
        final = xfm1

    if stages >= 7:
        x2ln = dscr("x2ln", [C, T])
        qfm2_d = dscr("qfm2_d", [C, T], BF16)
        kfm2_d = dscr("kfm2_d", [C, T], BF16)
        emit_layernorm(tr, cs, sh, pb, xfm1, [(x2ln, F32), (zbf, BF16)], 1e-5)
        emit_l0f_qkv(tr, cs, sh, pb, w, wb, zbf, qfm2_d, kfm2_d)
    if stages >= 8:
        ctx_d = dscr("ctx_d", [C, T], BF16)
        emit_l0f_attn(tr, cs, sh, pb, qfm2_d, kfm2_d, ctx_d)
    if stages >= 9:
        x2a_d = dscr("x2a_d", [C, T])
        out1f_d = dscr("out1f_d", [C, T])
        out1b_d = dscr("out1b_d", [C, T], BF16)
        emit_fm_proj(tr, cs, sh, pb, wb["L0_ffn_Wo"], w["L0_ffn_bo"], ctx_d,
                     x2a_d, res_fm=x2ln)
        emit_layernorm(tr, cs, sh, pb, x2a_d, [(out1f_d, F32), (out1b_d, BF16)],
                       1e-6, gamma=w["L0_ffn_ln1_g"], beta=w["L0_ffn_ln1_b"])
        final = out1f_d
    if stages >= 10:
        g_d = dscr("g_d", [8, T])
        emit_l0f_router(tr, cs, sh, pb, w, out1b_d, g_d)
    if stages >= 11:
        acc_d = dscr("acc_d", [C, T])
        emit_l0f_moe(tr, cs, sh, pb, w, wb, out1b_d, g_d, acc_d)
        final = acc_d
    if stages >= 12:
        x2c_d = dscr("x2c_d", [C, T])
        xfm2 = dscr("xfm2", [C, T])
        emit_add_fm(tr, cs, sh, pb, out1f_d, acc_d, x2c_d)
        emit_layernorm(tr, cs, sh, pb, x2c_d, [], 1e-6,
                       gamma=w["L0_ffn_ln2_g"], beta=w["L0_ffn_ln2_b"],
                       res_fm=xfm1, res_dst=xfm2)
        final = xfm2

    if stages >= 13:
        nbs = sh.col[:, 34:50]
        statall = sh.A2[:, 1040:1552]
        qfm3_d = dscr("qfm3_d", [C, T], BF16)
        kfm3_d = dscr("kfm3_d", [C, T], BF16)
        emit_layernorm(tr, cs, sh, pb, xfm2, [(zbf, BF16)], 1e-5)
        emit_fm_proj(tr, cs, sh, pb, wb["L1_attn_Wq"], w["L1_attn_bq"], zbf,
                     qfm3_d, dst_dtype=BF16)
        emit_fm_proj(tr, cs, sh, pb, wb["L1_attn_Wk"], w["L1_attn_bk"], zbf,
                     kfm3_d, dst_dtype=BF16, wslab=(sh.WB, "WB"), bslot=8)
        emit_l1a_qkv_v(tr, cs, sh, pb, w, wb, zbf)
        final = qfm3_d
    if stages >= 14:
        mask_td = dscr("mask_td", [512, 128])
        emit_l1a_stats(tr, cs, sh, pb, qfm3_d, kfm3_d, statall, nbs)
        emit_l1a_select(tr, cs, sh, pb, statall, mask_td)
    if stages >= 15:
        ctx3_d = dscr("ctx3_d", [C, T], BF16)
        xfm3 = dscr("xfm3", [C, T])
        emit_l1a_attn(tr, cs, sh, pb, qfm3_d, kfm3_d, mask_td, nbs, ctx3_d)
        emit_fm_proj(tr, cs, sh, pb, wb["L1_attn_Wo"], w["L1_attn_bo"], ctx3_d,
                     xfm3, res_fm=xfm2)
        final = xfm3
    if stages >= 16:
        g1_d = dscr("g1_d", [16, T])
        yacc_d = dscr("yacc_d", [C, T])
        emit_layernorm(tr, cs, sh, pb, xfm3, [(zbf, BF16)], 1e-5)
        emit_l1f_gate(tr, cs, sh, pb, w, zbf, g1_d)
    if stages >= 17:
        emit_l1f_experts(tr, cs, sh, pb, w, wb, zbf, g1_d, yacc_d)
        final = yacc_d
    if stages >= 18:
        ab_d = dscr("ab_d", [C, T], BF16)
        xfm4 = dscr("xfm4", [C, T])
        emit_l1f_shared_ab(tr, cs, sh, pb, w, wb, zbf, ab_d)
        emit_l1f_final(tr, cs, sh, pb, w, wb, ab_d, yacc_d, xfm3, xfm4)
        final = xfm4

    emit_exit_transpose(tr, cs, sh, pb, final, out)

    if taps:
        for tname in taps:
            ap = scratch[tname]
            tout = nc.dram_tensor("tap_" + tname, list(ap.shape), ap.dtype,
                                  kind="ExternalOutput").ap()
            tr.dma(tout, ap)

    tr.finalize_and_emit()
    used = [k for k in w if k != "x"]
    return nc, used


def kernel(**inputs):
    x = np.ascontiguousarray(inputs["L0_attn_x"]).astype(np.float32, copy=False)
    g = globals()
    nc, used = build_program(taps=g.get("_DEBUG_TAPS", ()), stages=g.get("_stages", 99))
    core_ids = g.get("_cores", None) or list(range(8))
    in_maps = []
    for core in core_ids:
        m = {"x": x[core]}
        for name in used:
            m[name] = np.ascontiguousarray(inputs[name]).astype(np.float32, copy=False)
        in_maps.append(m)
    res = run_bass_kernel_spmd(nc, in_maps, core_ids=core_ids)
    taps = g.get("_DEBUG_TAPS", ())
    if taps:
        g["_taps"] = [{t: r["tap_" + t] for t in taps} for r in res.results]
    return np.stack([r["out"] for r in res.results], axis=0)


# version 06
# speedup vs baseline: 1.0623x; 1.0623x over previous
import numpy as np
import concourse.bass as bass
import concourse.mybir as mybir
from concourse.bass_utils import run_bass_kernel_spmd

F32 = mybir.dt.float32
BF16 = mybir.dt.bfloat16
AF = mybir.ActivationFunctionType
ALU = mybir.AluOpType
AX = mybir.AxisListType

T = 4096
C = 1024
NCH = C // 128
NTB = T // 512
NTC = T // 128

_DEBUG_TAPS = ()

COMPUTE_ENGINES = ("tensor", "scalar", "vector", "gpsimd")
DMA_QUEUES = ("sync", "gpsimd", "scalar")


_SLOTTED = ("pbig", "WA", "WB", "P0", "P1")


def _slot_keys(name, x):
    pref = "pb" if name == "pbig" else name
    row = 1
    for d in x.tensor.shape[1:]:
        row *= d
    off = x.offset % row
    width = 1
    for d in x.shape[1:]:
        width *= d
    return [f"{pref}{i}" for i in range(off // 512, (off + width - 1) // 512 + 1)]


def _keys(x):
    if isinstance(x, tuple):
        ap = x[0]
        t = getattr(ap, "tensor", None)
        if t is not None and t.name in _SLOTTED:
            return _slot_keys(t.name, ap)
        return [x[1]]
    t = getattr(x, "tensor", None)
    if t is None:
        if x.name in _SLOTTED:
            return _slot_keys(x.name, x[tuple(slice(None) for _ in x.shape)])
        return [x.name]
    if t.name in _SLOTTED:
        return _slot_keys(t.name, x)
    return [t.name]


def _ap(x):
    if isinstance(x, tuple):
        x = x[0]
    if getattr(x, "tensor", None) is None and hasattr(x, "shape"):
        return x[tuple(slice(None) for _ in x.shape)]
    return x


class Op:
    __slots__ = ("eng", "fn", "reads", "writes", "dma", "qidx", "eidx",
                 "waits", "marked", "cnt")

    def __init__(self, eng, fn, reads, writes, dma):
        self.eng = eng
        self.fn = fn
        self.reads = reads
        self.writes = writes
        self.dma = dma
        self.waits = []
        self.marked = False
        self.cnt = 0


class TR:

    def __init__(self, nc):
        self.nc = nc
        self.ops = []
        self.uid = 0

    def name(self, base):
        self.uid += 1
        return f"{base}_{self.uid}"

    def arena(self, name, shape, dtype):
        if not hasattr(self, "_arenas"):
            self._arenas = {}
        if name not in self._arenas:
            self._arenas[name] = self.nc.alloc_sbuf_tensor(name, shape, dtype)
        return self._arenas[name]

    def constcol(self, val):
        if not hasattr(self, "_consts"):
            self._consts = {}
        if val not in self._consts:
            t = self.nc.alloc_sbuf_tensor(self.name(f"cc_{val}"), [128, 1], F32)
            self.memset(t, float(val))
            self._consts[val] = t
        return self._consts[val]

    def _rec(self, eng, fn, reads, writes, dma=False, xreads=(), xwrites=()):
        rk = [k for r in reads for k in _keys(r)] + list(xreads)
        wk = [k for w in writes for k in _keys(w)] + list(xwrites)
        self.ops.append(Op(eng, fn, rk, wk, dma))

    def mm(self, out, lhsT, rhs, start=True, stop=True):
        o, l, r = _ap(out), _ap(lhsT), _ap(rhs)
        self._rec("tensor", lambda e: e.matmul(o, l, r, start=start, stop=stop),
                  [lhsT, rhs], [out])

    def transpose(self, out, in_, ident):
        o, i, d = _ap(out), _ap(in_), _ap(ident)
        self._rec("tensor", lambda e: e.transpose(o, i, d), [in_, ident], [out])

    def act(self, out, in_, func, bias=0.0, scale=1.0, accum_out=None,
            xreads=(), xwrites=()):
        if isinstance(bias, (int, float)) and float(bias) != 0.0:
            p = _ap(out).shape[0]
            bias = self.constcol(float(bias))[0:p, 0:1]
        o, i = _ap(out), _ap(in_)
        b = _ap(bias) if not isinstance(bias, (int, float)) else float(bias)
        s = _ap(scale) if not isinstance(scale, (int, float)) else float(scale)
        ao = _ap(accum_out) if accum_out is not None else None
        reads = [in_] + [x for x in (bias, scale) if not isinstance(x, (int, float))]
        writes = [out] + ([accum_out] if accum_out is not None else [])
        self._rec("scalar",
                  lambda e: e.activation(o, i, func, bias=b, scale=s, accum_out=ao),
                  reads, writes, xreads=xreads, xwrites=xwrites)

    def s_copy(self, out, in_):
        o, i = _ap(out), _ap(in_)
        self._rec("scalar", lambda e: e.copy(o, i), [in_], [out])

    def square(self, out, in_):
        o, i = _ap(out), _ap(in_)
        self._rec("scalar", lambda e: e.square(o, i), [in_], [out])

    def tt(self, out, in0, in1, op, eng="vector"):
        o, a, b = _ap(out), _ap(in0), _ap(in1)
        self._rec(eng, lambda e: e.tensor_tensor(o, a, b, op), [in0, in1], [out])

    def ts(self, out, in0, s1, s2, op0, op1=ALU.bypass, eng="vector"):
        o, a = _ap(out), _ap(in0)
        v1 = _ap(s1) if not isinstance(s1, (int, float)) else float(s1)
        v2 = (_ap(s2) if not isinstance(s2, (int, float)) else
              (float(s2) if s2 is not None else None))
        reads = [in0] + [x for x in (s1, s2) if x is not None and not isinstance(x, (int, float))]
        self._rec(eng, lambda e: e.tensor_scalar(o, a, v1, v2, op0, op1), reads, [out])

    def stt(self, out, in0, scalar, in1, op0, op1, eng="vector"):
        o, a, b = _ap(out), _ap(in0), _ap(in1)
        s = _ap(scalar) if not isinstance(scalar, (int, float)) else float(scalar)
        reads = [in0, in1] + ([scalar] if not isinstance(scalar, (int, float)) else [])
        self._rec(eng, lambda e: e.scalar_tensor_tensor(o, a, s, b, op0, op1),
                  reads, [out])

    def reduce(self, out, in_, op, negate=False, eng="vector", xreads=()):
        o, i = _ap(out), _ap(in_)
        self._rec(eng, lambda e: e.tensor_reduce(o, i, AX.X, op, negate=negate),
                  [in_], [out], xreads=xreads)

    def recip(self, out, in_):
        o, i = _ap(out), _ap(in_)
        self._rec("vector", lambda e: e.reciprocal(o, i), [in_], [out])

    def select(self, out, mask, on_true, on_false):
        o, m, a, b = _ap(out), _ap(mask), _ap(on_true), _ap(on_false)
        self._rec("vector", lambda e: e.select(o, m, a, b), [mask, on_true, on_false], [out])

    def copy(self, out, in_, eng="vector"):
        o, i = _ap(out), _ap(in_)
        self._rec(eng, lambda e: e.tensor_copy(o, i), [in_], [out])

    def memset(self, out, val, eng="gpsimd"):
        o = _ap(out)
        self._rec(eng, lambda e: e.memset(o, val), [], [out])

    def iota(self, out, pattern, base=0, chmul=0):
        o = _ap(out)
        self._rec("gpsimd",
                  lambda e: e.iota(o, pattern=pattern, base=base, channel_multiplier=chmul,
                                   allow_small_or_imprecise_dtypes=True), [], [out])

    def affine(self, out, in_, compare_op, fill, pattern, chmul, base=0):
        o, i = _ap(out), _ap(in_)
        self._rec("gpsimd",
                  lambda e: e.affine_select(out=o, in_=i, compare_op=compare_op,
                                            fill=fill, base=base, pattern=pattern,
                                            channel_multiplier=chmul), [in_], [out])

    def dma(self, out, in_, queue="sync", accum_op=None, slow_ok=False):
        o, i = _ap(out), _ap(in_)
        kw = {}
        if accum_op is not None:
            kw["accum_op"] = accum_op
        if slow_ok:
            kw["allow_slow_non_contiguous"] = True
        reads = [in_] + ([out] if accum_op is not None else [])
        self._rec(queue, lambda e: e.dma_start(out=o, in_=i, **kw), reads, [out],
                  dma=True)

    NLANES = 16

    def finalize_and_emit(self):
        nc = self.nc
        isem = {e: nc.alloc_semaphore(f"si_{e}") for e in COMPUTE_ENGINES}
        dsem = {q: [nc.alloc_semaphore(f"sd_{q}{j}") for j in range(self.NLANES)]
                for q in DMA_QUEUES}
        eidx = {}
        qidx = {}
        for op in self.ops:
            if op.dma:
                op.qidx = qidx.get(op.eng, 0)
                qidx[op.eng] = op.qidx + 1
            else:
                op.eidx = eidx.get(op.eng, 0)
                eidx[op.eng] = op.eidx + 1

        ordn = {}
        cnte = {}
        for n, op in enumerate(self.ops):
            ordn[id(op)] = cnte.get(op.eng, 0)
            cnte[op.eng] = ordn[id(op)] + 1
        last_w = {}
        readers = {}
        deps = [[] for _ in self.ops]
        for n, op in enumerate(self.ops):
            need = []
            for r in op.reads:
                w = last_w.get(r)
                if w is not None:
                    need.append(w)
            for w_ in op.writes:
                lw = last_w.get(w_)
                if lw is not None:
                    need.append(lw)
                need.extend(readers.get(w_, ()))
            for p in need:
                if p is op:
                    continue
                if p.dma or p.eng != op.eng or op.eng in ("gpsimd", "vector", "scalar"):
                    deps[n].append(p)
                    p.marked = True
            for r in op.reads:
                readers.setdefault(r, []).append(op)
            for w_ in op.writes:
                last_w[w_] = op
                readers[w_] = []
        ecnt = {e: 0 for e in COMPUTE_ENGINES}
        for op in self.ops:
            if op.dma:
                op.cnt = (op.qidx // self.NLANES + 1) * 16
            else:
                if op.marked:
                    ecnt[op.eng] += 1
                op.cnt = ecnt[op.eng]
        known = {e: {} for e in COMPUTE_ENGINES + ("sync",)}
        for n, op in enumerate(self.ops):
            kn = known[op.eng]
            for p in deps[n]:
                sem = (dsem[p.eng][p.qidx % self.NLANES] if p.dma
                       else isem[p.eng])
                if kn.get(sem, 0) < p.cnt:
                    kn[sem] = p.cnt
                    op.waits.append((sem, p.cnt))

        ops_by_eng = {}
        for op in self.ops:
            ops_by_eng.setdefault(op.eng, []).append(op)

        with nc.Block() as blk:
            def body(eng_name):
                def _(e):
                    for op in ops_by_eng.get(eng_name, ()):
                        if op.dma and op.qidx >= self.NLANES:
                            e.wait_ge(dsem[op.eng][op.qidx % self.NLANES],
                                      (op.qidx // self.NLANES) * 16)
                        for sem, cnt in op.waits:
                            e.wait_ge(sem, cnt)
                        instr = op.fn(e)
                        if op.dma:
                            instr.then_inc(dsem[op.eng][op.qidx % self.NLANES], 16)
                        elif op.marked:
                            instr.then_inc(isem[op.eng], 1)
                return _

            for e in ("sync", "tensor", "scalar", "vector", "gpsimd"):
                getattr(blk, e)(body(e))

            @blk.sync
            def _(sync):
                for q in DMA_QUEUES:
                    n = qidx.get(q, 0)
                    for j in range(min(n, self.NLANES)):
                        tot = (n - j + self.NLANES - 1) // self.NLANES
                        sync.wait_ge(dsem[q][j], tot * 16)


class Consts:
    def __init__(self, tr):
        nc = tr.nc
        self.ident128 = nc.alloc_sbuf_tensor("ident128", [128, 128], F32)
        self.ident64 = nc.alloc_sbuf_tensor("ident64", [128, 64], F32)
        self.mask64_low = nc.alloc_sbuf_tensor("mask64_low", [128, 64], F32)
        self.mask64_up = nc.alloc_sbuf_tensor("mask64_up", [128, 64], F32)
        self.ones_col_f32 = nc.alloc_sbuf_tensor("ones_col_f32", [128, 1], F32)
        self.ones_row_f32 = nc.alloc_sbuf_tensor("ones_row_f32", [1, 128], F32)
        self.iota16 = nc.alloc_sbuf_tensor("iota16", [128, 16], F32)
        self.iota8 = nc.alloc_sbuf_tensor("iota8", [128, 8], F32)
        self.iota4 = nc.alloc_sbuf_tensor("iota4", [128, 4], F32)
        tr.memset(self.ident128, 0.0)
        tr.affine(self.ident128, self.ident128, ALU.not_equal, 1.0, [[-1, 128]], 1)
        tr.memset(self.ident64, 0.0)
        tr.memset(self.mask64_low, 1.0)
        tr.memset(self.mask64_up, 1.0)
        for hrows in (slice(0, 64), slice(64, 128)):
            iv = (self.ident64[hrows, :], "ident64")
            lv = (self.mask64_low[hrows, :], "mask64_low")
            uv = (self.mask64_up[hrows, :], "mask64_up")
            tr.affine(iv, iv, ALU.not_equal, 1.0, [[-1, 64]], 1)
            tr.affine(lv, lv, ALU.is_gt, 0.0, [[-1, 64]], 1)
            tr.affine(uv, uv, ALU.is_gt, 0.0, [[1, 64]], -1)
        tr.memset(self.ones_col_f32, 1.0)
        tr.memset(self.ones_row_f32, 1.0)
        tr.iota(self.iota16, [[1, 16]])
        tr.iota(self.iota8, [[1, 8]])
        tr.iota(self.iota4, [[1, 4]])


class SH:

    def __init__(self, tr):
        nc = tr.nc
        self.A0 = tr.arena("A0", [128, 8192], F32)
        self.A1 = tr.arena("A1", [128, 8192], F32)
        self.A2 = tr.arena("A2", [128, 2048], F32)
        self.P0 = tr.arena("P0", [128, 8192], BF16)
        self.P1 = tr.arena("P1", [128, 8192], BF16)
        self.WA = tr.arena("WA", [128, 8192], BF16)
        self.WB = tr.arena("WB", [128, 8192], BF16)
        self.xb = [nc.alloc_sbuf_tensor(f"sh_xb{i}", [128, 512], F32) for i in range(8)]
        self.zb = [nc.alloc_sbuf_tensor(f"sh_zb{i}", [128, 512], BF16) for i in range(8)]
        self.g = [nc.alloc_sbuf_tensor(f"sh_g{i}", [128, 512], F32) for i in range(4)]
        self.b16 = [nc.alloc_sbuf_tensor(f"sh_b{i}", [128, 512], BF16) for i in range(4)]
        self.row = nc.alloc_sbuf_tensor("sh_row", [1, 1024], F32)
        self.stat = nc.alloc_sbuf_tensor("sh_stat", [1, 2048], F32)
        self.col = nc.alloc_sbuf_tensor("sh_col", [128, 64], F32)
        self.bcol = nc.alloc_sbuf_tensor("sh_bcol", [128, 48], F32)
        self.moeA = nc.alloc_sbuf_tensor("sh_moeA", [128, 256], F32)
        self.moeB = nc.alloc_sbuf_tensor("sh_moeB", [128, 256], F32)
        self.moeT = nc.alloc_sbuf_tensor("sh_moeT", [128, 64], F32)
        self.g8 = nc.alloc_sbuf_tensor("sh_g8", [16, 128], F32)
        self.bstage = nc.alloc_sbuf_tensor("sh_bstage", [32, 128], F32)
        self.lrch = nc.alloc_sbuf_tensor("sh_lrch", [128, 32], F32)
        self.kf = [nc.alloc_sbuf_tensor("sh_kf0", [128, 1024], F32)] * 2
        self.kt = [nc.alloc_sbuf_tensor("sh_kt0", [128, 1024], BF16)] * 2
        self.vt = [nc.alloc_sbuf_tensor("sh_vt0", [128, 1024], BF16)] * 2
        self.u16 = nc.alloc_sbuf_tensor("sh_u16", [128, 1024], BF16)
        self.ulr16 = nc.alloc_sbuf_tensor("sh_ulr16", [128, 1024], BF16)
        self.r16 = nc.alloc_sbuf_tensor("sh_r16", [128, 1024], BF16)
        ch = {}
        for nm in ("L", "LT", "AT", "P", "PT"):
            ch[nm] = nc.alloc_sbuf_tensor(f"ch_{nm}", [128, 64], F32)
        self.ch = ch
        self.dnt = nc.alloc_sbuf_tensor("ch_dnt", [128, 64], BF16)
        self.l10t = nc.alloc_sbuf_tensor("ch_l10t", [64, 64], BF16)


def load_bias_cols(tr, cs, dram_vec, n, psum_bank, sh, slot):
    nch = (n + 127) // 128
    out = sh.bcol[:, slot:slot + nch]
    tr.dma(sh.bstage[0:nch, :], dram_vec.rearrange("(a b) -> a b", b=128))
    tr.transpose(psum_bank[:, 0:nch], sh.bstage[0:nch, :], cs.ident128[0:nch, 0:nch])
    tr.s_copy(out, psum_bank[:, 0:nch])
    return out


PRECAST = [
    ("L0_attn_gates_w", [C, 3 * C]), ("L0_attn_out_w", [C, C]),
    ("L0_attn_w1", [C, C]), ("L0_attn_w2", [C, C]),
    ("L0_ffn_Wq", [C, C]), ("L0_ffn_Wk", [C, C]), ("L0_ffn_Wv", [C, C]),
    ("L0_ffn_Wo", [C, C]),
    ("L0_ffn_We1", [8 * C, 4096]), ("L0_ffn_We2", [8 * 4096, C]),
    ("L1_attn_Wq", [C, C]), ("L1_attn_Wk", [C, C]), ("L1_attn_Wv", [C, C]),
    ("L1_attn_Wo", [C, C]),
    ("L1_ffn_We1", [16 * C, 512]), ("L1_ffn_We2", [16 * 512, C]),
    ("L1_ffn_We3", [16 * C, 512]),
    ("L1_ffn_ws1", [C, C]), ("L1_ffn_ws2", [C, C]), ("L1_ffn_ws3", [C, C]),
]


def emit_precast(tr, sh, w, wb):
    slabs = [(sh.WA, "WA"), (sh.WB, "WB"), (sh.P0, "P0"), (sh.P1, "P1")]
    i = 0
    for name, (rows, cols) in [(n, s) for n, s in PRECAST]:
        src2d = w[name]
        if len(src2d.shape) == 3:
            src2d = src2d.rearrange("e a b -> (e a) b")
        step = min(8192, cols)
        for r in range(rows // 128):
            for c0 in range(0, cols, step):
                sl, key = slabs[i % 4]
                i += 1
                n = min(step, cols - c0)
                tr.dma((sl[:, 0:n], key), src2d[r * 128:(r + 1) * 128, c0:c0 + n],
                       queue="gpsimd")
                tr.dma(wb[name][r * 128:(r + 1) * 128, c0:c0 + n], (sl[:, 0:n], key))


def emit_entry_transpose(tr, cs, sh, pb, src_tm, dst_fm):
    for tt in range(NTC):
        tin = (sh.A0[:, (tt % 2) * 1024:(tt % 2 + 1) * 1024], "A0")
        tr.dma(tin, src_tm[tt * 128:(tt + 1) * 128, :])
        for c in range(NCH):
            i = tt * NCH + c
            ps = pb[4 + i % 4]
            o = sh.g[i % 4]
            tr.transpose(ps[:, 0:128], (tin[0][:, c * 128:(c + 1) * 128], "A0"),
                         cs.ident128[:])
            tr.s_copy(o[:, 0:128], ps[:, 0:128])
            tr.dma(dst_fm[c * 128:(c + 1) * 128, tt * 128:(tt + 1) * 128], o[:, 0:128],
                   queue="gpsimd")


def emit_exit_transpose(tr, cs, sh, pb, src_fm, dst_tm):
    tiles = [(sh.A0[:, 0:2048], "A0"), (sh.A1[:, 0:2048], "A1")]
    seq = [(c, h) for c in range(NCH) for h in range(2)]
    for i, (c, h) in enumerate(seq):
        tin = tiles[i % 2]
        tr.dma(tin, src_fm[c * 128:(c + 1) * 128, h * 2048:(h + 1) * 2048])
        for j in range(16):
            n = i * 16 + j
            tt = h * 16 + j
            ps = pb[4 + n % 4]
            o = sh.g[n % 4]
            tr.transpose(ps[:, 0:128], (tin[0][:, j * 128:(j + 1) * 128], tin[1]),
                         cs.ident128[:])
            tr.s_copy(o[:, 0:128], ps[:, 0:128])
            tr.dma(dst_tm[tt * 128:(tt + 1) * 128, c * 128:(c + 1) * 128], o[:, 0:128],
                   queue="gpsimd")


def emit_layernorm(tr, cs, sh, pb, src_fm, dsts, eps, gamma=None, beta=None,
                   res_fm=None, res_dst=None):
    xb = sh.xb
    sq = [sh.g[0], sh.g[1]]
    mean_r = (sh.stat[0:1, 0:512], "ln_mean")
    m2_r = (sh.stat[0:1, 512:1024], "ln_m2")
    a_r = (sh.stat[0:1, 1024:1536], "ln_a")
    b_r = (sh.stat[0:1, 1536:2048], "ln_b")
    yf = [sh.g[2], sh.g[3]]
    yb16 = [sh.b16[0], sh.b16[1]]
    gcol = bcol = None
    if gamma is not None:
        gcol = load_bias_cols(tr, cs, gamma, C, pb[6], sh, 0)
        bcol = load_bias_cols(tr, cs, beta, C, pb[7], sh, 8)

    sum_ps = (pb[0][0:1, :], "pb0")
    sq_ps = (pb[1][0:1, :], "pb1")
    for tb in range(NTB):
        cols = slice(tb * 512, (tb + 1) * 512)
        for c in range(NCH):
            tr.dma(xb[c][:], src_fm[c * 128:(c + 1) * 128, cols])
            tr.square(sq[c % 2][:], xb[c][:])
            tr.mm(sum_ps, cs.ones_col_f32[:], xb[c][:], start=(c == 0), stop=(c == 7))
            tr.mm(sq_ps, cs.ones_col_f32[:], sq[c % 2][:], start=(c == 0), stop=(c == 7))
        tr.ts(mean_r, sum_ps, 1.0 / C, None, ALU.mult)
        tr.tt(m2_r, mean_r, mean_r, ALU.mult)
        tr.stt(m2_r, sq_ps, 1.0 / C, m2_r, ALU.mult, ALU.subtract)
        tr.act(a_r, m2_r, AF.Ln, bias=float(eps))
        tr.act(a_r, a_r, AF.Exp, scale=-0.5)
        tr.tt(b_r, mean_r, a_r, ALU.mult)
        tr.ts(b_r, b_r, -1.0, None, ALU.mult)
        tr.mm((pb[2][:], "pb2"), cs.ones_row_f32[:], a_r)
        tr.mm((pb[3][:], "pb3"), cs.ones_row_f32[:], b_r)
        for c in range(NCH):
            y = yf[c % 2]
            tr.tt(y[:], xb[c][:], (pb[2][:], "pb2"), ALU.mult)
            tr.tt(y[:], y[:], (pb[3][:], "pb3"), ALU.add)
            if gamma is not None:
                tr.act(y[:], y[:], AF.Identity, bias=bcol[:, c:c + 1], scale=gcol[:, c:c + 1])
            wrote16 = False
            for dst, dt in dsts:
                if dt == F32:
                    tr.dma(dst[c * 128:(c + 1) * 128, cols], y[:])
                else:
                    if not wrote16:
                        tr.copy(yb16[c % 2][:], y[:], eng="gpsimd")
                        wrote16 = True
                    tr.dma(dst[c * 128:(c + 1) * 128, cols], yb16[c % 2][:],
                           queue="scalar")
            if res_fm is not None:
                rbuf = (sh.A2[:, (c % 2) * 512:(c % 2 + 1) * 512], "A2")
                tr.dma(rbuf, res_fm[c * 128:(c + 1) * 128, cols])
                tr.tt(y[:], y[:], rbuf, ALU.add)
                tr.dma(res_dst[c * 128:(c + 1) * 128, cols], y[:], queue="gpsimd")


def emit_l0a_gates(tr, cs, sh, pb, w, wb, zbf, qfm_d, kfm_d, ktm_d, vtm_d):
    gwq = sh.WA
    for c in range(NCH):
        tr.dma(gwq[:, c * 1024:(c + 1) * 1024],
               wb["L0_attn_gates_w"][c * 128:(c + 1) * 128, 0:1024], queue="scalar")
        dst = sh.P0 if c < 4 else sh.P1
        tr.dma(dst[:, (c % 4) * 2048:((c % 4) + 1) * 2048],
               wb["L0_attn_gates_w"][c * 128:(c + 1) * 128, 1024:3072], queue="scalar")
    bias_q = load_bias_cols(tr, cs, w["L0_attn_gates_b"][0:1024], C, pb[6], sh, 16)
    biasg = lambda j: (sh.A1[:, j * 512:(j + 1) * 512], "A1")
    for j in range(4):
        tr.dma(biasg(j),
               w["L0_attn_gates_b"][1024 + j * 512:1024 + (j + 1) * 512]
               .unsqueeze(0).broadcast_to([128, 512]))
    lrw_f = load_bias_cols(tr, cs, w["L0_attn_lr_w"].rearrange("a o -> (a o)"), C,
                           pb[7], sh, 24)
    nc = tr.nc
    lrw = nc.alloc_sbuf_tensor("lrw_bf", [128, 8], BF16)
    tr.copy(lrw[:], lrw_f)
    lrb = nc.alloc_sbuf_tensor("lrb_pp", [128, 1], F32)
    tr.dma(lrb[:], w["L0_attn_lr_b"].unsqueeze(0).broadcast_to([128, 1]))
    identb = nc.alloc_sbuf_tensor("ident128_bf", [128, 128], BF16)
    tr.memset(identb, 0.0)
    tr.affine(identb, identb, ALU.not_equal, 1.0, [[-1, 128]], 1)

    ktmp = (sh.A0[:, 0:1024], "A0")
    ksq = (sh.A0[:, 1024:2048], "A0")
    krstd = (sh.col[:, 0:1], "sh_col")
    psb = (pb[5][:].bitcast(BF16), "pb5")
    for tb in range(NTB):
        cols = slice(tb * 512, (tb + 1) * 512)
        for c in range(NCH):
            tr.dma(sh.zb[c][:], zbf[c * 128:(c + 1) * 128, cols])
        for dc in range(NCH):
            qp = (pb[dc % 2][:], f"pb{dc % 2}")
            for c in range(NCH):
                tr.mm(qp, gwq[:, c * 1024 + dc * 128:c * 1024 + (dc + 1) * 128],
                      sh.zb[c][:], start=(c == 0), stop=(c == 7))
            qt = sh.b16[dc % 2]
            tr.act(qt[:], qp, AF.Identity, bias=bias_q[:, dc:dc + 1])
            tr.dma(qfm_d[dc * 128:(dc + 1) * 128, cols], qt[:])
        for tch in range(4):
            tcs = slice(tch * 128, (tch + 1) * 128)
            trow = tb * 4 + tch
            for c in range(NCH):
                zt = sh.zb[c][:, tcs]
                src = sh.P0 if c < 4 else sh.P1
                for j in range(4):
                    tr.mm((pb[j][:], f"pb{j}"), zt,
                          src[:, (c % 4) * 2048 + j * 512:(c % 4) * 2048 + (j + 1) * 512],
                          start=(c == 0), stop=(c == 7))
                tr.mm((pb[4][:, 0:1], "pb4"), zt, lrw[:, c:c + 1],
                      start=(c == 0), stop=(c == 7))
            tr.act((sh.lrch[:, trow:trow + 1], "lrch"), (pb[4][:, 0:1], "pb4"),
                   AF.Identity, bias=lrb[:, 0:1])
            tr.tt((ktmp[0][:, 0:512], "A0"), (pb[0][:], "pb0"), biasg(0), ALU.add)
            tr.tt((ktmp[0][:, 512:1024], "A0"), (pb[1][:], "pb1"), biasg(1), ALU.add)
            tr.square(ksq, ktmp)
            tr.reduce(krstd, ksq, ALU.add)
            tr.act(krstd, krstd, AF.Ln)
            tr.act(krstd, krstd, AF.Exp, scale=-0.5)
            kn = (sh.A2[:].bitcast(BF16)[:, 0:1024], "A2")
            tr.ts(kn, ktmp, krstd, None, ALU.mult)
            tr.dma(ktm_d[trow * 128:(trow + 1) * 128, :], kn)
            for c2 in range(NCH):
                tr.transpose((psb[0][:, 0:128], "pb5"),
                             (kn[0][:, c2 * 128:(c2 + 1) * 128], "A2"), identb[:])
                tr.s_copy(sh.b16[2][:, 0:128], (psb[0][:, 0:128], "pb5"))
                tr.dma(kfm_d[c2 * 128:(c2 + 1) * 128, trow * 128:(trow + 1) * 128],
                       sh.b16[2][:, 0:128], queue="scalar")
            vn = (sh.A2[:].bitcast(BF16)[:, 1024:2048], "A2")
            tr.tt((vn[0][:, 0:512], "A2"), (pb[2][:], "pb2"), biasg(2), ALU.add)
            tr.tt((vn[0][:, 512:1024], "A2"), (pb[3][:], "pb3"), biasg(3), ALU.add)
            tr.dma(vtm_d[trow * 128:(trow + 1) * 128, :], vn)


def emit_l0a_recurrence(tr, cs, sh, pb, kfm_d, ktm_d, vtm_d, rpart=9):
    S = (sh.A0, "A0")
    W = (sh.A1, "A1")
    for j in range(16):
        tr.memset((sh.A0[:, j * 512:(j + 1) * 512], "A0"), 0.0)
        tr.memset((sh.A1[:, j * 512:(j + 1) * 512], "A1"), 0.0)

    for i in range(NTC):
        kf, kt, vt = sh.kf[i % 2], sh.kt[i % 2], sh.vt[i % 2]
        for c in range(NCH):
            tr.dma(kf[:, c * 128:(c + 1) * 128],
                   kfm_d[c * 128:(c + 1) * 128, i * 128:(i + 1) * 128],
                   queue="gpsimd")
        tr.dma(kt[:], ktm_d[i * 128:(i + 1) * 128, :])
        tr.dma(vt[:], vtm_d[i * 128:(i + 1) * 128, :])
        if rpart < 1:
            continue
        kfA = [kf[:, c * 128:c * 128 + 64] for c in range(NCH)]
        kfB = [kf[:, c * 128 + 64:(c + 1) * 128] for c in range(NCH)]
        KA = (pb[6][0:64, 0:64], "pb6")
        KB = (pb[7][64:128, 0:64], "pb7")
        L10T = (pb[5][0:64, 0:64], "pb5")
        for c in range(NCH):
            tr.mm(KA, kfA[c], kfA[c], start=(c == 0), stop=(c == 7))
        for c in range(NCH):
            tr.mm(KB, kfB[c], kfB[c], start=(c == 0), stop=(c == 7))
        for c in range(NCH):
            tr.mm(L10T, kfA[c], kfB[c], start=(c == 0), stop=(c == 7))
        tr.s_copy(sh.l10t[:], L10T)
        if rpart < 2:
            continue
        ch = sh.ch
        for blk, Kps in ((0, KA), (1, KB)):
            rows = slice(0, 64) if blk == 0 else slice(64, 128)
            L = (ch["L"][rows, :], "ch_L")
            LT = (ch["LT"][rows, :], "ch_LT")
            AT = (ch["AT"][rows, :], "ch_AT")
            P = (ch["P"][rows, :], "ch_P")
            PT = (ch["PT"][rows, :], "ch_PT")
            ml = (cs.mask64_low[rows, :], "mask64_low")
            mu = (cs.mask64_up[rows, :], "mask64_up")
            iv = (cs.ident64[rows, :], "ident64")
            tr.tt(L, Kps, ml, ALU.mult)
            tr.tt(LT, Kps, mu, ALU.mult)
            tr.tt(AT, iv, LT, ALU.subtract)
            tr.copy(P, L)
            tr.copy(PT, LT)
            c1 = (pb[6][rows, 64:128], "pb6")
            c2 = (pb[7][rows, 64:128], "pb7")
            c3 = (pb[5][rows, 64:128], "pb5")
            for step in range(5):
                tr.mm(c1, PT, P)
                tr.mm(c2, P, PT)
                tr.s_copy(P, c1)
                tr.s_copy(PT, c2)
                tr.mm(c3, P, AT)
                tr.tt(AT, AT, c3, ALU.add)
            tr.copy((sh.dnt[rows, :], "ch_dnt"), AT)
        if rpart < 3:
            continue
        rps = [(pb[0][:], "pb0"), (pb[1][:], "pb1")]
        for cc in range(NCH):
            for h in range(2):
                tr.mm(rps[h], kf[:, cc * 128:(cc + 1) * 128],
                      (sh.A0[:, cc * 1024 + h * 512:cc * 1024 + (h + 1) * 512], "A0"),
                      start=(cc == 0), stop=(cc == 7))
        ups = [(pb[2][:], "pb2"), (pb[3][:], "pb3")]
        rsub = rpart - 3 if rpart < 9 else 9
        for h in range(2):
            tr.tt((sh.r16[0:64, h * 512:(h + 1) * 512], "r16"),
                  vt[0:64, h * 512:(h + 1) * 512],
                  (pb[h][0:64, :], f"pb{h}"), ALU.subtract)
            if rsub < 1:
                continue
            tr.mm((ups[h][0][0:64, :], f"pb{2 + h}"), (sh.dnt[0:64, :], "ch_dnt"),
                  (sh.r16[0:64, h * 512:(h + 1) * 512], "r16"))
            tr.s_copy((sh.u16[0:64, h * 512:(h + 1) * 512], "u16"),
                      (ups[h][0][0:64, :], f"pb{2 + h}"))
            if rsub < 2:
                continue
            tr.mm((pb[4 + h][64:128, :], f"pb{4 + h}"), sh.l10t[:],
                  (sh.u16[0:64, h * 512:(h + 1) * 512], "u16"))
            tr.tt((sh.r16[64:128, h * 512:(h + 1) * 512], "r16"),
                  vt[64:128, h * 512:(h + 1) * 512],
                  (pb[h][64:128, :], f"pb{h}"), ALU.subtract)
            tr.tt((sh.r16[64:128, h * 512:(h + 1) * 512], "r16"),
                  (sh.r16[64:128, h * 512:(h + 1) * 512], "r16"),
                  (pb[4 + h][64:128, :], f"pb{4 + h}"), ALU.subtract)
            if rsub < 3:
                continue
            tr.mm((ups[h][0][64:128, :], f"pb{2 + h}"), (sh.dnt[64:128, :], "ch_dnt"),
                  (sh.r16[64:128, h * 512:(h + 1) * 512], "r16"))
            tr.s_copy((sh.u16[64:128, h * 512:(h + 1) * 512], "u16"),
                      (ups[h][0][64:128, :], f"pb{2 + h}"))
            if rsub < 4:
                continue
            tr.ts((sh.ulr16[:, h * 512:(h + 1) * 512], "ulr16"),
                  (sh.u16[:, h * 512:(h + 1) * 512], "u16"),
                  (sh.lrch[:, i:i + 1], "lrch"), None, ALU.mult)
        if rpart < 4:
            continue
        pbig_r = _ap(pb[0]).tensor
        for cc in range(0, NCH, 2):
            for src16, arena, akey in ((sh.u16, sh.A0, "A0"), (sh.ulr16, sh.A1, "A1")):
                for j in range(2):
                    for h in range(2):
                        tr.mm(pbig_r[:, (4 + j * 2 + h) * 512:(5 + j * 2 + h) * 512],
                              kt[:, (cc + j) * 128:(cc + j + 1) * 128],
                              (src16[:, h * 512:(h + 1) * 512],
                               "u16" if src16 is sh.u16 else "ulr16"))
                dst = (arena[:, cc * 1024:(cc + 2) * 1024], akey)
                tr.tt(dst, dst, pbig_r[:, 2048:4096], ALU.add)


def emit_l0a_output(tr, cs, sh, pb, w, wb, qfm_d, xln_d, x1a_d):
    nc = tr.nc
    for j in range(16):
        tr.s_copy((sh.WA[:, j * 512:(j + 1) * 512], "WA"),
                  (sh.A1[:, j * 512:(j + 1) * 512], "A1"))
    for c in range(NCH):
        tr.dma(sh.WB[:, c * 1024:(c + 1) * 1024],
               wb["L0_attn_out_w"][c * 128:(c + 1) * 128, :], queue="scalar")
    outb = load_bias_cols(tr, cs, w["L0_attn_out_b"], C, pb[6], sh, 0)
    for tb in range(NTB):
        cols = slice(tb * 512, (tb + 1) * 512)
        for c in range(NCH):
            tr.dma(sh.zb[c][:], qfm_d[c * 128:(c + 1) * 128, cols])
        for cp in range(NCH):
            yp = (pb[cp % 2][:], f"pb{cp % 2}")
            for c in range(NCH):
                tr.mm(yp, (sh.WA[:, c * 1024 + cp * 128:c * 1024 + (cp + 1) * 128], "WA"),
                      sh.zb[c][:], start=(c == 0), stop=(c == 7))
            tr.s_copy((sh.P0[:, cp * 512:(cp + 1) * 512], "P0"), yp)
        for dc in range(NCH):
            op_ = (pb[2 + dc % 2][:], f"pb{2 + dc % 2}")
            for cp in range(NCH):
                tr.mm(op_, sh.WB[:, cp * 1024 + dc * 128:cp * 1024 + (dc + 1) * 128],
                      (sh.P0[:, cp * 512:(cp + 1) * 512], "P0"),
                      start=(cp == 0), stop=(cp == 7))
            g0 = sh.g[dc % 2]
            tr.act(g0[:], op_, AF.Identity, bias=outb[:, dc:dc + 1])
            g1 = sh.g[2 + dc % 2]
            tr.dma(g1[:], xln_d[dc * 128:(dc + 1) * 128, cols])
            tr.tt(g1[:], g1[:], g0[:], ALU.add)
            tr.dma(x1a_d[dc * 128:(dc + 1) * 128, cols], g1[:])


def emit_l0a_mlp(tr, cs, sh, pb, w, wb, z2bf, x1a_d, xfm0, xfm1):
    for c in range(NCH):
        tr.dma(sh.WA[:, c * 1024:(c + 1) * 1024],
               wb["L0_attn_w1"][c * 128:(c + 1) * 128, :], queue="scalar")
        tr.dma(sh.WB[:, c * 1024:(c + 1) * 1024],
               wb["L0_attn_w2"][c * 128:(c + 1) * 128, :], queue="scalar")
    b1c = load_bias_cols(tr, cs, w["L0_attn_b1"], C, pb[6], sh, 0)
    b2c = load_bias_cols(tr, cs, w["L0_attn_b2"], C, pb[7], sh, 8)
    for tb in range(NTB):
        cols = slice(tb * 512, (tb + 1) * 512)
        for c in range(NCH):
            tr.dma(sh.zb[c][:], z2bf[c * 128:(c + 1) * 128, cols])
        for f in range(NCH):
            hp = (pb[f % 2][:], f"pb{f % 2}")
            for c in range(NCH):
                tr.mm(hp, (sh.WA[:, c * 1024 + f * 128:c * 1024 + (f + 1) * 128], "WA"),
                      sh.zb[c][:], start=(c == 0), stop=(c == 7))
            tr.act((sh.P0[:, f * 512:(f + 1) * 512], "P0"), hp, AF.Relu,
                   bias=b1c[:, f:f + 1])
        for dc in range(NCH):
            op_ = (pb[2 + dc % 2][:], f"pb{2 + dc % 2}")
            for f in range(NCH):
                tr.mm(op_, (sh.WB[:, f * 1024 + dc * 128:f * 1024 + (dc + 1) * 128], "WB"),
                      (sh.P0[:, f * 512:(f + 1) * 512], "P0"),
                      start=(f == 0), stop=(f == 7))
            g0 = sh.g[dc % 2]
            tr.act(g0[:], op_, AF.Identity, bias=b2c[:, dc:dc + 1])
            g1 = sh.g[2 + dc % 2]
            tr.dma(g1[:], x1a_d[dc * 128:(dc + 1) * 128, cols])
            tr.tt(g1[:], g1[:], g0[:], ALU.add)
            g2 = (sh.A0[:, (dc % 2) * 512:(dc % 2 + 1) * 512], "A0")
            tr.dma(g2, xfm0[dc * 128:(dc + 1) * 128, cols])
            tr.tt(g1[:], g1[:], g2, ALU.add)
            tr.dma(xfm1[dc * 128:(dc + 1) * 128, cols], g1[:])



def vaug_tile(sh, kt):
    if kt < 15:
        return (sh.A0[:].bitcast(BF16)[:, kt * 1040:(kt + 1) * 1040], "A0")
    if kt < 30:
        j = kt - 15
        return (sh.A1[:].bitcast(BF16)[:, j * 1040:(j + 1) * 1040], "A1")
    j = kt - 30
    return (sh.A2[:].bitcast(BF16)[:, j * 1040:(j + 1) * 1040], "A2")


def pt_tile(sh, kt):
    if kt < 16:
        return (sh.P0[:, kt * 512:(kt + 1) * 512], "P0")
    return (sh.P1[:, (kt - 16) * 512:(kt - 15) * 512], "P1")


def emit_fm_proj(tr, cs, sh, pb, w_dram, bias_vec, src_bf, dst, res_fm=None,
                 func=AF.Identity, dst_dtype=F32, wslab=None, bslot=0):
    wsl, wkey = wslab if wslab is not None else (sh.WA, "WA")
    for c in range(NCH):
        tr.dma((wsl[:, c * 1024:(c + 1) * 1024], wkey),
               w_dram[c * 128:(c + 1) * 128, :], queue="scalar")
    bcol = load_bias_cols(tr, cs, bias_vec, C, pb[6], sh, bslot) if bias_vec is not None else None
    for tb in range(NTB):
        cols = slice(tb * 512, (tb + 1) * 512)
        for c in range(NCH):
            tr.dma(sh.zb[c][:], src_bf[c * 128:(c + 1) * 128, cols])
        for dc in range(NCH):
            op_ = (pb[dc % 2][:], f"pb{dc % 2}")
            for c in range(NCH):
                tr.mm(op_, (wsl[:, c * 1024 + dc * 128:c * 1024 + (dc + 1) * 128], wkey),
                      sh.zb[c][:], start=(c == 0), stop=(c == 7))
            if dst_dtype == BF16:
                o = sh.b16[dc % 2]
                if bcol is not None:
                    tr.act(o[:], op_, func, bias=bcol[:, dc:dc + 1])
                else:
                    tr.s_copy(o[:], op_)
                tr.dma(dst[dc * 128:(dc + 1) * 128, cols], o[:])
            else:
                g0 = sh.g[dc % 2]
                if bcol is not None:
                    tr.act(g0[:], op_, func, bias=bcol[:, dc:dc + 1])
                else:
                    tr.s_copy(g0[:], op_)
                if res_fm is not None:
                    g1 = sh.g[2 + dc % 2]
                    tr.dma(g1[:], res_fm[dc * 128:(dc + 1) * 128, cols])
                    tr.tt(g0[:], g0[:], g1[:], ALU.add)
                tr.dma(dst[dc * 128:(dc + 1) * 128, cols], g0[:])


def emit_l0f_qkv(tr, cs, sh, pb, w, wb, zsrc, qfm2_d, kfm2_d):
    for c in range(NCH):
        tr.dma((sh.WA[:, c * 1024:(c + 1) * 1024], "WA"),
               wb["L0_ffn_Wq"][c * 128:(c + 1) * 128, :], queue="scalar")
        tr.dma((sh.WB[:, c * 1024:(c + 1) * 1024], "WB"),
               wb["L0_ffn_Wk"][c * 128:(c + 1) * 128, :], queue="scalar")
        tr.dma((sh.P0[:, c * 1024:(c + 1) * 1024], "P0"),
               wb["L0_ffn_Wv"][c * 128:(c + 1) * 128, :], queue="scalar")
    for tb in range(NTB):
        cols = slice(tb * 512, (tb + 1) * 512)
        for c in range(NCH):
            tr.dma(sh.zb[c][:], zsrc[c * 128:(c + 1) * 128, cols])
        for dc in range(NCH):
            qp = (pb[dc % 2][:], f"pb{dc % 2}")
            kp = (pb[2 + dc % 2][:], f"pb{2 + dc % 2}")
            for c in range(NCH):
                tr.mm(qp, (sh.WA[:, c * 1024 + dc * 128:c * 1024 + (dc + 1) * 128], "WA"),
                      sh.zb[c][:], start=(c == 0), stop=(c == 7))
            for c in range(NCH):
                tr.mm(kp, (sh.WB[:, c * 1024 + dc * 128:c * 1024 + (dc + 1) * 128], "WB"),
                      sh.zb[c][:], start=(c == 0), stop=(c == 7))
            tr.s_copy(sh.b16[0][:], qp)
            tr.dma(qfm2_d[dc * 128:(dc + 1) * 128, cols], sh.b16[0][:])
            tr.s_copy(sh.b16[1][:], kp)
            tr.dma(kfm2_d[dc * 128:(dc + 1) * 128, cols], sh.b16[1][:], queue="scalar")
        for tch in range(4):
            trow = tb * 4 + tch
            tcs = slice(tch * 128, (tch + 1) * 128)
            for c in range(NCH):
                zt = sh.zb[c][:, tcs]
                tr.mm((pb[4][:], "pb4"), zt, (sh.P0[:, c * 1024:c * 1024 + 512], "P0"),
                      start=(c == 0), stop=(c == 7))
                tr.mm((pb[5][:], "pb5"), zt, (sh.P0[:, c * 1024 + 512:(c + 1) * 1024], "P0"),
                      start=(c == 0), stop=(c == 7))
            vt = vaug_tile(sh, trow)
            tr.memset(vt, 1.0)
            for hh in range(8):
                tr.s_copy((vt[0][:, hh * 65:hh * 65 + 64], vt[1]),
                          (pb[4][:, hh * 64:(hh + 1) * 64], "pb4"))
            for hh in range(8):
                tr.s_copy((vt[0][:, (8 + hh) * 65:(8 + hh) * 65 + 64], vt[1]),
                          (pb[5][:, hh * 64:(hh + 1) * 64], "pb5"))


def emit_l0f_attn(tr, cs, sh, pb, qfm2_d, kfm2_d, ctx_d):
    pbig = _ap(pb[0]).tensor
    kslab = (sh.WA[:, 0:4096], "WA")
    qslab = (sh.WB[:, 0:4096], "WB")
    for hp in range(8):
        tr.dma(kslab, kfm2_d[hp * 128:(hp + 1) * 128, :])
        tr.dma(qslab, qfm2_d[hp * 128:(hp + 1) * 128, :])
        for h2 in range(2):
            base = 64 * h2
            rows = slice(base, base + 64)
            for which, slab in ((0, qslab), (1, kslab)):
                for j in range(NTB):
                    sq = (sh.xb[j][rows, :], f"sh_xb{j}")
                    tr.square(sq, (slab[0][rows, j * 512:(j + 1) * 512], slab[1]))
                    tr.mm((pb[6][0:1, :], "pb6"), cs.ones_col_f32[rows, 0:1], sq)
                    tr.reduce((sh.stat[0:1, which * 8 + j:which * 8 + j + 1], "ln_mean"),
                              (pb[6][0:1, :], "pb6"), ALU.max)
            tr.reduce((sh.stat[0:1, 16:17], "ln_mean"), (sh.stat[0:1, 0:8], "ln_mean"), ALU.max)
            tr.reduce((sh.stat[0:1, 17:18], "ln_mean"), (sh.stat[0:1, 8:16], "ln_mean"), ALU.max)
            tr.tt((sh.stat[0:1, 18:19], "ln_mean"), (sh.stat[0:1, 16:17], "ln_mean"),
                  (sh.stat[0:1, 17:18], "ln_mean"), ALU.mult)
            tr.act((sh.stat[0:1, 18:19], "ln_mean"), (sh.stat[0:1, 18:19], "ln_mean"), AF.Ln)
            tr.act((sh.stat[0:1, 18:19], "ln_mean"), (sh.stat[0:1, 18:19], "ln_mean"),
                   AF.Exp, scale=0.5)
            tr.ts((sh.stat[0:1, 18:19], "ln_mean"), (sh.stat[0:1, 18:19], "ln_mean"),
                  -0.125, None, ALU.mult)
            tr.mm((pb[7][:, 0:1], "pb7"), cs.ones_row_f32[:],
                  (sh.stat[0:1, 18:19], "ln_mean"))
            negb = (sh.col[:, 4:5], "sh_col")
            tr.s_copy(negb, (pb[7][:, 0:1], "pb7"))
            h = hp * 2 + h2
            for qb in range(NTB):
                qcs = slice(qb * 512, (qb + 1) * 512)
                cp = (pb[4][0:65, :], "pb4")
                for half in range(2):
                    for kp in range(8):
                        b0 = (kp % 2) * 2
                        for j in range(2):
                            kt = half * 16 + kp * 2 + j
                            tr.mm(pbig[:, (b0 + j) * 512:(b0 + j + 1) * 512],
                                  (kslab[0][rows, kt * 128:(kt + 1) * 128], kslab[1]),
                                  (qslab[0][rows, qcs], qslab[1]))
                        kt0 = half * 16 + kp * 2
                        ptp = ((sh.P0 if half == 0 else sh.P1)
                               [:, (kt0 % 16) * 512:((kt0 % 16) + 2) * 512],
                               "P0" if half == 0 else "P1")
                        tr.act(ptp, pbig[:, b0 * 512:(b0 + 2) * 512], AF.Exp,
                               bias=negb, scale=0.125)
                    for kt in range(half * 16, half * 16 + 16):
                        vt = vaug_tile(sh, kt)
                        tr.mm(cp, (vt[0][:, h * 65:(h + 1) * 65], vt[1]),
                              pt_tile(sh, kt), start=(kt == 0), stop=(kt == 31))
                tr.recip((sh.row[0:1, 0:512], "sh_row"), (pb[4][64:65, :], "pb4"))
                tr.mm((pb[5][0:64, :], "pb5"), cs.ones_row_f32[0:1, 0:64],
                      (sh.row[0:1, 0:512], "sh_row"))
                tr.s_copy((sh.g[0][0:64, :], "sh_g0"), (pb[5][0:64, :], "pb5"))
                ctx = sh.xb[4 * h2 + qb // 2][:].bitcast(BF16)
                tr.tt((ctx[0:64, (qb % 2) * 512:(qb % 2 + 1) * 512], f"sh_xb{4 * h2 + qb // 2}"),
                      (pb[4][0:64, :], "pb4"), (sh.g[0][0:64, :], "sh_g0"), ALU.mult)
        for h2 in range(2):
            for j in range(4):
                ctx = sh.xb[4 * h2 + j][:].bitcast(BF16)
                tr.dma(ctx_d[hp * 128 + h2 * 64:hp * 128 + h2 * 64 + 64,
                             j * 1024:(j + 1) * 1024],
                       (ctx[0:64, :], f"sh_xb{4 * h2 + j}"), queue="gpsimd")


def emit_l0f_router(tr, cs, sh, pb, w, out1b_d, g_d):
    nc = tr.nc
    wr = nc.alloc_sbuf_tensor("wr_bf", [128, 64], BF16)
    for c in range(NCH):
        tr.dma(wr[:, c * 8:(c + 1) * 8], w["L0_ffn_Wr"][c * 128:(c + 1) * 128, :],
               queue="gpsimd")
    big8 = nc.alloc_sbuf_tensor("big8", [128, 8], F32)
    tr.memset(big8, 1e4)
    tr.tt(big8[:], big8[:], cs.iota8[:], ALU.add)
    for tb in range(NTB):
        cols = slice(tb * 512, (tb + 1) * 512)
        for c in range(NCH):
            tr.dma(sh.zb[c][:], out1b_d[c * 128:(c + 1) * 128, cols])
        for tch in range(4):
            ti = tb * 4 + tch
            tcs = slice(tch * 128, (tch + 1) * 128)
            lp = (pb[tch % 2][:, 0:8], f"pb{tch % 2}")
            for c in range(NCH):
                tr.mm(lp, sh.zb[c][:, tcs], wr[:, c * 8:(c + 1) * 8],
                      start=(c == 0), stop=(c == 7))
            lg = (sh.moeA[:, 0:8], "moeA")
            tr.s_copy(lg, lp)
            nm = (sh.moeT[:, 0:1], "moeT")
            tr.reduce(nm, lg, ALU.max, negate=True)
            ex = (sh.moeT[:, 8:16], "moeT")
            tr.act(ex, lg, AF.Exp, bias=nm)
            sm = (sh.moeT[:, 1:2], "moeT")
            tr.reduce(sm, ex, ALU.add)
            rs = (sh.moeT[:, 2:3], "moeT")
            tr.recip(rs, sm)
            pr = (sh.moeB[:, 0:8], "moeB")
            tr.ts(pr, ex, rs, None, ALU.mult)
            m1 = (sh.moeT[:, 3:4], "moeT")
            tr.reduce(m1, pr, ALU.max)
            t0 = (sh.moeT[:, 16:24], "moeT")
            tr.ts(t0, pr, m1, None, ALU.is_ge)
            t1 = (sh.moeT[:, 24:32], "moeT")
            tr.stt(t1, t0, -1e4, big8[:], ALU.mult, ALU.add)
            i1 = (sh.moeT[:, 4:5], "moeT")
            tr.reduce(i1, t1, ALU.min)
            oh1 = (sh.moeT[:, 32:40], "moeT")
            tr.ts(oh1, cs.iota8[:], i1, None, ALU.is_equal)
            p2m = (sh.moeT[:, 40:48], "moeT")
            tr.stt(p2m, oh1, -1e30, pr, ALU.mult, ALU.add)
            m2 = (sh.moeT[:, 5:6], "moeT")
            tr.reduce(m2, p2m, ALU.max)
            tr.ts(t0, p2m, m2, None, ALU.is_ge)
            tr.stt(t1, t0, -1e4, big8[:], ALU.mult, ALU.add)
            i2 = (sh.moeT[:, 6:7], "moeT")
            tr.reduce(i2, t1, ALU.min)
            oh2 = (sh.moeT[:, 48:56], "moeT")
            tr.ts(oh2, cs.iota8[:], i2, None, ALU.is_equal)
            e1 = (sh.moeT[:, 7:8], "moeT")
            tr.act(e1, m1, AF.Exp)
            e2 = (sh.moeT[:, 56:57], "moeT")
            tr.act(e2, m2, AF.Exp)
            den = (sh.moeT[:, 57:58], "moeT")
            tr.tt(den, e1, e2, ALU.add)
            rden = (sh.moeT[:, 58:59], "moeT")
            tr.recip(rden, den)
            w1 = (sh.moeT[:, 59:60], "moeT")
            tr.tt(w1, e1, rden, ALU.mult)
            w2 = (sh.moeT[:, 60:61], "moeT")
            tr.tt(w2, e2, rden, ALU.mult)
            Gc = (sh.moeA[:, 8:16], "moeA")
            tr.ts(Gc, oh1, w1, None, ALU.mult)
            Gc2 = (sh.moeA[:, 16:24], "moeA")
            tr.ts(Gc2, oh2, w2, None, ALU.mult)
            tr.tt(Gc, Gc, Gc2, ALU.add)
            tr.transpose((pb[7][0:8, 0:128], "pb7"), Gc, cs.ident128[:])
            tr.s_copy((sh.g8[0:8, :], "sh_g8"), (pb[7][0:8, 0:128], "pb7"))
            tr.dma(g_d[:, ti * 128:(ti + 1) * 128], (sh.g8[0:8, :], "sh_g8"),
                   queue="scalar")


def emit_l0f_moe(tr, cs, sh, pb, w, wb, out1b_d, g_d, acc_d):
    for e in range(8):
        be1c = load_bias_cols(tr, cs, w["L0_ffn_be1"][e, :], 4096, pb[6], sh, 0)
        be2c = load_bias_cols(tr, cs, w["L0_ffn_be2"][e, :], C, pb[7], sh, 32)
        for tb in range(NTB):
            cols = slice(tb * 512, (tb + 1) * 512)
            for c in range(NCH):
                tr.dma(sh.zb[c][:], out1b_d[c * 128:(c + 1) * 128, cols])
            gate = sh.g[2]
            tr.dma(gate[:], g_d[e:e + 1, cols].broadcast_to([128, 512]))
            for fg in range(8):
                hh = (fg % 2) * 512
                for c in range(NCH):
                    tr.dma(sh.WA[:, c * 1024 + hh:c * 1024 + hh + 512],
                           wb["L0_ffn_We1"][e * 1024 + c * 128:e * 1024 + (c + 1) * 128,
                                            fg * 512:(fg + 1) * 512],
                           queue="scalar")
                for fi in range(4):
                    f = fg * 4 + fi
                    hp_ = (pb[f % 2][:], f"pb{f % 2}")
                    for c in range(NCH):
                        tr.mm(hp_, sh.WA[:, c * 1024 + hh + fi * 128:
                                         c * 1024 + hh + (fi + 1) * 128],
                              sh.zb[c][:], start=(c == 0), stop=(c == 7))
                    tr.act(pt_tile(sh, f), hp_, AF.Gelu, bias=be1c[:, f:f + 1])
            for f in range(32):
                tr.dma((sh.WB[:, (f % 8) * 1024:(f % 8 + 1) * 1024], "WB"),
                       wb["L0_ffn_We2"][e * 4096 + f * 128:e * 4096 + (f + 1) * 128, :],
                       queue="scalar")
                for dc in range(NCH):
                    tr.mm((pb[dc][:], f"pb{dc}"),
                          (sh.WB[:, (f % 8) * 1024 + dc * 128:(f % 8) * 1024 + (dc + 1) * 128],
                           "WB"), pt_tile(sh, f), start=(f == 0), stop=(f == 31))
            for dc in range(NCH):
                g0 = sh.g[dc % 2]
                tr.stt(g0[:], (pb[dc][:], f"pb{dc}"), be2c[:, dc:dc + 1], gate[:],
                       ALU.add, ALU.mult)
                tr.dma(acc_d[dc * 128:(dc + 1) * 128, cols], g0[:], queue="gpsimd",
                       accum_op=(ALU.add if e > 0 else None))


def emit_add_fm(tr, cs, sh, pb, a_d, b_d, dst_d):
    for c in range(NCH):
        for tb in range(NTB):
            cols = slice(tb * 512, (tb + 1) * 512)
            g0, g1 = sh.g[tb % 2], sh.g[2 + tb % 2]
            tr.dma(g0[:], a_d[c * 128:(c + 1) * 128, cols])
            tr.dma(g1[:], b_d[c * 128:(c + 1) * 128, cols], queue="scalar")
            tr.tt(g0[:], g0[:], g1[:], ALU.add)
            tr.dma(dst_d[c * 128:(c + 1) * 128, cols], g0[:], queue="gpsimd")


def emit_l1a_qkv_v(tr, cs, sh, pb, w, wb, zsrc):
    for c in range(NCH):
        tr.dma((sh.P0[:, c * 1024:(c + 1) * 1024], "P0"),
               wb["L1_attn_Wv"][c * 128:(c + 1) * 128, :], queue="scalar")
    tr.dma(sh.g[2][:], w["L1_attn_bv"][0:512].unsqueeze(0).broadcast_to([128, 512]))
    tr.dma(sh.g[3][:], w["L1_attn_bv"][512:1024].unsqueeze(0).broadcast_to([128, 512]))
    for tb in range(NTB):
        cols = slice(tb * 512, (tb + 1) * 512)
        for c in range(NCH):
            tr.dma(sh.zb[c][:], zsrc[c * 128:(c + 1) * 128, cols])
        for tch in range(4):
            trow = tb * 4 + tch
            tcs = slice(tch * 128, (tch + 1) * 128)
            for c in range(NCH):
                zt = sh.zb[c][:, tcs]
                tr.mm((pb[4][:], "pb4"), zt, (sh.P0[:, c * 1024:c * 1024 + 512], "P0"),
                      start=(c == 0), stop=(c == 7))
                tr.mm((pb[5][:], "pb5"), zt, (sh.P0[:, c * 1024 + 512:(c + 1) * 1024], "P0"),
                      start=(c == 0), stop=(c == 7))
            h0 = sh.g[0]
            tr.tt(h0[:], (pb[4][:], "pb4"), sh.g[2][:], ALU.add)
            h1 = sh.g[1]
            tr.tt(h1[:], (pb[5][:], "pb5"), sh.g[3][:], ALU.add)
            vt = vaug_tile(sh, trow)
            tr.memset(vt, 1.0)
            for hh in range(8):
                tr.s_copy((vt[0][:, hh * 65:hh * 65 + 64], vt[1]),
                          (h0[:, hh * 64:(hh + 1) * 64], "sh_g0"))
            for hh in range(8):
                tr.s_copy((vt[0][:, (8 + hh) * 65:(8 + hh) * 65 + 64], vt[1]),
                          (h1[:, hh * 64:(hh + 1) * 64], "sh_g1"))


def emit_l1a_stats(tr, cs, sh, pb, qfm3_d, kfm3_d, statall, negbs):
    nc = tr.nc
    kslab = (sh.WA[:, 0:4096], "WA")
    qslab = (sh.WB[:, 0:4096], "WB")
    ks16 = sh.col[:].bitcast(BF16)
    for hp in range(8):
        tr.dma(kslab, kfm3_d[hp * 128:(hp + 1) * 128, :])
        tr.dma(qslab, qfm3_d[hp * 128:(hp + 1) * 128, :])
        for h2 in range(2):
            h = 2 * hp + h2
            base = 64 * h2
            rows = slice(base, base + 64)
            for which, slab in ((0, qslab), (1, kslab)):
                for j in range(NTB):
                    sq = (sh.xb[j][rows, :], f"sh_xb{j}")
                    tr.square(sq, (slab[0][rows, j * 512:(j + 1) * 512], slab[1]))
                    tr.mm((pb[6][0:1, :], "pb6"), cs.ones_col_f32[rows, 0:1], sq)
                    tr.reduce((sh.stat[0:1, which * 8 + j:which * 8 + j + 1], "ln_mean"),
                              (pb[6][0:1, :], "pb6"), ALU.max)
            tr.reduce((sh.stat[0:1, 16:17], "ln_mean"), (sh.stat[0:1, 0:8], "ln_mean"), ALU.max)
            tr.reduce((sh.stat[0:1, 17:18], "ln_mean"), (sh.stat[0:1, 8:16], "ln_mean"), ALU.max)
            tr.tt((sh.stat[0:1, 18:19], "ln_mean"), (sh.stat[0:1, 16:17], "ln_mean"),
                  (sh.stat[0:1, 17:18], "ln_mean"), ALU.mult)
            tr.act((sh.stat[0:1, 18:19], "ln_mean"), (sh.stat[0:1, 18:19], "ln_mean"), AF.Ln)
            tr.act((sh.stat[0:1, 18:19], "ln_mean"), (sh.stat[0:1, 18:19], "ln_mean"),
                   AF.Exp, scale=0.5)
            tr.s_copy((sh.stat[0:1, 32 + h:33 + h], "ln_bound"),
                      (sh.stat[0:1, 18:19], "ln_mean"))
            tr.ts((sh.stat[0:1, 18:19], "ln_mean"), (sh.stat[0:1, 18:19], "ln_mean"),
                  -0.125, None, ALU.mult)
            tr.mm((pb[7][:, 0:1], "pb7"), cs.ones_row_f32[:], (sh.stat[0:1, 18:19], "ln_mean"))
            tr.s_copy((negbs[:, h:h + 1], "sh_col"), (pb[7][:, 0:1], "pb7"))
            tr.reduce((sh.col[rows, 32:33], "sh_col"), (kslab[0][rows, :], kslab[1]),
                      ALU.add)
            tr.copy((ks16[rows, 100:101], "sh_col"), (sh.col[rows, 32:33], "sh_col"))
            pbig_s = _ap(pb[0]).tensor
            for qt in range(NTC):
                qv = (qslab[0][rows, qt * 128:(qt + 1) * 128], qslab[1])
                for half in range(2):
                    hb = (qt * 2 + half) % 2
                    for j in range(4):
                        kb = half * 4 + j
                        tr.mm(pbig_s[:, (hb * 4 + j) * 512:(hb * 4 + j + 1) * 512],
                              qv, (kslab[0][rows, kb * 512:(kb + 1) * 512], kslab[1]))
                    tr.reduce((sh.moeT[:, qt % 4 * 2 + half:qt % 4 * 2 + half + 1],
                               "moeT"),
                              pbig_s[:, hb * 2048:(hb + 1) * 2048], ALU.max)
                tr.reduce((sh.moeT[:, 8:9], "moeT"),
                          (sh.moeT[:, qt % 4 * 2:qt % 4 * 2 + 2], "moeT"), ALU.max)
                tr.mm((sh_pbq := pbig_s[0:128, 4095:4096]), qv,
                      (ks16[rows, 100:101], "sh_col")) if False else None
                tr.mm(pbig_s[:, 2048 + (qt % 2) * 512:2048 + (qt % 2) * 512 + 1].bitcast(F32) if False else pbig_s[:, ((qt + 1) % 2) * 2048:((qt + 1) % 2) * 2048 + 1],
                      qv, (ks16[rows, 100:101], "sh_col"))
                tr.ts((sh.moeT[:, 9:10], "moeT"),
                      pbig_s[:, ((qt + 1) % 2) * 2048:((qt + 1) % 2) * 2048 + 1],
                      1.0 / T, None, ALU.mult)
                tr.tt((statall[:, h * 32 + qt:h * 32 + qt + 1], "A2"),
                      (sh.moeT[:, 8:9], "moeT"), (sh.moeT[:, 9:10], "moeT"), ALU.subtract)


def emit_l1a_select(tr, cs, sh, pb, statall, mask_td):
    U = 1228.0
    lo = (sh.stat[0:1, 64:80], "ln_bis")
    hi = (sh.stat[0:1, 80:96], "ln_bis")
    mid = (sh.stat[0:1, 96:112], "ln_bis")
    cnt = (sh.stat[0:1, 112:128], "ln_bis")
    tmp = (sh.stat[0:1, 128:144], "ln_bis")
    tr.memset(lo, 0.0)
    tr.ts(hi, (sh.stat[0:1, 32:48], "ln_bound"), 2.0, 1.0, ALU.mult, ALU.add)
    msk = sh.xb[0]
    red = (sh.xb[1][:, 0:16], "sh_xb1")
    for it in range(60):
        tr.tt(mid, lo, hi, ALU.add)
        tr.ts(mid, mid, 0.5, None, ALU.mult)
        tr.mm((pb[3][:], "pb3"), cs.ones_row_f32[:],
              (_ap(mid).unsqueeze(2).broadcast_to([1, 16, 32]), "ln_bis"))
        tr.tt((msk[:], "sh_xb0"), (statall[:], "A2"), (pb[3][:], "pb3"), ALU.is_ge)
        tr.reduce(red, (msk[:].rearrange("p (h w) -> p h w", h=16), "sh_xb0"), ALU.add)
        tr.mm((pb[4][0:1, 0:16], "pb4"), cs.ones_col_f32[:], red)
        tr.ts(cnt, (pb[4][0:1, 0:16], "pb4"), U, None, ALU.is_ge)
        tr.tt(tmp, mid, lo, ALU.subtract)
        tr.tt(tmp, cnt, tmp, ALU.mult)
        tr.tt(lo, lo, tmp, ALU.add)
        tr.tt(tmp, hi, mid, ALU.subtract)
        tr.tt(tmp, cnt, tmp, ALU.mult)
        tr.tt(hi, mid, tmp, ALU.add)
    for h in range(16):
        tr.mm((pb[5][:, 0:1], "pb5"), cs.ones_row_f32[:],
              (sh.stat[0:1, 64 + h:65 + h], "ln_bis"))
        tr.s_copy((sh.col[:, 33:34], "sh_col"), (pb[5][:, 0:1], "pb5"))
        tr.ts((msk[:, h * 32:(h + 1) * 32], "sh_xb0"),
              (statall[:, h * 32:(h + 1) * 32], "A2"),
              (sh.col[:, 33:34], "sh_col"), None, ALU.is_ge)
    for cb in range(4):
        tr.transpose((pb[6][:, 0:128], "pb6"),
                     (msk[:, cb * 128:(cb + 1) * 128], "sh_xb0"), cs.ident128[:])
        tr.s_copy((sh.g[0][:, 0:128], "sh_g0"), (pb[6][:, 0:128], "pb6"))
        tr.dma(mask_td[cb * 128:(cb + 1) * 128, :], sh.g[0][:, 0:128], queue="gpsimd")


def emit_l1a_attn(tr, cs, sh, pb, qfm3_d, kfm3_d, mask_td, negbs, ctx_d):
    pbig = _ap(pb[0]).tensor
    kslab = (sh.WA[:, 0:4096], "WA")
    qslab = (sh.WB[:, 0:4096], "WB")
    for hp in range(8):
        tr.dma(kslab, kfm3_d[hp * 128:(hp + 1) * 128, :])
        tr.dma(qslab, qfm3_d[hp * 128:(hp + 1) * 128, :])
        for h2 in range(2):
            h = hp * 2 + h2
            base = 64 * h2
            rows = slice(base, base + 64)
            negb = (negbs[:, h:h + 1], "sh_col")
            for qb in range(NTB):
                qcs = slice(qb * 512, (qb + 1) * 512)
                cp = (pb[4][0:65, :], "pb4")
                for half in range(2):
                    for kp in range(8):
                        b0 = (kp % 2) * 2
                        for j in range(2):
                            kt = half * 16 + kp * 2 + j
                            tr.mm(pbig[:, (b0 + j) * 512:(b0 + j + 1) * 512],
                                  (kslab[0][rows, kt * 128:(kt + 1) * 128], kslab[1]),
                                  (qslab[0][rows, qcs], qslab[1]))
                        kt0 = half * 16 + kp * 2
                        ptp = ((sh.P0 if half == 0 else sh.P1)
                               [:, (kt0 % 16) * 512:((kt0 % 16) + 2) * 512],
                               "P0" if half == 0 else "P1")
                        tr.act(ptp, pbig[:, b0 * 512:(b0 + 2) * 512], AF.Exp,
                               bias=negb, scale=0.125)
                    for kt in range(half * 16, half * 16 + 16):
                        vt = vaug_tile(sh, kt)
                        tr.mm(cp, (vt[0][:, h * 65:(h + 1) * 65], vt[1]),
                              pt_tile(sh, kt), start=(kt == 0), stop=(kt == 31))
                tr.recip((sh.row[0:1, 0:512], "sh_row"), (pb[4][64:65, :], "pb4"))
                mr = (sh.row[0:1, 512:1024], "sh_row")
                tr.dma(mr, mask_td[h * 32 + qb * 4:h * 32 + qb * 4 + 4, :]
                       .rearrange("a b -> (a b)").unsqueeze(0))
                tr.tt((sh.row[0:1, 0:512], "sh_row"), (sh.row[0:1, 0:512], "sh_row"),
                      mr, ALU.mult)
                tr.mm((pb[5][0:64, :], "pb5"), cs.ones_row_f32[0:1, 0:64],
                      (sh.row[0:1, 0:512], "sh_row"))
                tr.s_copy((sh.g[0][0:64, :], "sh_g0"), (pb[5][0:64, :], "pb5"))
                ctx = sh.xb[4 * h2 + qb // 2][:].bitcast(BF16)
                tr.tt((ctx[0:64, (qb % 2) * 512:(qb % 2 + 1) * 512],
                       f"sh_xb{4 * h2 + qb // 2}"),
                      (pb[4][0:64, :], "pb4"), (sh.g[0][0:64, :], "sh_g0"), ALU.mult)
        for h2 in range(2):
            for j in range(4):
                ctx = sh.xb[4 * h2 + j][:].bitcast(BF16)
                tr.dma(ctx_d[hp * 128 + h2 * 64:hp * 128 + h2 * 64 + 64,
                             j * 1024:(j + 1) * 1024],
                       (ctx[0:64, :], f"sh_xb{4 * h2 + j}"), queue="gpsimd")


def emit_l1f_gate(tr, cs, sh, pb, w, zsrc, g1_d):
    nc = tr.nc
    gwT_ap = sh.A2[:].bitcast(BF16)
    gwT = lambda a, b: (gwT_ap[:, 3072 + a:3072 + b], "A2")
    gwstage = (sh.A2[0:16, 0:1024], "A2")
    tr.dma(gwstage, w["L1_ffn_gate_w"])
    for c in range(NCH):
        tr.transpose((pb[6][0:128, 0:16], "pb6"),
                     (gwstage[0][:, c * 128:(c + 1) * 128], "A2"),
                     cs.ident128[0:16, 0:16])
        tr.s_copy(gwT(c * 16, (c + 1) * 16), (pb[6][0:128, 0:16], "pb6"))
    gb = (sh.moeB[:, 148:164], "moeB")
    tr.dma(gb, w["L1_ffn_gate_b"].unsqueeze(0).broadcast_to([128, 16]))
    big16 = (sh.moeB[:, 128:144], "moeB")
    tr.memset(big16, 1e4)
    tr.tt(big16, big16, cs.iota16[:], ALU.add)
    big4 = (sh.moeB[:, 144:148], "moeB")
    tr.memset(big4, 1e4)
    tr.tt(big4, big4, cs.iota4[:], ALU.add)
    mt = lambda a, b: (sh.moeT[:, a:b], "moeT")
    ma = lambda a, b: (sh.moeA[:, a:b], "moeA")
    mb = lambda a, b: (sh.moeB[:, a:b], "moeB")
    for tb in range(NTB):
        cols = slice(tb * 512, (tb + 1) * 512)
        for c in range(NCH):
            tr.dma(sh.zb[c][:], zsrc[c * 128:(c + 1) * 128, cols])
        for tch in range(4):
            ti = tb * 4 + tch
            tcs = slice(tch * 128, (tch + 1) * 128)
            lp = (pb[tch % 2][:, 0:16], f"pb{tch % 2}")
            for c in range(NCH):
                tr.mm(lp, sh.zb[c][:, tcs], gwT(c * 16, (c + 1) * 16),
                      start=(c == 0), stop=(c == 7))
            lg = ma(0, 16)
            tr.s_copy(lg, lp)
            nm = mt(0, 1)
            tr.reduce(nm, lg, ALU.max, negate=True)
            ex = ma(16, 32)
            tr.act(ex, lg, AF.Exp, bias=nm)
            sm = mt(1, 2)
            tr.reduce(sm, ex, ALU.add)
            rs = mt(2, 3)
            tr.recip(rs, sm)
            orig = mb(0, 16)
            tr.ts(orig, ex, rs, None, ALU.mult)
            scores = ma(32, 48)
            tr.tt(scores, orig, gb, ALU.add)
            for g in range(4):
                sg = ma(32 + g * 4, 36 + g * 4)
                m1g = mt(3, 4)
                tr.reduce(m1g, sg, ALU.max)
                mkg = mt(16, 20)
                tr.ts(mkg, sg, m1g, None, ALU.is_ge)
                p2 = mt(20, 24)
                tr.stt(p2, mkg, -1e30, sg, ALU.mult, ALU.add)
                m2g = mt(4, 5)
                tr.reduce(m2g, p2, ALU.max)
                tr.tt((sh.moeT[:, 8 + g:9 + g], "moeT"), m1g, m2g, ALU.add)
            gs = mt(8, 12)
            mg1 = mt(5, 6)
            tr.reduce(mg1, gs, ALU.max)
            mkg = mt(16, 20)
            tr.ts(mkg, gs, mg1, None, ALU.is_ge)
            t4 = mt(20, 24)
            tr.stt(t4, mkg, -1e4, big4, ALU.mult, ALU.add)
            ig1 = mt(6, 7)
            tr.reduce(ig1, t4, ALU.min)
            ohg1 = mt(24, 28)
            tr.ts(ohg1, cs.iota4[:], ig1, None, ALU.is_equal)
            gs2 = mt(28, 32)
            tr.stt(gs2, ohg1, -1e30, gs, ALU.mult, ALU.add)
            mg2 = mt(5, 6)
            tr.reduce(mg2, gs2, ALU.max)
            tr.ts(mkg, gs2, mg2, None, ALU.is_ge)
            tr.stt(t4, mkg, -1e4, big4, ALU.mult, ALU.add)
            ig2 = mt(7, 8)
            tr.reduce(ig2, t4, ALU.min)
            ohg2 = mt(32, 36)
            tr.ts(ohg2, cs.iota4[:], ig2, None, ALU.is_equal)
            keep = mt(36, 40)
            tr.tt(keep, ohg1, ohg2, ALU.add)
            for g in range(4):
                kadj = mt(12, 13)
                tr.ts(kadj, (sh.moeT[:, 36 + g:37 + g], "moeT"), -1.0, 1e30,
                      ALU.add, ALU.mult)
                tr.ts(ma(48 + g * 4, 52 + g * 4), ma(32 + g * 4, 36 + g * 4),
                      kadj, None, ALU.add)
            sp16 = ma(48, 64)
            m1 = mt(13, 14)
            tr.reduce(m1, sp16, ALU.max)
            mk16 = mb(16, 32)
            tr.ts(mk16, sp16, m1, None, ALU.is_ge)
            t16_ = mb(32, 48)
            tr.stt(t16_, mk16, -1e4, big16, ALU.mult, ALU.add)
            i1 = mt(14, 15)
            tr.reduce(i1, t16_, ALU.min)
            oh1 = mb(48, 64)
            tr.ts(oh1, cs.iota16[:], i1, None, ALU.is_equal)
            sp2 = mb(64, 80)
            tr.stt(sp2, oh1, -1e30, sp16, ALU.mult, ALU.add)
            m2 = mt(13, 14)
            tr.reduce(m2, sp2, ALU.max)
            tr.ts(mk16, sp2, m2, None, ALU.is_ge)
            tr.stt(t16_, mk16, -1e4, big16, ALU.mult, ALU.add)
            i2 = mt(15, 16)
            tr.reduce(i2, t16_, ALU.min)
            oh2 = mb(80, 96)
            tr.ts(oh2, cs.iota16[:], i2, None, ALU.is_equal)
            wtmp = mb(96, 112)
            tr.tt(wtmp, orig, oh1, ALU.mult)
            w1 = mt(13, 14)
            tr.reduce(w1, wtmp, ALU.add)
            tr.tt(wtmp, orig, oh2, ALU.mult)
            w2 = mt(14, 15)
            tr.reduce(w2, wtmp, ALU.add)
            Gc = mb(112, 128)
            tr.ts(Gc, oh1, w1, None, ALU.mult)
            Gc2 = mb(96, 112)
            tr.ts(Gc2, oh2, w2, None, ALU.mult)
            tr.tt(Gc, Gc, Gc2, ALU.add)
            tr.transpose((pb[7][0:16, 0:128], "pb7"), Gc, cs.ident128[:])
            tr.s_copy((sh.g8[:], "sh_g8"), (pb[7][0:16, 0:128], "pb7"))
            tr.dma(g1_d[:, ti * 128:(ti + 1) * 128], (sh.g8[:], "sh_g8"),
                   queue="scalar")


def emit_l1f_experts(tr, cs, sh, pb, w, wb, zsrc, g1_d, yacc_d):
    for e in range(16):
        be1c = load_bias_cols(tr, cs, w["L1_ffn_be1"][e, :], 512, pb[6], sh, 0)
        be3c = load_bias_cols(tr, cs, w["L1_ffn_be3"][e, :], 512, pb[6], sh, 4)
        be2c = load_bias_cols(tr, cs, w["L1_ffn_be2"][e, :], C, pb[7], sh, 8)
        for c in range(NCH):
            tr.dma((sh.WA[:, c * 512:(c + 1) * 512], "WA"),
                   wb["L1_ffn_We1"][e * 1024 + c * 128:e * 1024 + (c + 1) * 128, :],
                   queue="scalar")
            tr.dma((sh.WA[:, 4096 + c * 512:4096 + (c + 1) * 512], "WA"),
                   wb["L1_ffn_We3"][e * 1024 + c * 128:e * 1024 + (c + 1) * 128, :],
                   queue="scalar")
        for ic in range(4):
            tr.dma((sh.WB[:, ic * 1024:(ic + 1) * 1024], "WB"),
                   wb["L1_ffn_We2"][e * 512 + ic * 128:e * 512 + (ic + 1) * 128, :],
                   queue="scalar")
        for tb in range(NTB):
            cols = slice(tb * 512, (tb + 1) * 512)
            for c in range(NCH):
                tr.dma(sh.zb[c][:], zsrc[c * 128:(c + 1) * 128, cols])
            gate = sh.g[2]
            tr.dma(gate[:], g1_d[e:e + 1, cols].broadcast_to([128, 512]))
            for ic in range(4):
                h1p = (pb[0][:], "pb0")
                h3p = (pb[1][:], "pb1")
                for c in range(NCH):
                    tr.mm(h1p, (sh.WA[:, c * 512 + ic * 128:c * 512 + (ic + 1) * 128], "WA"),
                          sh.zb[c][:], start=(c == 0), stop=(c == 7))
                for c in range(NCH):
                    tr.mm(h3p, (sh.WA[:, 4096 + c * 512 + ic * 128:4096 + c * 512 + (ic + 1) * 128],
                                "WA"), sh.zb[c][:], start=(c == 0), stop=(c == 7))
                tr.act(sh.b16[0][:], h1p, AF.Silu, bias=be1c[:, ic:ic + 1])
                tr.act(sh.b16[1][:], h3p, AF.Identity, bias=be3c[:, ic:ic + 1])
                tr.tt(pt_tile(sh, ic), sh.b16[0][:], sh.b16[1][:], ALU.mult)
            for dc in range(NCH):
                op_ = (pb[2 + dc % 2][:], f"pb{2 + dc % 2}")
                for ic in range(4):
                    tr.mm(op_, (sh.WB[:, ic * 1024 + dc * 128:ic * 1024 + (dc + 1) * 128], "WB"),
                          pt_tile(sh, ic), start=(ic == 0), stop=(ic == 3))
                g0 = sh.g[dc % 2]
                tr.stt(g0[:], op_, be2c[:, dc:dc + 1], gate[:], ALU.add, ALU.mult)
                tr.dma(yacc_d[dc * 128:(dc + 1) * 128, cols], g0[:], queue="gpsimd",
                       accum_op=(ALU.add if e > 0 else None))


def emit_l1f_shared_ab(tr, cs, sh, pb, w, wb, zsrc, ab_d):
    for c in range(NCH):
        tr.dma((sh.WA[:, c * 1024:(c + 1) * 1024], "WA"),
               wb["L1_ffn_ws1"][c * 128:(c + 1) * 128, :], queue="scalar")
        tr.dma((sh.WB[:, c * 1024:(c + 1) * 1024], "WB"),
               wb["L1_ffn_ws3"][c * 128:(c + 1) * 128, :], queue="scalar")
    bs1c = load_bias_cols(tr, cs, w["L1_ffn_bs1"], C, pb[6], sh, 0)
    bs3c = load_bias_cols(tr, cs, w["L1_ffn_bs3"], C, pb[7], sh, 8)
    for tb in range(NTB):
        cols = slice(tb * 512, (tb + 1) * 512)
        for c in range(NCH):
            tr.dma(sh.zb[c][:], zsrc[c * 128:(c + 1) * 128, cols])
        for dc in range(NCH):
            ap_ = (pb[dc % 2][:], f"pb{dc % 2}")
            bp_ = (pb[2 + dc % 2][:], f"pb{2 + dc % 2}")
            for c in range(NCH):
                tr.mm(ap_, (sh.WA[:, c * 1024 + dc * 128:c * 1024 + (dc + 1) * 128], "WA"),
                      sh.zb[c][:], start=(c == 0), stop=(c == 7))
            for c in range(NCH):
                tr.mm(bp_, (sh.WB[:, c * 1024 + dc * 128:c * 1024 + (dc + 1) * 128], "WB"),
                      sh.zb[c][:], start=(c == 0), stop=(c == 7))
            tr.act(sh.b16[0][:], ap_, AF.Silu, bias=bs1c[:, dc:dc + 1])
            tr.act(sh.b16[1][:], bp_, AF.Identity, bias=bs3c[:, dc:dc + 1])
            tr.tt(sh.b16[2][:], sh.b16[0][:], sh.b16[1][:], ALU.mult)
            tr.dma(ab_d[dc * 128:(dc + 1) * 128, cols], sh.b16[2][:])


def emit_l1f_final(tr, cs, sh, pb, w, wb, ab_d, yacc_d, xfm3, xfm4):
    for c in range(NCH):
        tr.dma((sh.WA[:, c * 1024:(c + 1) * 1024], "WA"),
               wb["L1_ffn_ws2"][c * 128:(c + 1) * 128, :], queue="scalar")
    bs2c = load_bias_cols(tr, cs, w["L1_ffn_bs2"], C, pb[6], sh, 0)
    for tb in range(NTB):
        cols = slice(tb * 512, (tb + 1) * 512)
        for c in range(NCH):
            tr.dma(sh.zb[c][:], ab_d[c * 128:(c + 1) * 128, cols])
        for dc in range(NCH):
            op_ = (pb[dc % 2][:], f"pb{dc % 2}")
            for c in range(NCH):
                tr.mm(op_, (sh.WA[:, c * 1024 + dc * 128:c * 1024 + (dc + 1) * 128], "WA"),
                      sh.zb[c][:], start=(c == 0), stop=(c == 7))
            g0 = sh.g[dc % 2]
            tr.act(g0[:], op_, AF.Identity, bias=bs2c[:, dc:dc + 1])
            g1 = sh.g[2 + dc % 2]
            tr.dma(g1[:], yacc_d[dc * 128:(dc + 1) * 128, cols])
            tr.tt(g0[:], g0[:], g1[:], ALU.add)
            tr.dma(g1[:], xfm3[dc * 128:(dc + 1) * 128, cols], queue="scalar")
            tr.tt(g0[:], g0[:], g1[:], ALU.add)
            tr.dma(xfm4[dc * 128:(dc + 1) * 128, cols], g0[:], queue="gpsimd")


WEIGHT_SPECS = {
    "L0_attn_dn_g": [C], "L0_attn_dn_b": [C],
    "L0_attn_gates_w": [C, 3 * C], "L0_attn_gates_b": [3 * C],
    "L0_attn_lr_w": [C, 1], "L0_attn_lr_b": [1],
    "L0_attn_out_w": [C, C], "L0_attn_out_b": [C],
    "L0_attn_mn_g": [C], "L0_attn_mn_b": [C],
    "L0_attn_w1": [C, C], "L0_attn_b1": [C],
    "L0_attn_w2": [C, C], "L0_attn_b2": [C],
    "L0_ffn_Wq": [C, C], "L0_ffn_Wk": [C, C], "L0_ffn_Wv": [C, C],
    "L0_ffn_Wo": [C, C], "L0_ffn_bo": [C], "L0_ffn_Wr": [C, 8],
    "L0_ffn_We1": [8, C, 4096], "L0_ffn_be1": [8, 4096],
    "L0_ffn_We2": [8, 4096, C], "L0_ffn_be2": [8, C],
    "L0_ffn_ln1_g": [C], "L0_ffn_ln1_b": [C],
    "L0_ffn_ln2_g": [C], "L0_ffn_ln2_b": [C],
    "L1_attn_Wq": [C, C], "L1_attn_bq": [C], "L1_attn_Wk": [C, C],
    "L1_attn_bk": [C], "L1_attn_Wv": [C, C], "L1_attn_bv": [C],
    "L1_attn_Wo": [C, C], "L1_attn_bo": [C],
    "L1_ffn_gate_w": [16, C], "L1_ffn_gate_b": [16],
    "L1_ffn_We1": [16, C, 512], "L1_ffn_be1": [16, 512],
    "L1_ffn_We2": [16, 512, C], "L1_ffn_be2": [16, C],
    "L1_ffn_We3": [16, C, 512], "L1_ffn_be3": [16, 512],
    "L1_ffn_ws1": [C, C], "L1_ffn_bs1": [C],
    "L1_ffn_ws2": [C, C], "L1_ffn_bs2": [C],
    "L1_ffn_ws3": [C, C], "L1_ffn_bs3": [C],
}

N_STAGES = 1


def build_program(taps=(), stages=99):
    nc = bass.Bass("TRN2", target_bir_lowering=False)
    tr = TR(nc)
    pbig = nc.alloc_psum_tensor("pbig", [128, 4096], F32)
    pb = [pbig[:, i * 512:(i + 1) * 512] for i in range(8)]
    cs = Consts(tr)
    sh = SH(tr)

    class LazyW(dict):
        def __missing__(self, name):
            ap = nc.dram_tensor(name, list(WEIGHT_SPECS[name]), F32,
                                kind="ExternalInput").ap()
            self[name] = ap
            return ap

    w = LazyW()
    w["x"] = nc.dram_tensor("x", [T, C], F32, kind="ExternalInput").ap()
    wb = {}
    for name, (rows, cols) in [(n, (s[0], s[1])) for n, s in PRECAST]:
        full = WEIGHT_SPECS[name]
        wb[name] = nc.dram_tensor("b16_" + name, [rows, cols], BF16,
                                  kind="Internal").ap()

    out = nc.dram_tensor("out", [T, C], F32, kind="ExternalOutput").ap()

    scratch = {}

    def dscr(name, shape, dtype=F32):
        ap = nc.dram_tensor(name, list(shape), dtype, kind="Internal").ap()
        scratch[name] = ap
        return ap

    xfm0 = dscr("xfm0", [C, T])
    xln = dscr("xln", [C, T])
    zbf = dscr("zbf", [C, T], BF16)
    qfm_d = dscr("qfm_d", [C, T], BF16)
    kfm_d = dscr("kfm_d", [C, T], BF16)
    ktm_d = dscr("ktm_d", [T, C], BF16)
    vtm_d = dscr("vtm_d", [T, C], BF16)
    x1a_d = dscr("x1a_d", [C, T])
    z2bf = dscr("z2bf", [C, T], BF16)
    xfm1 = dscr("xfm1", [C, T])

    emit_precast(tr, sh, w, wb)
    emit_entry_transpose(tr, cs, sh, pb, w["x"], xfm0)
    final = xfm0
    if stages >= 2:
        emit_layernorm(tr, cs, sh, pb, xfm0, [(xln, F32)], 1e-5)
        emit_layernorm(tr, cs, sh, pb, xln, [(zbf, BF16)], 1e-5,
                       gamma=w["L0_attn_dn_g"], beta=w["L0_attn_dn_b"])
        final = xln
    if stages >= 3:
        emit_l0a_gates(tr, cs, sh, pb, w, wb, zbf, qfm_d, kfm_d, ktm_d, vtm_d)
    if stages >= 4:
        emit_l0a_recurrence(tr, cs, sh, pb, kfm_d, ktm_d, vtm_d,
                            rpart=globals().get("_rpart", 9))
    if stages >= 5:
        emit_l0a_output(tr, cs, sh, pb, w, wb, qfm_d, xln, x1a_d)
        final = x1a_d
    if stages >= 6:
        emit_layernorm(tr, cs, sh, pb, x1a_d, [(z2bf, BF16)], 1e-5,
                       gamma=w["L0_attn_mn_g"], beta=w["L0_attn_mn_b"])
        emit_l0a_mlp(tr, cs, sh, pb, w, wb, z2bf, x1a_d, xfm0, xfm1)
        final = xfm1

    if stages >= 7:
        x2ln = dscr("x2ln", [C, T])
        qfm2_d = dscr("qfm2_d", [C, T], BF16)
        kfm2_d = dscr("kfm2_d", [C, T], BF16)
        emit_layernorm(tr, cs, sh, pb, xfm1, [(x2ln, F32), (zbf, BF16)], 1e-5)
        emit_l0f_qkv(tr, cs, sh, pb, w, wb, zbf, qfm2_d, kfm2_d)
    if stages >= 8:
        ctx_d = dscr("ctx_d", [C, T], BF16)
        emit_l0f_attn(tr, cs, sh, pb, qfm2_d, kfm2_d, ctx_d)
    if stages >= 9:
        x2a_d = dscr("x2a_d", [C, T])
        out1f_d = dscr("out1f_d", [C, T])
        out1b_d = dscr("out1b_d", [C, T], BF16)
        emit_fm_proj(tr, cs, sh, pb, wb["L0_ffn_Wo"], w["L0_ffn_bo"], ctx_d,
                     x2a_d, res_fm=x2ln)
        emit_layernorm(tr, cs, sh, pb, x2a_d, [(out1f_d, F32), (out1b_d, BF16)],
                       1e-6, gamma=w["L0_ffn_ln1_g"], beta=w["L0_ffn_ln1_b"])
        final = out1f_d
    if stages >= 10:
        g_d = dscr("g_d", [8, T])
        emit_l0f_router(tr, cs, sh, pb, w, out1b_d, g_d)
    if stages >= 11:
        acc_d = dscr("acc_d", [C, T])
        emit_l0f_moe(tr, cs, sh, pb, w, wb, out1b_d, g_d, acc_d)
        final = acc_d
    if stages >= 12:
        x2c_d = dscr("x2c_d", [C, T])
        xfm2 = dscr("xfm2", [C, T])
        emit_add_fm(tr, cs, sh, pb, out1f_d, acc_d, x2c_d)
        emit_layernorm(tr, cs, sh, pb, x2c_d, [], 1e-6,
                       gamma=w["L0_ffn_ln2_g"], beta=w["L0_ffn_ln2_b"],
                       res_fm=xfm1, res_dst=xfm2)
        final = xfm2

    if stages >= 13:
        nbs = sh.col[:, 34:50]
        statall = sh.A2[:, 1040:1552]
        qfm3_d = dscr("qfm3_d", [C, T], BF16)
        kfm3_d = dscr("kfm3_d", [C, T], BF16)
        emit_layernorm(tr, cs, sh, pb, xfm2, [(zbf, BF16)], 1e-5)
        emit_fm_proj(tr, cs, sh, pb, wb["L1_attn_Wq"], w["L1_attn_bq"], zbf,
                     qfm3_d, dst_dtype=BF16)
        emit_fm_proj(tr, cs, sh, pb, wb["L1_attn_Wk"], w["L1_attn_bk"], zbf,
                     kfm3_d, dst_dtype=BF16, wslab=(sh.WB, "WB"), bslot=8)
        emit_l1a_qkv_v(tr, cs, sh, pb, w, wb, zbf)
        final = qfm3_d
    if stages >= 14:
        mask_td = dscr("mask_td", [512, 128])
        emit_l1a_stats(tr, cs, sh, pb, qfm3_d, kfm3_d, statall, nbs)
        emit_l1a_select(tr, cs, sh, pb, statall, mask_td)
    if stages >= 15:
        ctx3_d = dscr("ctx3_d", [C, T], BF16)
        xfm3 = dscr("xfm3", [C, T])
        emit_l1a_attn(tr, cs, sh, pb, qfm3_d, kfm3_d, mask_td, nbs, ctx3_d)
        emit_fm_proj(tr, cs, sh, pb, wb["L1_attn_Wo"], w["L1_attn_bo"], ctx3_d,
                     xfm3, res_fm=xfm2)
        final = xfm3
    if stages >= 16:
        g1_d = dscr("g1_d", [16, T])
        yacc_d = dscr("yacc_d", [C, T])
        emit_layernorm(tr, cs, sh, pb, xfm3, [(zbf, BF16)], 1e-5)
        emit_l1f_gate(tr, cs, sh, pb, w, zbf, g1_d)
    if stages >= 17:
        emit_l1f_experts(tr, cs, sh, pb, w, wb, zbf, g1_d, yacc_d)
        final = yacc_d
    if stages >= 18:
        ab_d = dscr("ab_d", [C, T], BF16)
        xfm4 = dscr("xfm4", [C, T])
        emit_l1f_shared_ab(tr, cs, sh, pb, w, wb, zbf, ab_d)
        emit_l1f_final(tr, cs, sh, pb, w, wb, ab_d, yacc_d, xfm3, xfm4)
        final = xfm4

    emit_exit_transpose(tr, cs, sh, pb, final, out)

    if taps:
        for tname in taps:
            ap = scratch[tname]
            tout = nc.dram_tensor("tap_" + tname, list(ap.shape), ap.dtype,
                                  kind="ExternalOutput").ap()
            tr.dma(tout, ap)

    tr.finalize_and_emit()
    used = [k for k in w if k != "x"]
    return nc, used


def kernel(**inputs):
    x = np.ascontiguousarray(inputs["L0_attn_x"]).astype(np.float32, copy=False)
    g = globals()
    nc, used = build_program(taps=g.get("_DEBUG_TAPS", ()), stages=g.get("_stages", 99))
    core_ids = g.get("_cores", None) or list(range(8))
    in_maps = []
    for core in core_ids:
        m = {"x": x[core]}
        for name in used:
            m[name] = np.ascontiguousarray(inputs[name]).astype(np.float32, copy=False)
        in_maps.append(m)
    res = run_bass_kernel_spmd(nc, in_maps, core_ids=core_ids)
    taps = g.get("_DEBUG_TAPS", ())
    if taps:
        g["_taps"] = [{t: r["tap_" + t] for t in taps} for r in res.results]
    return np.stack([r["out"] for r in res.results], axis=0)


# version 07
# speedup vs baseline: 1.2411x; 1.1683x over previous
import numpy as np
import concourse.bass as bass
import concourse.mybir as mybir
from concourse.bass_utils import run_bass_kernel_spmd

F32 = mybir.dt.float32
BF16 = mybir.dt.bfloat16
AF = mybir.ActivationFunctionType
ALU = mybir.AluOpType
AX = mybir.AxisListType

T = 4096
C = 1024
NCH = C // 128
NTB = T // 512
NTC = T // 128

_DEBUG_TAPS = ()

COMPUTE_ENGINES = ("tensor", "scalar", "vector", "gpsimd")
DMA_QUEUES = ("sync", "gpsimd", "scalar")


_SLOTTED = ("pbig", "WA", "WB", "P0", "P1")


def _slot_keys(name, x):
    pref = "pb" if name == "pbig" else name
    row = 1
    for d in x.tensor.shape[1:]:
        row *= d
    off = x.offset % row
    width = 1
    for d in x.shape[1:]:
        width *= d
    return [f"{pref}{i}" for i in range(off // 512, (off + width - 1) // 512 + 1)]


def _keys(x):
    if isinstance(x, tuple):
        ap = x[0]
        t = getattr(ap, "tensor", None)
        if t is not None and t.name in _SLOTTED:
            return _slot_keys(t.name, ap)
        return [x[1]]
    t = getattr(x, "tensor", None)
    if t is None:
        if x.name in _SLOTTED:
            return _slot_keys(x.name, x[tuple(slice(None) for _ in x.shape)])
        return [x.name]
    if t.name in _SLOTTED:
        return _slot_keys(t.name, x)
    return [t.name]


def _ap(x):
    if isinstance(x, tuple):
        x = x[0]
    if getattr(x, "tensor", None) is None and hasattr(x, "shape"):
        return x[tuple(slice(None) for _ in x.shape)]
    return x


class Op:
    __slots__ = ("eng", "fn", "reads", "writes", "dma", "qidx", "eidx",
                 "waits", "marked", "cnt")

    def __init__(self, eng, fn, reads, writes, dma):
        self.eng = eng
        self.fn = fn
        self.reads = reads
        self.writes = writes
        self.dma = dma
        self.waits = []
        self.marked = False
        self.cnt = 0


class TR:

    def __init__(self, nc):
        self.nc = nc
        self.ops = []
        self.uid = 0

    def name(self, base):
        self.uid += 1
        return f"{base}_{self.uid}"

    def arena(self, name, shape, dtype):
        if not hasattr(self, "_arenas"):
            self._arenas = {}
        if name not in self._arenas:
            self._arenas[name] = self.nc.alloc_sbuf_tensor(name, shape, dtype)
        return self._arenas[name]

    def constcol(self, val):
        if not hasattr(self, "_consts"):
            self._consts = {}
        if val not in self._consts:
            t = self.nc.alloc_sbuf_tensor(self.name(f"cc_{val}"), [128, 1], F32)
            self.memset(t, float(val))
            self._consts[val] = t
        return self._consts[val]

    def _rec(self, eng, fn, reads, writes, dma=False, xreads=(), xwrites=()):
        rk = [k for r in reads for k in _keys(r)] + list(xreads)
        wk = [k for w in writes for k in _keys(w)] + list(xwrites)
        self.ops.append(Op(eng, fn, rk, wk, dma))

    def mm(self, out, lhsT, rhs, start=True, stop=True):
        o, l, r = _ap(out), _ap(lhsT), _ap(rhs)
        self._rec("tensor", lambda e: e.matmul(o, l, r, start=start, stop=stop),
                  [lhsT, rhs], [out])

    def transpose(self, out, in_, ident):
        o, i, d = _ap(out), _ap(in_), _ap(ident)
        self._rec("tensor", lambda e: e.transpose(o, i, d), [in_, ident], [out])

    def act(self, out, in_, func, bias=0.0, scale=1.0, accum_out=None,
            xreads=(), xwrites=()):
        if isinstance(bias, (int, float)) and float(bias) != 0.0:
            p = _ap(out).shape[0]
            bias = self.constcol(float(bias))[0:p, 0:1]
        o, i = _ap(out), _ap(in_)
        b = _ap(bias) if not isinstance(bias, (int, float)) else float(bias)
        s = _ap(scale) if not isinstance(scale, (int, float)) else float(scale)
        ao = _ap(accum_out) if accum_out is not None else None
        reads = [in_] + [x for x in (bias, scale) if not isinstance(x, (int, float))]
        writes = [out] + ([accum_out] if accum_out is not None else [])
        self._rec("scalar",
                  lambda e: e.activation(o, i, func, bias=b, scale=s, accum_out=ao),
                  reads, writes, xreads=xreads, xwrites=xwrites)

    def s_copy(self, out, in_):
        o, i = _ap(out), _ap(in_)
        self._rec("scalar", lambda e: e.copy(o, i), [in_], [out])

    def square(self, out, in_):
        o, i = _ap(out), _ap(in_)
        self._rec("scalar", lambda e: e.square(o, i), [in_], [out])

    def tt(self, out, in0, in1, op, eng="vector"):
        o, a, b = _ap(out), _ap(in0), _ap(in1)
        self._rec(eng, lambda e: e.tensor_tensor(o, a, b, op), [in0, in1], [out])

    def ts(self, out, in0, s1, s2, op0, op1=ALU.bypass, eng="vector"):
        o, a = _ap(out), _ap(in0)
        v1 = _ap(s1) if not isinstance(s1, (int, float)) else float(s1)
        v2 = (_ap(s2) if not isinstance(s2, (int, float)) else
              (float(s2) if s2 is not None else None))
        reads = [in0] + [x for x in (s1, s2) if x is not None and not isinstance(x, (int, float))]
        self._rec(eng, lambda e: e.tensor_scalar(o, a, v1, v2, op0, op1), reads, [out])

    def stt(self, out, in0, scalar, in1, op0, op1, eng="vector"):
        o, a, b = _ap(out), _ap(in0), _ap(in1)
        s = _ap(scalar) if not isinstance(scalar, (int, float)) else float(scalar)
        reads = [in0, in1] + ([scalar] if not isinstance(scalar, (int, float)) else [])
        self._rec(eng, lambda e: e.scalar_tensor_tensor(o, a, s, b, op0, op1),
                  reads, [out])

    def reduce(self, out, in_, op, negate=False, eng="vector", xreads=()):
        o, i = _ap(out), _ap(in_)
        self._rec(eng, lambda e: e.tensor_reduce(o, i, AX.X, op, negate=negate),
                  [in_], [out], xreads=xreads)

    def recip(self, out, in_):
        o, i = _ap(out), _ap(in_)
        self._rec("vector", lambda e: e.reciprocal(o, i), [in_], [out])

    def select(self, out, mask, on_true, on_false):
        o, m, a, b = _ap(out), _ap(mask), _ap(on_true), _ap(on_false)
        self._rec("vector", lambda e: e.select(o, m, a, b), [mask, on_true, on_false], [out])

    def copy(self, out, in_, eng="vector"):
        o, i = _ap(out), _ap(in_)
        self._rec(eng, lambda e: e.tensor_copy(o, i), [in_], [out])

    def memset(self, out, val, eng="gpsimd"):
        o = _ap(out)
        self._rec(eng, lambda e: e.memset(o, val), [], [out])

    def iota(self, out, pattern, base=0, chmul=0):
        o = _ap(out)
        self._rec("gpsimd",
                  lambda e: e.iota(o, pattern=pattern, base=base, channel_multiplier=chmul,
                                   allow_small_or_imprecise_dtypes=True), [], [out])

    def affine(self, out, in_, compare_op, fill, pattern, chmul, base=0):
        o, i = _ap(out), _ap(in_)
        self._rec("gpsimd",
                  lambda e: e.affine_select(out=o, in_=i, compare_op=compare_op,
                                            fill=fill, base=base, pattern=pattern,
                                            channel_multiplier=chmul), [in_], [out])

    def dma(self, out, in_, queue="sync", accum_op=None, slow_ok=False):
        o, i = _ap(out), _ap(in_)
        kw = {}
        if accum_op is not None:
            kw["accum_op"] = accum_op
        if slow_ok:
            kw["allow_slow_non_contiguous"] = True
        reads = [in_] + ([out] if accum_op is not None else [])
        self._rec(queue, lambda e: e.dma_start(out=o, in_=i, **kw), reads, [out],
                  dma=True)

    NLANES = 16

    def finalize_and_emit(self):
        nc = self.nc
        isem = {e: nc.alloc_semaphore(f"si_{e}") for e in COMPUTE_ENGINES}
        dsem = {q: [nc.alloc_semaphore(f"sd_{q}{j}") for j in range(self.NLANES)]
                for q in DMA_QUEUES}
        eidx = {}
        qidx = {}
        for op in self.ops:
            if op.dma:
                op.qidx = qidx.get(op.eng, 0)
                qidx[op.eng] = op.qidx + 1
            else:
                op.eidx = eidx.get(op.eng, 0)
                eidx[op.eng] = op.eidx + 1

        ordn = {}
        cnte = {}
        for n, op in enumerate(self.ops):
            ordn[id(op)] = cnte.get(op.eng, 0)
            cnte[op.eng] = ordn[id(op)] + 1
        last_w = {}
        readers = {}
        deps = [[] for _ in self.ops]
        for n, op in enumerate(self.ops):
            need = []
            for r in op.reads:
                w = last_w.get(r)
                if w is not None:
                    need.append(w)
            for w_ in op.writes:
                lw = last_w.get(w_)
                if lw is not None:
                    need.append(lw)
                need.extend(readers.get(w_, ()))
            for p in need:
                if p is op:
                    continue
                if p.dma or p.eng != op.eng or op.eng in ("gpsimd", "vector", "scalar"):
                    deps[n].append(p)
                    p.marked = True
            for r in op.reads:
                readers.setdefault(r, []).append(op)
            for w_ in op.writes:
                last_w[w_] = op
                readers[w_] = []
        ecnt = {e: 0 for e in COMPUTE_ENGINES}
        for op in self.ops:
            if op.dma:
                op.cnt = (op.qidx // self.NLANES + 1) * 16
            else:
                if op.marked:
                    ecnt[op.eng] += 1
                op.cnt = ecnt[op.eng]
        known = {e: {} for e in COMPUTE_ENGINES + ("sync",)}
        for n, op in enumerate(self.ops):
            kn = known[op.eng]
            for p in deps[n]:
                sem = (dsem[p.eng][p.qidx % self.NLANES] if p.dma
                       else isem[p.eng])
                if kn.get(sem, 0) < p.cnt:
                    kn[sem] = p.cnt
                    op.waits.append((sem, p.cnt))

        ops_by_eng = {}
        for op in self.ops:
            ops_by_eng.setdefault(op.eng, []).append(op)

        with nc.Block() as blk:
            def body(eng_name):
                def _(e):
                    for op in ops_by_eng.get(eng_name, ()):
                        if op.dma and op.qidx >= self.NLANES:
                            e.wait_ge(dsem[op.eng][op.qidx % self.NLANES],
                                      (op.qidx // self.NLANES) * 16)
                        for sem, cnt in op.waits:
                            e.wait_ge(sem, cnt)
                        instr = op.fn(e)
                        if op.dma:
                            instr.then_inc(dsem[op.eng][op.qidx % self.NLANES], 16)
                        elif op.marked:
                            instr.then_inc(isem[op.eng], 1)
                return _

            for e in ("sync", "tensor", "scalar", "vector", "gpsimd"):
                getattr(blk, e)(body(e))

            @blk.sync
            def _(sync):
                for q in DMA_QUEUES:
                    n = qidx.get(q, 0)
                    for j in range(min(n, self.NLANES)):
                        tot = (n - j + self.NLANES - 1) // self.NLANES
                        sync.wait_ge(dsem[q][j], tot * 16)


class Consts:
    def __init__(self, tr):
        nc = tr.nc
        self.ident128 = nc.alloc_sbuf_tensor("ident128", [128, 128], F32)
        self.ident64 = nc.alloc_sbuf_tensor("ident64", [128, 64], F32)
        self.mask64_low = nc.alloc_sbuf_tensor("mask64_low", [128, 64], F32)
        self.mask64_up = nc.alloc_sbuf_tensor("mask64_up", [128, 64], F32)
        self.ones_col_f32 = nc.alloc_sbuf_tensor("ones_col_f32", [128, 1], F32)
        self.ones_row_f32 = nc.alloc_sbuf_tensor("ones_row_f32", [1, 128], F32)
        self.iota16 = nc.alloc_sbuf_tensor("iota16", [128, 16], F32)
        self.iota8 = nc.alloc_sbuf_tensor("iota8", [128, 8], F32)
        self.iota4 = nc.alloc_sbuf_tensor("iota4", [128, 4], F32)
        tr.memset(self.ident128, 0.0)
        tr.affine(self.ident128, self.ident128, ALU.not_equal, 1.0, [[-1, 128]], 1)
        tr.memset(self.ident64, 0.0)
        tr.memset(self.mask64_low, 1.0)
        tr.memset(self.mask64_up, 1.0)
        for hrows in (slice(0, 64), slice(64, 128)):
            iv = (self.ident64[hrows, :], "ident64")
            lv = (self.mask64_low[hrows, :], "mask64_low")
            uv = (self.mask64_up[hrows, :], "mask64_up")
            tr.affine(iv, iv, ALU.not_equal, 1.0, [[-1, 64]], 1)
            tr.affine(lv, lv, ALU.is_gt, 0.0, [[-1, 64]], 1)
            tr.affine(uv, uv, ALU.is_gt, 0.0, [[1, 64]], -1)
        tr.memset(self.ones_col_f32, 1.0)
        tr.memset(self.ones_row_f32, 1.0)
        tr.iota(self.iota16, [[1, 16]])
        tr.iota(self.iota8, [[1, 8]])
        tr.iota(self.iota4, [[1, 4]])


class SH:

    def __init__(self, tr):
        nc = tr.nc
        self.A0 = tr.arena("A0", [128, 8192], F32)
        self.A1 = tr.arena("A1", [128, 8192], F32)
        self.A2 = tr.arena("A2", [128, 2048], F32)
        self.P0 = tr.arena("P0", [128, 8192], BF16)
        self.P1 = tr.arena("P1", [128, 8192], BF16)
        self.WA = tr.arena("WA", [128, 8192], BF16)
        self.WB = tr.arena("WB", [128, 8192], BF16)
        self.xb = [nc.alloc_sbuf_tensor(f"sh_xb{i}", [128, 512], F32) for i in range(8)]
        self.zb = [nc.alloc_sbuf_tensor(f"sh_zb{i}", [128, 512], BF16) for i in range(8)]
        self.g = [nc.alloc_sbuf_tensor(f"sh_g{i}", [128, 512], F32) for i in range(4)]
        self.b16 = [nc.alloc_sbuf_tensor(f"sh_b{i}", [128, 512], BF16) for i in range(4)]
        self.row = nc.alloc_sbuf_tensor("sh_row", [1, 1024], F32)
        self.stat = nc.alloc_sbuf_tensor("sh_stat", [1, 2048], F32)
        self.col = nc.alloc_sbuf_tensor("sh_col", [128, 64], F32)
        self.bcol = nc.alloc_sbuf_tensor("sh_bcol", [128, 48], F32)
        self.moeA = nc.alloc_sbuf_tensor("sh_moeA", [128, 256], F32)
        self.moeB = nc.alloc_sbuf_tensor("sh_moeB", [128, 256], F32)
        self.moeT = nc.alloc_sbuf_tensor("sh_moeT", [128, 64], F32)
        self.g8 = nc.alloc_sbuf_tensor("sh_g8", [16, 128], F32)
        self.bstage = nc.alloc_sbuf_tensor("sh_bstage", [32, 128], F32)
        self.lrch = nc.alloc_sbuf_tensor("sh_lrch", [128, 32], F32)
        self.kf = [nc.alloc_sbuf_tensor("sh_kf0", [128, 1024], F32)] * 2
        self.kt = [nc.alloc_sbuf_tensor("sh_kt0", [128, 1024], BF16)] * 2
        self.vt = [nc.alloc_sbuf_tensor("sh_vt0", [128, 1024], BF16)] * 2
        self.u16 = nc.alloc_sbuf_tensor("sh_u16", [128, 1024], BF16)
        self.ulr16 = nc.alloc_sbuf_tensor("sh_ulr16", [128, 1024], BF16)
        self.r16 = nc.alloc_sbuf_tensor("sh_r16", [128, 1024], BF16)
        ch = {}
        for nm in ("L", "LT", "AT", "P", "PT"):
            ch[nm] = nc.alloc_sbuf_tensor(f"ch_{nm}", [128, 128], F32)
        self.ch = ch
        self.dnt = nc.alloc_sbuf_tensor("ch_dnt", [128, 128], BF16)
        self.l10t = nc.alloc_sbuf_tensor("ch_l10t", [64, 128], BF16)


def load_bias_cols(tr, cs, dram_vec, n, psum_bank, sh, slot):
    nch = (n + 127) // 128
    out = sh.bcol[:, slot:slot + nch]
    tr.dma(sh.bstage[0:nch, :], dram_vec.rearrange("(a b) -> a b", b=128))
    tr.transpose(psum_bank[:, 0:nch], sh.bstage[0:nch, :], cs.ident128[0:nch, 0:nch])
    tr.s_copy(out, psum_bank[:, 0:nch])
    return out


PRECAST = [
    ("L0_attn_gates_w", [C, 3 * C]), ("L0_attn_out_w", [C, C]),
    ("L0_attn_w1", [C, C]), ("L0_attn_w2", [C, C]),
    ("L0_ffn_Wq", [C, C]), ("L0_ffn_Wk", [C, C]), ("L0_ffn_Wv", [C, C]),
    ("L0_ffn_Wo", [C, C]),
    ("L0_ffn_We1", [8 * C, 4096]), ("L0_ffn_We2", [8 * 4096, C]),
    ("L1_attn_Wq", [C, C]), ("L1_attn_Wk", [C, C]), ("L1_attn_Wv", [C, C]),
    ("L1_attn_Wo", [C, C]),
    ("L1_ffn_We1", [16 * C, 512]), ("L1_ffn_We2", [16 * 512, C]),
    ("L1_ffn_We3", [16 * C, 512]),
    ("L1_ffn_ws1", [C, C]), ("L1_ffn_ws2", [C, C]), ("L1_ffn_ws3", [C, C]),
]


def emit_precast(tr, sh, w, wb):
    slabs = [(sh.WA, "WA"), (sh.WB, "WB"), (sh.P0, "P0"), (sh.P1, "P1")]
    i = 0
    for name, (rows, cols) in [(n, s) for n, s in PRECAST]:
        src2d = w[name]
        if len(src2d.shape) == 3:
            src2d = src2d.rearrange("e a b -> (e a) b")
        step = min(8192, cols)
        for r in range(rows // 128):
            for c0 in range(0, cols, step):
                sl, key = slabs[i % 4]
                i += 1
                n = min(step, cols - c0)
                tr.dma((sl[:, 0:n], key), src2d[r * 128:(r + 1) * 128, c0:c0 + n],
                       queue="gpsimd")
                tr.dma(wb[name][r * 128:(r + 1) * 128, c0:c0 + n], (sl[:, 0:n], key))


def emit_entry_transpose(tr, cs, sh, pb, src_tm, dst_fm):
    for tt in range(NTC):
        tin = (sh.A0[:, (tt % 2) * 1024:(tt % 2 + 1) * 1024], "A0")
        tr.dma(tin, src_tm[tt * 128:(tt + 1) * 128, :])
        for c in range(NCH):
            i = tt * NCH + c
            ps = pb[4 + i % 4]
            o = sh.g[i % 4]
            tr.transpose(ps[:, 0:128], (tin[0][:, c * 128:(c + 1) * 128], "A0"),
                         cs.ident128[:])
            tr.s_copy(o[:, 0:128], ps[:, 0:128])
            tr.dma(dst_fm[c * 128:(c + 1) * 128, tt * 128:(tt + 1) * 128], o[:, 0:128],
                   queue="gpsimd")


def emit_exit_transpose(tr, cs, sh, pb, src_fm, dst_tm):
    tiles = [(sh.A0[:, 0:2048], "A0"), (sh.A1[:, 0:2048], "A1")]
    seq = [(c, h) for c in range(NCH) for h in range(2)]
    for i, (c, h) in enumerate(seq):
        tin = tiles[i % 2]
        tr.dma(tin, src_fm[c * 128:(c + 1) * 128, h * 2048:(h + 1) * 2048])
        for j in range(16):
            n = i * 16 + j
            tt = h * 16 + j
            ps = pb[4 + n % 4]
            o = sh.g[n % 4]
            tr.transpose(ps[:, 0:128], (tin[0][:, j * 128:(j + 1) * 128], tin[1]),
                         cs.ident128[:])
            tr.s_copy(o[:, 0:128], ps[:, 0:128])
            tr.dma(dst_tm[tt * 128:(tt + 1) * 128, c * 128:(c + 1) * 128], o[:, 0:128],
                   queue="gpsimd")


def emit_layernorm(tr, cs, sh, pb, src_fm, dsts, eps, gamma=None, beta=None,
                   res_fm=None, res_dst=None):
    xb = sh.xb
    sq = [sh.g[0], sh.g[1]]
    mean_r = (sh.stat[0:1, 0:512], "ln_mean")
    m2_r = (sh.stat[0:1, 512:1024], "ln_m2")
    a_r = (sh.stat[0:1, 1024:1536], "ln_a")
    b_r = (sh.stat[0:1, 1536:2048], "ln_b")
    yf = [sh.g[2], sh.g[3]]
    yb16 = [sh.b16[0], sh.b16[1]]
    gcol = bcol = None
    if gamma is not None:
        gcol = load_bias_cols(tr, cs, gamma, C, pb[6], sh, 0)
        bcol = load_bias_cols(tr, cs, beta, C, pb[7], sh, 8)

    sum_ps = (pb[0][0:1, :], "pb0")
    sq_ps = (pb[1][0:1, :], "pb1")
    for tb in range(NTB):
        cols = slice(tb * 512, (tb + 1) * 512)
        for c in range(NCH):
            tr.dma(xb[c][:], src_fm[c * 128:(c + 1) * 128, cols])
            tr.square(sq[c % 2][:], xb[c][:])
            tr.mm(sum_ps, cs.ones_col_f32[:], xb[c][:], start=(c == 0), stop=(c == 7))
            tr.mm(sq_ps, cs.ones_col_f32[:], sq[c % 2][:], start=(c == 0), stop=(c == 7))
        tr.ts(mean_r, sum_ps, 1.0 / C, None, ALU.mult)
        tr.tt(m2_r, mean_r, mean_r, ALU.mult)
        tr.stt(m2_r, sq_ps, 1.0 / C, m2_r, ALU.mult, ALU.subtract)
        tr.act(a_r, m2_r, AF.Ln, bias=float(eps))
        tr.act(a_r, a_r, AF.Exp, scale=-0.5)
        tr.tt(b_r, mean_r, a_r, ALU.mult)
        tr.ts(b_r, b_r, -1.0, None, ALU.mult)
        tr.mm((pb[2][:], "pb2"), cs.ones_row_f32[:], a_r)
        tr.mm((pb[3][:], "pb3"), cs.ones_row_f32[:], b_r)
        for c in range(NCH):
            y = yf[c % 2]
            tr.tt(y[:], xb[c][:], (pb[2][:], "pb2"), ALU.mult)
            tr.tt(y[:], y[:], (pb[3][:], "pb3"), ALU.add)
            if gamma is not None:
                tr.act(y[:], y[:], AF.Identity, bias=bcol[:, c:c + 1], scale=gcol[:, c:c + 1])
            wrote16 = False
            for dst, dt in dsts:
                if dt == F32:
                    tr.dma(dst[c * 128:(c + 1) * 128, cols], y[:])
                else:
                    if not wrote16:
                        tr.copy(yb16[c % 2][:], y[:], eng="gpsimd")
                        wrote16 = True
                    tr.dma(dst[c * 128:(c + 1) * 128, cols], yb16[c % 2][:],
                           queue="scalar")
            if res_fm is not None:
                rbuf = (sh.A2[:, (c % 2) * 512:(c % 2 + 1) * 512], "A2")
                tr.dma(rbuf, res_fm[c * 128:(c + 1) * 128, cols])
                tr.tt(y[:], y[:], rbuf, ALU.add)
                tr.dma(res_dst[c * 128:(c + 1) * 128, cols], y[:], queue="gpsimd")


def emit_l0a_gates(tr, cs, sh, pb, w, wb, zbf, qfm_d, kfm_d, ktm_d, vtm_d):
    gwq = sh.WA
    for c in range(NCH):
        tr.dma(gwq[:, c * 1024:(c + 1) * 1024],
               wb["L0_attn_gates_w"][c * 128:(c + 1) * 128, 0:1024], queue="scalar")
        dst = sh.P0 if c < 4 else sh.P1
        tr.dma(dst[:, (c % 4) * 2048:((c % 4) + 1) * 2048],
               wb["L0_attn_gates_w"][c * 128:(c + 1) * 128, 1024:3072], queue="scalar")
    bias_q = load_bias_cols(tr, cs, w["L0_attn_gates_b"][0:1024], C, pb[6], sh, 16)
    biasg = lambda j: (sh.A1[:, j * 512:(j + 1) * 512], "A1")
    for j in range(4):
        tr.dma(biasg(j),
               w["L0_attn_gates_b"][1024 + j * 512:1024 + (j + 1) * 512]
               .unsqueeze(0).broadcast_to([128, 512]))
    lrw_f = load_bias_cols(tr, cs, w["L0_attn_lr_w"].rearrange("a o -> (a o)"), C,
                           pb[7], sh, 24)
    nc = tr.nc
    lrw = nc.alloc_sbuf_tensor("lrw_bf", [128, 8], BF16)
    tr.copy(lrw[:], lrw_f)
    lrb = nc.alloc_sbuf_tensor("lrb_pp", [128, 1], F32)
    tr.dma(lrb[:], w["L0_attn_lr_b"].unsqueeze(0).broadcast_to([128, 1]))
    identb = nc.alloc_sbuf_tensor("ident128_bf", [128, 128], BF16)
    tr.memset(identb, 0.0)
    tr.affine(identb, identb, ALU.not_equal, 1.0, [[-1, 128]], 1)

    ktmp = (sh.A0[:, 0:1024], "A0")
    ksq = (sh.A0[:, 1024:2048], "A0")
    krstd = (sh.col[:, 0:1], "sh_col")
    psb = (pb[5][:].bitcast(BF16), "pb5")
    for tb in range(NTB):
        cols = slice(tb * 512, (tb + 1) * 512)
        for c in range(NCH):
            tr.dma(sh.zb[c][:], zbf[c * 128:(c + 1) * 128, cols])
        for dc in range(NCH):
            qp = (pb[dc % 2][:], f"pb{dc % 2}")
            for c in range(NCH):
                tr.mm(qp, gwq[:, c * 1024 + dc * 128:c * 1024 + (dc + 1) * 128],
                      sh.zb[c][:], start=(c == 0), stop=(c == 7))
            qt = sh.b16[dc % 2]
            tr.act(qt[:], qp, AF.Identity, bias=bias_q[:, dc:dc + 1])
            tr.dma(qfm_d[dc * 128:(dc + 1) * 128, cols], qt[:])
        for tch in range(4):
            tcs = slice(tch * 128, (tch + 1) * 128)
            trow = tb * 4 + tch
            for c in range(NCH):
                zt = sh.zb[c][:, tcs]
                src = sh.P0 if c < 4 else sh.P1
                for j in range(4):
                    tr.mm((pb[j][:], f"pb{j}"), zt,
                          src[:, (c % 4) * 2048 + j * 512:(c % 4) * 2048 + (j + 1) * 512],
                          start=(c == 0), stop=(c == 7))
                tr.mm((pb[4][:, 0:1], "pb4"), zt, lrw[:, c:c + 1],
                      start=(c == 0), stop=(c == 7))
            tr.act((sh.lrch[:, trow:trow + 1], "lrch"), (pb[4][:, 0:1], "pb4"),
                   AF.Identity, bias=lrb[:, 0:1])
            tr.tt((ktmp[0][:, 0:512], "A0"), (pb[0][:], "pb0"), biasg(0), ALU.add)
            tr.tt((ktmp[0][:, 512:1024], "A0"), (pb[1][:], "pb1"), biasg(1), ALU.add)
            tr.square(ksq, ktmp)
            tr.reduce(krstd, ksq, ALU.add)
            tr.act(krstd, krstd, AF.Ln)
            tr.act(krstd, krstd, AF.Exp, scale=-0.5)
            kn = (sh.A2[:].bitcast(BF16)[:, 0:1024], "A2")
            tr.ts(kn, ktmp, krstd, None, ALU.mult)
            tr.dma(ktm_d[trow * 128:(trow + 1) * 128, :], kn)
            for c2 in range(NCH):
                tr.transpose((psb[0][:, 0:128], "pb5"),
                             (kn[0][:, c2 * 128:(c2 + 1) * 128], "A2"), identb[:])
                tr.s_copy(sh.b16[2][:, 0:128], (psb[0][:, 0:128], "pb5"))
                tr.dma(kfm_d[c2 * 128:(c2 + 1) * 128, trow * 128:(trow + 1) * 128],
                       sh.b16[2][:, 0:128], queue="scalar")
            vn = (sh.A2[:].bitcast(BF16)[:, 1024:2048], "A2")
            tr.tt((vn[0][:, 0:512], "A2"), (pb[2][:], "pb2"), biasg(2), ALU.add)
            tr.tt((vn[0][:, 512:1024], "A2"), (pb[3][:], "pb3"), biasg(3), ALU.add)
            tr.dma(vtm_d[trow * 128:(trow + 1) * 128, :], vn)


def emit_l0a_recurrence(tr, cs, sh, pb, kfm_d, ktm_d, vtm_d, rpart=9):
    S = (sh.A0, "A0")
    W = (sh.A1, "A1")
    for j in range(16):
        tr.memset((sh.A0[:, j * 512:(j + 1) * 512], "A0"), 0.0)
        tr.memset((sh.A1[:, j * 512:(j + 1) * 512], "A1"), 0.0)

    for i in range(NTC):
        kf, kt, vt = sh.kf[i % 2], sh.kt[i % 2], sh.vt[i % 2]
        for c in range(NCH):
            tr.dma(kf[:, c * 128:(c + 1) * 128],
                   kfm_d[c * 128:(c + 1) * 128, i * 128:(i + 1) * 128],
                   queue="gpsimd")
        tr.dma(kt[:], ktm_d[i * 128:(i + 1) * 128, :])
        tr.dma(vt[:], vtm_d[i * 128:(i + 1) * 128, :])
        if rpart < 1:
            continue
        kfA = [kf[:, c * 128:c * 128 + 64] for c in range(NCH)]
        kfB = [kf[:, c * 128 + 64:(c + 1) * 128] for c in range(NCH)]
        par = i % 2
        pc = slice(par * 64, par * 64 + 64)
        KA = (pb[4][0:64, 0:64], "pb4")
        KB = (pb[5][64:128, 0:64], "pb5")
        L10T = (pb[6][0:64, 0:64], "pb6")
        for c in range(NCH):
            tr.mm(KA, kfA[c], kfA[c], start=(c == 0), stop=(c == 7))
        for c in range(NCH):
            tr.mm(KB, kfB[c], kfB[c], start=(c == 0), stop=(c == 7))
        for c in range(NCH):
            tr.mm(L10T, kfA[c], kfB[c], start=(c == 0), stop=(c == 7))
        tr.s_copy((sh.l10t[:, pc], f"l10t{par}"), L10T)
        ch = sh.ch
        for blk, Kps in ((0, KA), (1, KB)):
            rows = slice(0, 64) if blk == 0 else slice(64, 128)
            L = (ch["L"][rows, pc], f"ch_L{par}")
            LT = (ch["LT"][rows, pc], f"ch_LT{par}")
            AT = (ch["AT"][rows, pc], f"ch_AT{par}")
            P = (ch["P"][rows, pc], f"ch_P{par}")
            PT = (ch["PT"][rows, pc], f"ch_PT{par}")
            ml = (cs.mask64_low[rows, :], "mask64_low")
            mu = (cs.mask64_up[rows, :], "mask64_up")
            iv = (cs.ident64[rows, :], "ident64")
            tr.tt(L, Kps, ml, ALU.mult)
            tr.tt(LT, Kps, mu, ALU.mult)
            tr.tt(AT, iv, LT, ALU.subtract)
            tr.copy(P, L)
            tr.copy(PT, LT)
            if blk == 0:
                c1 = (pb[4][0:64, 64:128], "pb4")
                c2 = (pb[5][0:64, 64:128], "pb5")
                c3 = (pb[6][0:64, 64:128], "pb6")
            else:
                c1 = (pb[7][64:128, 0:64], "pb7")
                c2 = (pb[7][64:128, 64:128], "pb7")
                c3 = (pb[5][64:128, 64:128], "pb5")
            for step in range(5):
                tr.mm(c1, PT, P)
                tr.mm(c2, P, PT)
                tr.s_copy(P, c1)
                tr.s_copy(PT, c2)
                tr.mm(c3, P, AT)
                tr.tt(AT, AT, c3, ALU.add)
            tr.copy((sh.dnt[rows, pc], f"dnt{par}"), AT)
        rps = [(pb[0][:], "pb0"), (pb[1][:], "pb1")]
        for cc in range(NCH):
            for h in range(2):
                tr.mm(rps[h], kf[:, cc * 128:(cc + 1) * 128],
                      (sh.A0[:, cc * 1024 + h * 512:cc * 1024 + (h + 1) * 512], "A0"),
                      start=(cc == 0), stop=(cc == 7))
        ups = [(pb[2][:], "pb2"), (pb[3][:], "pb3")]
        for h in range(2):
            tr.tt((sh.r16[0:64, h * 512:(h + 1) * 512], "r16"),
                  vt[0:64, h * 512:(h + 1) * 512],
                  (pb[h][0:64, :], f"pb{h}"), ALU.subtract)
            tr.mm((ups[h][0][0:64, :], f"pb{2 + h}"), (sh.dnt[0:64, pc], f"dnt{par}"),
                  (sh.r16[0:64, h * 512:(h + 1) * 512], "r16"))
            tr.s_copy((sh.u16[0:64, h * 512:(h + 1) * 512], "u16"),
                      (ups[h][0][0:64, :], f"pb{2 + h}"))
            tr.mm((ups[h][0][64:128, :], f"pb{2 + h}"), (sh.l10t[:, pc], f"l10t{par}"),
                  (sh.u16[0:64, h * 512:(h + 1) * 512], "u16"))
            tr.tt((sh.r16[64:128, h * 512:(h + 1) * 512], "r16"),
                  vt[64:128, h * 512:(h + 1) * 512],
                  (pb[h][64:128, :], f"pb{h}"), ALU.subtract)
            tr.tt((sh.r16[64:128, h * 512:(h + 1) * 512], "r16"),
                  (sh.r16[64:128, h * 512:(h + 1) * 512], "r16"),
                  (ups[h][0][64:128, :], f"pb{2 + h}"), ALU.subtract)
            tr.mm((ups[h][0][64:128, :], f"pb{2 + h}"), (sh.dnt[64:128, pc], f"dnt{par}"),
                  (sh.r16[64:128, h * 512:(h + 1) * 512], "r16"))
            tr.s_copy((sh.u16[64:128, h * 512:(h + 1) * 512], "u16"),
                      (ups[h][0][64:128, :], f"pb{2 + h}"))
            tr.ts((sh.ulr16[:, h * 512:(h + 1) * 512], "ulr16"),
                  (sh.u16[:, h * 512:(h + 1) * 512], "u16"),
                  (sh.lrch[:, i:i + 1], "lrch"), None, ALU.mult)
        pbig_r = _ap(pb[0]).tensor
        for cc in range(0, NCH, 2):
            for src16, arena, akey in ((sh.u16, sh.A0, "A0"), (sh.ulr16, sh.A1, "A1")):
                for j in range(2):
                    for h in range(2):
                        tr.mm(pbig_r[:, (j * 2 + h) * 512:(j * 2 + h + 1) * 512],
                              kt[:, (cc + j) * 128:(cc + j + 1) * 128],
                              (src16[:, h * 512:(h + 1) * 512],
                               "u16" if src16 is sh.u16 else "ulr16"))
                dst = (arena[:, cc * 1024:(cc + 2) * 1024], akey)
                tr.tt(dst, dst, pbig_r[:, 0:2048], ALU.add)


def emit_l0a_output(tr, cs, sh, pb, w, wb, qfm_d, xln_d, x1a_d):
    nc = tr.nc
    for j in range(16):
        tr.s_copy((sh.WA[:, j * 512:(j + 1) * 512], "WA"),
                  (sh.A1[:, j * 512:(j + 1) * 512], "A1"))
    for c in range(NCH):
        tr.dma(sh.WB[:, c * 1024:(c + 1) * 1024],
               wb["L0_attn_out_w"][c * 128:(c + 1) * 128, :], queue="scalar")
    outb = load_bias_cols(tr, cs, w["L0_attn_out_b"], C, pb[6], sh, 0)
    for tb in range(NTB):
        cols = slice(tb * 512, (tb + 1) * 512)
        for c in range(NCH):
            tr.dma(sh.zb[c][:], qfm_d[c * 128:(c + 1) * 128, cols])
        for cp in range(NCH):
            yp = (pb[cp % 2][:], f"pb{cp % 2}")
            for c in range(NCH):
                tr.mm(yp, (sh.WA[:, c * 1024 + cp * 128:c * 1024 + (cp + 1) * 128], "WA"),
                      sh.zb[c][:], start=(c == 0), stop=(c == 7))
            tr.s_copy((sh.P0[:, cp * 512:(cp + 1) * 512], "P0"), yp)
        for dc in range(NCH):
            op_ = (pb[2 + dc % 2][:], f"pb{2 + dc % 2}")
            for cp in range(NCH):
                tr.mm(op_, sh.WB[:, cp * 1024 + dc * 128:cp * 1024 + (dc + 1) * 128],
                      (sh.P0[:, cp * 512:(cp + 1) * 512], "P0"),
                      start=(cp == 0), stop=(cp == 7))
            g0 = sh.g[dc % 2]
            tr.act(g0[:], op_, AF.Identity, bias=outb[:, dc:dc + 1])
            g1 = sh.g[2 + dc % 2]
            tr.dma(g1[:], xln_d[dc * 128:(dc + 1) * 128, cols])
            tr.tt(g1[:], g1[:], g0[:], ALU.add)
            tr.dma(x1a_d[dc * 128:(dc + 1) * 128, cols], g1[:])


def emit_l0a_mlp(tr, cs, sh, pb, w, wb, z2bf, x1a_d, xfm0, xfm1):
    for c in range(NCH):
        tr.dma(sh.WA[:, c * 1024:(c + 1) * 1024],
               wb["L0_attn_w1"][c * 128:(c + 1) * 128, :], queue="scalar")
        tr.dma(sh.WB[:, c * 1024:(c + 1) * 1024],
               wb["L0_attn_w2"][c * 128:(c + 1) * 128, :], queue="scalar")
    b1c = load_bias_cols(tr, cs, w["L0_attn_b1"], C, pb[6], sh, 0)
    b2c = load_bias_cols(tr, cs, w["L0_attn_b2"], C, pb[7], sh, 8)
    for tb in range(NTB):
        cols = slice(tb * 512, (tb + 1) * 512)
        for c in range(NCH):
            tr.dma(sh.zb[c][:], z2bf[c * 128:(c + 1) * 128, cols])
        for f in range(NCH):
            hp = (pb[f % 2][:], f"pb{f % 2}")
            for c in range(NCH):
                tr.mm(hp, (sh.WA[:, c * 1024 + f * 128:c * 1024 + (f + 1) * 128], "WA"),
                      sh.zb[c][:], start=(c == 0), stop=(c == 7))
            tr.act((sh.P0[:, f * 512:(f + 1) * 512], "P0"), hp, AF.Relu,
                   bias=b1c[:, f:f + 1])
        for dc in range(NCH):
            op_ = (pb[2 + dc % 2][:], f"pb{2 + dc % 2}")
            for f in range(NCH):
                tr.mm(op_, (sh.WB[:, f * 1024 + dc * 128:f * 1024 + (dc + 1) * 128], "WB"),
                      (sh.P0[:, f * 512:(f + 1) * 512], "P0"),
                      start=(f == 0), stop=(f == 7))
            g0 = sh.g[dc % 2]
            tr.act(g0[:], op_, AF.Identity, bias=b2c[:, dc:dc + 1])
            g1 = sh.g[2 + dc % 2]
            tr.dma(g1[:], x1a_d[dc * 128:(dc + 1) * 128, cols])
            tr.tt(g1[:], g1[:], g0[:], ALU.add)
            g2 = (sh.A0[:, (dc % 2) * 512:(dc % 2 + 1) * 512], "A0")
            tr.dma(g2, xfm0[dc * 128:(dc + 1) * 128, cols])
            tr.tt(g1[:], g1[:], g2, ALU.add)
            tr.dma(xfm1[dc * 128:(dc + 1) * 128, cols], g1[:])



def vaug_tile(sh, kt):
    if kt < 15:
        return (sh.A0[:].bitcast(BF16)[:, kt * 1040:(kt + 1) * 1040], "A0")
    if kt < 30:
        j = kt - 15
        return (sh.A1[:].bitcast(BF16)[:, j * 1040:(j + 1) * 1040], "A1")
    j = kt - 30
    return (sh.A2[:].bitcast(BF16)[:, j * 1040:(j + 1) * 1040], "A2")


def pt_tile(sh, kt):
    if kt < 16:
        return (sh.P0[:, kt * 512:(kt + 1) * 512], "P0")
    return (sh.P1[:, (kt - 16) * 512:(kt - 15) * 512], "P1")


def emit_fm_proj(tr, cs, sh, pb, w_dram, bias_vec, src_bf, dst, res_fm=None,
                 func=AF.Identity, dst_dtype=F32, wslab=None, bslot=0):
    wsl, wkey = wslab if wslab is not None else (sh.WA, "WA")
    for c in range(NCH):
        tr.dma((wsl[:, c * 1024:(c + 1) * 1024], wkey),
               w_dram[c * 128:(c + 1) * 128, :], queue="scalar")
    bcol = load_bias_cols(tr, cs, bias_vec, C, pb[6], sh, bslot) if bias_vec is not None else None
    for tb in range(NTB):
        cols = slice(tb * 512, (tb + 1) * 512)
        for c in range(NCH):
            tr.dma(sh.zb[c][:], src_bf[c * 128:(c + 1) * 128, cols])
        for dc in range(NCH):
            op_ = (pb[dc % 2][:], f"pb{dc % 2}")
            for c in range(NCH):
                tr.mm(op_, (wsl[:, c * 1024 + dc * 128:c * 1024 + (dc + 1) * 128], wkey),
                      sh.zb[c][:], start=(c == 0), stop=(c == 7))
            if dst_dtype == BF16:
                o = sh.b16[dc % 2]
                if bcol is not None:
                    tr.act(o[:], op_, func, bias=bcol[:, dc:dc + 1])
                else:
                    tr.s_copy(o[:], op_)
                tr.dma(dst[dc * 128:(dc + 1) * 128, cols], o[:])
            else:
                g0 = sh.g[dc % 2]
                if bcol is not None:
                    tr.act(g0[:], op_, func, bias=bcol[:, dc:dc + 1])
                else:
                    tr.s_copy(g0[:], op_)
                if res_fm is not None:
                    g1 = sh.g[2 + dc % 2]
                    tr.dma(g1[:], res_fm[dc * 128:(dc + 1) * 128, cols])
                    tr.tt(g0[:], g0[:], g1[:], ALU.add)
                tr.dma(dst[dc * 128:(dc + 1) * 128, cols], g0[:])


def emit_l0f_qkv(tr, cs, sh, pb, w, wb, zsrc, qfm2_d, kfm2_d):
    for c in range(NCH):
        tr.dma((sh.WA[:, c * 1024:(c + 1) * 1024], "WA"),
               wb["L0_ffn_Wq"][c * 128:(c + 1) * 128, :], queue="scalar")
        tr.dma((sh.WB[:, c * 1024:(c + 1) * 1024], "WB"),
               wb["L0_ffn_Wk"][c * 128:(c + 1) * 128, :], queue="scalar")
        tr.dma((sh.P0[:, c * 1024:(c + 1) * 1024], "P0"),
               wb["L0_ffn_Wv"][c * 128:(c + 1) * 128, :], queue="scalar")
    for tb in range(NTB):
        cols = slice(tb * 512, (tb + 1) * 512)
        for c in range(NCH):
            tr.dma(sh.zb[c][:], zsrc[c * 128:(c + 1) * 128, cols])
        for dc in range(NCH):
            qp = (pb[dc % 2][:], f"pb{dc % 2}")
            kp = (pb[2 + dc % 2][:], f"pb{2 + dc % 2}")
            for c in range(NCH):
                tr.mm(qp, (sh.WA[:, c * 1024 + dc * 128:c * 1024 + (dc + 1) * 128], "WA"),
                      sh.zb[c][:], start=(c == 0), stop=(c == 7))
            for c in range(NCH):
                tr.mm(kp, (sh.WB[:, c * 1024 + dc * 128:c * 1024 + (dc + 1) * 128], "WB"),
                      sh.zb[c][:], start=(c == 0), stop=(c == 7))
            tr.s_copy(sh.b16[0][:], qp)
            tr.dma(qfm2_d[dc * 128:(dc + 1) * 128, cols], sh.b16[0][:])
            tr.s_copy(sh.b16[1][:], kp)
            tr.dma(kfm2_d[dc * 128:(dc + 1) * 128, cols], sh.b16[1][:], queue="scalar")
        for tch in range(4):
            trow = tb * 4 + tch
            tcs = slice(tch * 128, (tch + 1) * 128)
            for c in range(NCH):
                zt = sh.zb[c][:, tcs]
                tr.mm((pb[4][:], "pb4"), zt, (sh.P0[:, c * 1024:c * 1024 + 512], "P0"),
                      start=(c == 0), stop=(c == 7))
                tr.mm((pb[5][:], "pb5"), zt, (sh.P0[:, c * 1024 + 512:(c + 1) * 1024], "P0"),
                      start=(c == 0), stop=(c == 7))
            vt = vaug_tile(sh, trow)
            tr.memset(vt, 1.0)
            for hh in range(8):
                tr.s_copy((vt[0][:, hh * 65:hh * 65 + 64], vt[1]),
                          (pb[4][:, hh * 64:(hh + 1) * 64], "pb4"))
            for hh in range(8):
                tr.s_copy((vt[0][:, (8 + hh) * 65:(8 + hh) * 65 + 64], vt[1]),
                          (pb[5][:, hh * 64:(hh + 1) * 64], "pb5"))


def emit_l0f_attn(tr, cs, sh, pb, qfm2_d, kfm2_d, ctx_d):
    pbig = _ap(pb[0]).tensor
    kslab = (sh.WA[:, 0:4096], "WA")
    qslab = (sh.WB[:, 0:4096], "WB")
    for hp in range(8):
        tr.dma(kslab, kfm2_d[hp * 128:(hp + 1) * 128, :])
        tr.dma(qslab, qfm2_d[hp * 128:(hp + 1) * 128, :])
        for h2 in range(2):
            base = 64 * h2
            rows = slice(base, base + 64)
            for which, slab in ((0, qslab), (1, kslab)):
                for j in range(NTB):
                    sq = (sh.xb[j][rows, :], f"sh_xb{j}")
                    tr.square(sq, (slab[0][rows, j * 512:(j + 1) * 512], slab[1]))
                    tr.mm((pb[6][0:1, :], "pb6"), cs.ones_col_f32[rows, 0:1], sq)
                    tr.reduce((sh.stat[0:1, which * 8 + j:which * 8 + j + 1], "ln_mean"),
                              (pb[6][0:1, :], "pb6"), ALU.max)
            tr.reduce((sh.stat[0:1, 16:17], "ln_mean"), (sh.stat[0:1, 0:8], "ln_mean"), ALU.max)
            tr.reduce((sh.stat[0:1, 17:18], "ln_mean"), (sh.stat[0:1, 8:16], "ln_mean"), ALU.max)
            tr.tt((sh.stat[0:1, 18:19], "ln_mean"), (sh.stat[0:1, 16:17], "ln_mean"),
                  (sh.stat[0:1, 17:18], "ln_mean"), ALU.mult)
            tr.act((sh.stat[0:1, 18:19], "ln_mean"), (sh.stat[0:1, 18:19], "ln_mean"), AF.Ln)
            tr.act((sh.stat[0:1, 18:19], "ln_mean"), (sh.stat[0:1, 18:19], "ln_mean"),
                   AF.Exp, scale=0.5)
            tr.ts((sh.stat[0:1, 18:19], "ln_mean"), (sh.stat[0:1, 18:19], "ln_mean"),
                  -0.125, None, ALU.mult)
            tr.mm((pb[7][:, 0:1], "pb7"), cs.ones_row_f32[:],
                  (sh.stat[0:1, 18:19], "ln_mean"))
            negb = (sh.col[:, 4:5], "sh_col")
            tr.s_copy(negb, (pb[7][:, 0:1], "pb7"))
            h = hp * 2 + h2
            for qb in range(NTB):
                qcs = slice(qb * 512, (qb + 1) * 512)
                cp = (pb[4][0:65, :], "pb4")
                for half in range(2):
                    for kp in range(8):
                        b0 = (kp % 2) * 2
                        for j in range(2):
                            kt = half * 16 + kp * 2 + j
                            tr.mm(pbig[:, (b0 + j) * 512:(b0 + j + 1) * 512],
                                  (kslab[0][rows, kt * 128:(kt + 1) * 128], kslab[1]),
                                  (qslab[0][rows, qcs], qslab[1]))
                        kt0 = half * 16 + kp * 2
                        ptp = ((sh.P0 if half == 0 else sh.P1)
                               [:, (kt0 % 16) * 512:((kt0 % 16) + 2) * 512],
                               "P0" if half == 0 else "P1")
                        tr.act(ptp, pbig[:, b0 * 512:(b0 + 2) * 512], AF.Exp,
                               bias=negb, scale=0.125)
                    for kt in range(half * 16, half * 16 + 16):
                        vt = vaug_tile(sh, kt)
                        tr.mm(cp, (vt[0][:, h * 65:(h + 1) * 65], vt[1]),
                              pt_tile(sh, kt), start=(kt == 0), stop=(kt == 31))
                tr.recip((sh.row[0:1, 0:512], "sh_row"), (pb[4][64:65, :], "pb4"))
                tr.mm((pb[5][0:64, :], "pb5"), cs.ones_row_f32[0:1, 0:64],
                      (sh.row[0:1, 0:512], "sh_row"))
                tr.s_copy((sh.g[0][0:64, :], "sh_g0"), (pb[5][0:64, :], "pb5"))
                ctx = sh.xb[4 * h2 + qb // 2][:].bitcast(BF16)
                tr.tt((ctx[0:64, (qb % 2) * 512:(qb % 2 + 1) * 512], f"sh_xb{4 * h2 + qb // 2}"),
                      (pb[4][0:64, :], "pb4"), (sh.g[0][0:64, :], "sh_g0"), ALU.mult)
        for h2 in range(2):
            for j in range(4):
                ctx = sh.xb[4 * h2 + j][:].bitcast(BF16)
                tr.dma(ctx_d[hp * 128 + h2 * 64:hp * 128 + h2 * 64 + 64,
                             j * 1024:(j + 1) * 1024],
                       (ctx[0:64, :], f"sh_xb{4 * h2 + j}"), queue="gpsimd")


def emit_l0f_router(tr, cs, sh, pb, w, out1b_d, g_d):
    nc = tr.nc
    wr = nc.alloc_sbuf_tensor("wr_bf", [128, 64], BF16)
    for c in range(NCH):
        tr.dma(wr[:, c * 8:(c + 1) * 8], w["L0_ffn_Wr"][c * 128:(c + 1) * 128, :],
               queue="gpsimd")
    big8 = nc.alloc_sbuf_tensor("big8", [128, 8], F32)
    tr.memset(big8, 1e4)
    tr.tt(big8[:], big8[:], cs.iota8[:], ALU.add)
    for tb in range(NTB):
        cols = slice(tb * 512, (tb + 1) * 512)
        for c in range(NCH):
            tr.dma(sh.zb[c][:], out1b_d[c * 128:(c + 1) * 128, cols])
        for tch in range(4):
            ti = tb * 4 + tch
            tcs = slice(tch * 128, (tch + 1) * 128)
            lp = (pb[tch % 2][:, 0:8], f"pb{tch % 2}")
            for c in range(NCH):
                tr.mm(lp, sh.zb[c][:, tcs], wr[:, c * 8:(c + 1) * 8],
                      start=(c == 0), stop=(c == 7))
            lg = (sh.moeA[:, 0:8], "moeA")
            tr.s_copy(lg, lp)
            nm = (sh.moeT[:, 0:1], "moeT")
            tr.reduce(nm, lg, ALU.max, negate=True)
            ex = (sh.moeT[:, 8:16], "moeT")
            tr.act(ex, lg, AF.Exp, bias=nm)
            sm = (sh.moeT[:, 1:2], "moeT")
            tr.reduce(sm, ex, ALU.add)
            rs = (sh.moeT[:, 2:3], "moeT")
            tr.recip(rs, sm)
            pr = (sh.moeB[:, 0:8], "moeB")
            tr.ts(pr, ex, rs, None, ALU.mult)
            m1 = (sh.moeT[:, 3:4], "moeT")
            tr.reduce(m1, pr, ALU.max)
            t0 = (sh.moeT[:, 16:24], "moeT")
            tr.ts(t0, pr, m1, None, ALU.is_ge)
            t1 = (sh.moeT[:, 24:32], "moeT")
            tr.stt(t1, t0, -1e4, big8[:], ALU.mult, ALU.add)
            i1 = (sh.moeT[:, 4:5], "moeT")
            tr.reduce(i1, t1, ALU.min)
            oh1 = (sh.moeT[:, 32:40], "moeT")
            tr.ts(oh1, cs.iota8[:], i1, None, ALU.is_equal)
            p2m = (sh.moeT[:, 40:48], "moeT")
            tr.stt(p2m, oh1, -1e30, pr, ALU.mult, ALU.add)
            m2 = (sh.moeT[:, 5:6], "moeT")
            tr.reduce(m2, p2m, ALU.max)
            tr.ts(t0, p2m, m2, None, ALU.is_ge)
            tr.stt(t1, t0, -1e4, big8[:], ALU.mult, ALU.add)
            i2 = (sh.moeT[:, 6:7], "moeT")
            tr.reduce(i2, t1, ALU.min)
            oh2 = (sh.moeT[:, 48:56], "moeT")
            tr.ts(oh2, cs.iota8[:], i2, None, ALU.is_equal)
            e1 = (sh.moeT[:, 7:8], "moeT")
            tr.act(e1, m1, AF.Exp)
            e2 = (sh.moeT[:, 56:57], "moeT")
            tr.act(e2, m2, AF.Exp)
            den = (sh.moeT[:, 57:58], "moeT")
            tr.tt(den, e1, e2, ALU.add)
            rden = (sh.moeT[:, 58:59], "moeT")
            tr.recip(rden, den)
            w1 = (sh.moeT[:, 59:60], "moeT")
            tr.tt(w1, e1, rden, ALU.mult)
            w2 = (sh.moeT[:, 60:61], "moeT")
            tr.tt(w2, e2, rden, ALU.mult)
            Gc = (sh.moeA[:, 8:16], "moeA")
            tr.ts(Gc, oh1, w1, None, ALU.mult)
            Gc2 = (sh.moeA[:, 16:24], "moeA")
            tr.ts(Gc2, oh2, w2, None, ALU.mult)
            tr.tt(Gc, Gc, Gc2, ALU.add)
            tr.transpose((pb[7][0:8, 0:128], "pb7"), Gc, cs.ident128[:])
            tr.s_copy((sh.g8[0:8, :], "sh_g8"), (pb[7][0:8, 0:128], "pb7"))
            tr.dma(g_d[:, ti * 128:(ti + 1) * 128], (sh.g8[0:8, :], "sh_g8"),
                   queue="scalar")


def emit_l0f_moe(tr, cs, sh, pb, w, wb, out1b_d, g_d, acc_d):
    for e in range(8):
        be1c = load_bias_cols(tr, cs, w["L0_ffn_be1"][e, :], 4096, pb[6], sh, 0)
        be2c = load_bias_cols(tr, cs, w["L0_ffn_be2"][e, :], C, pb[7], sh, 32)
        for tb in range(NTB):
            cols = slice(tb * 512, (tb + 1) * 512)
            for c in range(NCH):
                tr.dma(sh.zb[c][:], out1b_d[c * 128:(c + 1) * 128, cols])
            gate = sh.g[2]
            tr.dma(gate[:], g_d[e:e + 1, cols].broadcast_to([128, 512]))
            for fg in range(8):
                hh = (fg % 2) * 512
                for c in range(NCH):
                    tr.dma(sh.WA[:, c * 1024 + hh:c * 1024 + hh + 512],
                           wb["L0_ffn_We1"][e * 1024 + c * 128:e * 1024 + (c + 1) * 128,
                                            fg * 512:(fg + 1) * 512],
                           queue="scalar")
                for fi in range(4):
                    f = fg * 4 + fi
                    hp_ = (pb[f % 2][:], f"pb{f % 2}")
                    for c in range(NCH):
                        tr.mm(hp_, sh.WA[:, c * 1024 + hh + fi * 128:
                                         c * 1024 + hh + (fi + 1) * 128],
                              sh.zb[c][:], start=(c == 0), stop=(c == 7))
                    tr.act(pt_tile(sh, f), hp_, AF.Gelu, bias=be1c[:, f:f + 1])
            for f in range(32):
                tr.dma((sh.WB[:, (f % 8) * 1024:(f % 8 + 1) * 1024], "WB"),
                       wb["L0_ffn_We2"][e * 4096 + f * 128:e * 4096 + (f + 1) * 128, :],
                       queue="scalar")
                for dc in range(NCH):
                    tr.mm((pb[dc][:], f"pb{dc}"),
                          (sh.WB[:, (f % 8) * 1024 + dc * 128:(f % 8) * 1024 + (dc + 1) * 128],
                           "WB"), pt_tile(sh, f), start=(f == 0), stop=(f == 31))
            for dc in range(NCH):
                g0 = sh.g[dc % 2]
                tr.stt(g0[:], (pb[dc][:], f"pb{dc}"), be2c[:, dc:dc + 1], gate[:],
                       ALU.add, ALU.mult)
                tr.dma(acc_d[dc * 128:(dc + 1) * 128, cols], g0[:], queue="gpsimd",
                       accum_op=(ALU.add if e > 0 else None))


def emit_add_fm(tr, cs, sh, pb, a_d, b_d, dst_d):
    for c in range(NCH):
        for tb in range(NTB):
            cols = slice(tb * 512, (tb + 1) * 512)
            g0, g1 = sh.g[tb % 2], sh.g[2 + tb % 2]
            tr.dma(g0[:], a_d[c * 128:(c + 1) * 128, cols])
            tr.dma(g1[:], b_d[c * 128:(c + 1) * 128, cols], queue="scalar")
            tr.tt(g0[:], g0[:], g1[:], ALU.add)
            tr.dma(dst_d[c * 128:(c + 1) * 128, cols], g0[:], queue="gpsimd")


def emit_l1a_qkv_v(tr, cs, sh, pb, w, wb, zsrc):
    for c in range(NCH):
        tr.dma((sh.P0[:, c * 1024:(c + 1) * 1024], "P0"),
               wb["L1_attn_Wv"][c * 128:(c + 1) * 128, :], queue="scalar")
    tr.dma(sh.g[2][:], w["L1_attn_bv"][0:512].unsqueeze(0).broadcast_to([128, 512]))
    tr.dma(sh.g[3][:], w["L1_attn_bv"][512:1024].unsqueeze(0).broadcast_to([128, 512]))
    for tb in range(NTB):
        cols = slice(tb * 512, (tb + 1) * 512)
        for c in range(NCH):
            tr.dma(sh.zb[c][:], zsrc[c * 128:(c + 1) * 128, cols])
        for tch in range(4):
            trow = tb * 4 + tch
            tcs = slice(tch * 128, (tch + 1) * 128)
            for c in range(NCH):
                zt = sh.zb[c][:, tcs]
                tr.mm((pb[4][:], "pb4"), zt, (sh.P0[:, c * 1024:c * 1024 + 512], "P0"),
                      start=(c == 0), stop=(c == 7))
                tr.mm((pb[5][:], "pb5"), zt, (sh.P0[:, c * 1024 + 512:(c + 1) * 1024], "P0"),
                      start=(c == 0), stop=(c == 7))
            h0 = sh.g[0]
            tr.tt(h0[:], (pb[4][:], "pb4"), sh.g[2][:], ALU.add)
            h1 = sh.g[1]
            tr.tt(h1[:], (pb[5][:], "pb5"), sh.g[3][:], ALU.add)
            vt = vaug_tile(sh, trow)
            tr.memset(vt, 1.0)
            for hh in range(8):
                tr.s_copy((vt[0][:, hh * 65:hh * 65 + 64], vt[1]),
                          (h0[:, hh * 64:(hh + 1) * 64], "sh_g0"))
            for hh in range(8):
                tr.s_copy((vt[0][:, (8 + hh) * 65:(8 + hh) * 65 + 64], vt[1]),
                          (h1[:, hh * 64:(hh + 1) * 64], "sh_g1"))


def emit_l1a_stats(tr, cs, sh, pb, qfm3_d, kfm3_d, statall, negbs):
    nc = tr.nc
    kslab = (sh.WA[:, 0:4096], "WA")
    qslab = (sh.WB[:, 0:4096], "WB")
    ks16 = sh.col[:].bitcast(BF16)
    for hp in range(8):
        tr.dma(kslab, kfm3_d[hp * 128:(hp + 1) * 128, :])
        tr.dma(qslab, qfm3_d[hp * 128:(hp + 1) * 128, :])
        for h2 in range(2):
            h = 2 * hp + h2
            base = 64 * h2
            rows = slice(base, base + 64)
            for which, slab in ((0, qslab), (1, kslab)):
                for j in range(NTB):
                    sq = (sh.xb[j][rows, :], f"sh_xb{j}")
                    tr.square(sq, (slab[0][rows, j * 512:(j + 1) * 512], slab[1]))
                    tr.mm((pb[6][0:1, :], "pb6"), cs.ones_col_f32[rows, 0:1], sq)
                    tr.reduce((sh.stat[0:1, which * 8 + j:which * 8 + j + 1], "ln_mean"),
                              (pb[6][0:1, :], "pb6"), ALU.max)
            tr.reduce((sh.stat[0:1, 16:17], "ln_mean"), (sh.stat[0:1, 0:8], "ln_mean"), ALU.max)
            tr.reduce((sh.stat[0:1, 17:18], "ln_mean"), (sh.stat[0:1, 8:16], "ln_mean"), ALU.max)
            tr.tt((sh.stat[0:1, 18:19], "ln_mean"), (sh.stat[0:1, 16:17], "ln_mean"),
                  (sh.stat[0:1, 17:18], "ln_mean"), ALU.mult)
            tr.act((sh.stat[0:1, 18:19], "ln_mean"), (sh.stat[0:1, 18:19], "ln_mean"), AF.Ln)
            tr.act((sh.stat[0:1, 18:19], "ln_mean"), (sh.stat[0:1, 18:19], "ln_mean"),
                   AF.Exp, scale=0.5)
            tr.s_copy((sh.stat[0:1, 32 + h:33 + h], "ln_bound"),
                      (sh.stat[0:1, 18:19], "ln_mean"))
            tr.ts((sh.stat[0:1, 18:19], "ln_mean"), (sh.stat[0:1, 18:19], "ln_mean"),
                  -0.125, None, ALU.mult)
            tr.mm((pb[7][:, 0:1], "pb7"), cs.ones_row_f32[:], (sh.stat[0:1, 18:19], "ln_mean"))
            tr.s_copy((negbs[:, h:h + 1], "sh_col"), (pb[7][:, 0:1], "pb7"))
            tr.reduce((sh.col[rows, 32:33], "sh_col"), (kslab[0][rows, :], kslab[1]),
                      ALU.add)
            tr.copy((ks16[rows, 100:101], "sh_col"), (sh.col[rows, 32:33], "sh_col"))
            pbig_s = _ap(pb[0]).tensor
            for qt in range(NTC):
                qv = (qslab[0][rows, qt * 128:(qt + 1) * 128], qslab[1])
                for half in range(2):
                    hb = (qt * 2 + half) % 2
                    for j in range(4):
                        kb = half * 4 + j
                        tr.mm(pbig_s[:, (hb * 4 + j) * 512:(hb * 4 + j + 1) * 512],
                              qv, (kslab[0][rows, kb * 512:(kb + 1) * 512], kslab[1]))
                    tr.reduce((sh.moeT[:, qt % 4 * 2 + half:qt % 4 * 2 + half + 1],
                               "moeT"),
                              pbig_s[:, hb * 2048:(hb + 1) * 2048], ALU.max)
                tr.reduce((sh.moeT[:, 8:9], "moeT"),
                          (sh.moeT[:, qt % 4 * 2:qt % 4 * 2 + 2], "moeT"), ALU.max)
                tr.mm((sh_pbq := pbig_s[0:128, 4095:4096]), qv,
                      (ks16[rows, 100:101], "sh_col")) if False else None
                tr.mm(pbig_s[:, 2048 + (qt % 2) * 512:2048 + (qt % 2) * 512 + 1].bitcast(F32) if False else pbig_s[:, ((qt + 1) % 2) * 2048:((qt + 1) % 2) * 2048 + 1],
                      qv, (ks16[rows, 100:101], "sh_col"))
                tr.ts((sh.moeT[:, 9:10], "moeT"),
                      pbig_s[:, ((qt + 1) % 2) * 2048:((qt + 1) % 2) * 2048 + 1],
                      1.0 / T, None, ALU.mult)
                tr.tt((statall[:, h * 32 + qt:h * 32 + qt + 1], "A2"),
                      (sh.moeT[:, 8:9], "moeT"), (sh.moeT[:, 9:10], "moeT"), ALU.subtract)


def emit_l1a_select(tr, cs, sh, pb, statall, mask_td):
    U = 1228.0
    lo = (sh.stat[0:1, 64:80], "ln_bis")
    hi = (sh.stat[0:1, 80:96], "ln_bis")
    mid = (sh.stat[0:1, 96:112], "ln_bis")
    cnt = (sh.stat[0:1, 112:128], "ln_bis")
    tmp = (sh.stat[0:1, 128:144], "ln_bis")
    tr.memset(lo, 0.0)
    tr.ts(hi, (sh.stat[0:1, 32:48], "ln_bound"), 2.0, 1.0, ALU.mult, ALU.add)
    msk = sh.xb[0]
    red = (sh.xb[1][:, 0:16], "sh_xb1")
    for it in range(60):
        tr.tt(mid, lo, hi, ALU.add)
        tr.ts(mid, mid, 0.5, None, ALU.mult)
        tr.mm((pb[3][:], "pb3"), cs.ones_row_f32[:],
              (_ap(mid).unsqueeze(2).broadcast_to([1, 16, 32]), "ln_bis"))
        tr.tt((msk[:], "sh_xb0"), (statall[:], "A2"), (pb[3][:], "pb3"), ALU.is_ge)
        tr.reduce(red, (msk[:].rearrange("p (h w) -> p h w", h=16), "sh_xb0"), ALU.add)
        tr.mm((pb[4][0:1, 0:16], "pb4"), cs.ones_col_f32[:], red)
        tr.ts(cnt, (pb[4][0:1, 0:16], "pb4"), U, None, ALU.is_ge)
        tr.tt(tmp, mid, lo, ALU.subtract)
        tr.tt(tmp, cnt, tmp, ALU.mult)
        tr.tt(lo, lo, tmp, ALU.add)
        tr.tt(tmp, hi, mid, ALU.subtract)
        tr.tt(tmp, cnt, tmp, ALU.mult)
        tr.tt(hi, mid, tmp, ALU.add)
    for h in range(16):
        tr.mm((pb[5][:, 0:1], "pb5"), cs.ones_row_f32[:],
              (sh.stat[0:1, 64 + h:65 + h], "ln_bis"))
        tr.s_copy((sh.col[:, 33:34], "sh_col"), (pb[5][:, 0:1], "pb5"))
        tr.ts((msk[:, h * 32:(h + 1) * 32], "sh_xb0"),
              (statall[:, h * 32:(h + 1) * 32], "A2"),
              (sh.col[:, 33:34], "sh_col"), None, ALU.is_ge)
    for cb in range(4):
        tr.transpose((pb[6][:, 0:128], "pb6"),
                     (msk[:, cb * 128:(cb + 1) * 128], "sh_xb0"), cs.ident128[:])
        tr.s_copy((sh.g[0][:, 0:128], "sh_g0"), (pb[6][:, 0:128], "pb6"))
        tr.dma(mask_td[cb * 128:(cb + 1) * 128, :], sh.g[0][:, 0:128], queue="gpsimd")


def emit_l1a_attn(tr, cs, sh, pb, qfm3_d, kfm3_d, mask_td, negbs, ctx_d):
    pbig = _ap(pb[0]).tensor
    kslab = (sh.WA[:, 0:4096], "WA")
    qslab = (sh.WB[:, 0:4096], "WB")
    for hp in range(8):
        tr.dma(kslab, kfm3_d[hp * 128:(hp + 1) * 128, :])
        tr.dma(qslab, qfm3_d[hp * 128:(hp + 1) * 128, :])
        for h2 in range(2):
            h = hp * 2 + h2
            base = 64 * h2
            rows = slice(base, base + 64)
            negb = (negbs[:, h:h + 1], "sh_col")
            for qb in range(NTB):
                qcs = slice(qb * 512, (qb + 1) * 512)
                cp = (pb[4][0:65, :], "pb4")
                for half in range(2):
                    for kp in range(8):
                        b0 = (kp % 2) * 2
                        for j in range(2):
                            kt = half * 16 + kp * 2 + j
                            tr.mm(pbig[:, (b0 + j) * 512:(b0 + j + 1) * 512],
                                  (kslab[0][rows, kt * 128:(kt + 1) * 128], kslab[1]),
                                  (qslab[0][rows, qcs], qslab[1]))
                        kt0 = half * 16 + kp * 2
                        ptp = ((sh.P0 if half == 0 else sh.P1)
                               [:, (kt0 % 16) * 512:((kt0 % 16) + 2) * 512],
                               "P0" if half == 0 else "P1")
                        tr.act(ptp, pbig[:, b0 * 512:(b0 + 2) * 512], AF.Exp,
                               bias=negb, scale=0.125)
                    for kt in range(half * 16, half * 16 + 16):
                        vt = vaug_tile(sh, kt)
                        tr.mm(cp, (vt[0][:, h * 65:(h + 1) * 65], vt[1]),
                              pt_tile(sh, kt), start=(kt == 0), stop=(kt == 31))
                tr.recip((sh.row[0:1, 0:512], "sh_row"), (pb[4][64:65, :], "pb4"))
                mr = (sh.row[0:1, 512:1024], "sh_row")
                tr.dma(mr, mask_td[h * 32 + qb * 4:h * 32 + qb * 4 + 4, :]
                       .rearrange("a b -> (a b)").unsqueeze(0))
                tr.tt((sh.row[0:1, 0:512], "sh_row"), (sh.row[0:1, 0:512], "sh_row"),
                      mr, ALU.mult)
                tr.mm((pb[5][0:64, :], "pb5"), cs.ones_row_f32[0:1, 0:64],
                      (sh.row[0:1, 0:512], "sh_row"))
                tr.s_copy((sh.g[0][0:64, :], "sh_g0"), (pb[5][0:64, :], "pb5"))
                ctx = sh.xb[4 * h2 + qb // 2][:].bitcast(BF16)
                tr.tt((ctx[0:64, (qb % 2) * 512:(qb % 2 + 1) * 512],
                       f"sh_xb{4 * h2 + qb // 2}"),
                      (pb[4][0:64, :], "pb4"), (sh.g[0][0:64, :], "sh_g0"), ALU.mult)
        for h2 in range(2):
            for j in range(4):
                ctx = sh.xb[4 * h2 + j][:].bitcast(BF16)
                tr.dma(ctx_d[hp * 128 + h2 * 64:hp * 128 + h2 * 64 + 64,
                             j * 1024:(j + 1) * 1024],
                       (ctx[0:64, :], f"sh_xb{4 * h2 + j}"), queue="gpsimd")


def emit_l1f_gate(tr, cs, sh, pb, w, zsrc, g1_d):
    nc = tr.nc
    gwT_ap = sh.A2[:].bitcast(BF16)
    gwT = lambda a, b: (gwT_ap[:, 3072 + a:3072 + b], "A2")
    gwstage = (sh.A2[0:16, 0:1024], "A2")
    tr.dma(gwstage, w["L1_ffn_gate_w"])
    for c in range(NCH):
        tr.transpose((pb[6][0:128, 0:16], "pb6"),
                     (gwstage[0][:, c * 128:(c + 1) * 128], "A2"),
                     cs.ident128[0:16, 0:16])
        tr.s_copy(gwT(c * 16, (c + 1) * 16), (pb[6][0:128, 0:16], "pb6"))
    gb = (sh.moeB[:, 148:164], "moeB")
    tr.dma(gb, w["L1_ffn_gate_b"].unsqueeze(0).broadcast_to([128, 16]))
    big16 = (sh.moeB[:, 128:144], "moeB")
    tr.memset(big16, 1e4)
    tr.tt(big16, big16, cs.iota16[:], ALU.add)
    big4 = (sh.moeB[:, 144:148], "moeB")
    tr.memset(big4, 1e4)
    tr.tt(big4, big4, cs.iota4[:], ALU.add)
    mt = lambda a, b: (sh.moeT[:, a:b], "moeT")
    ma = lambda a, b: (sh.moeA[:, a:b], "moeA")
    mb = lambda a, b: (sh.moeB[:, a:b], "moeB")
    for tb in range(NTB):
        cols = slice(tb * 512, (tb + 1) * 512)
        for c in range(NCH):
            tr.dma(sh.zb[c][:], zsrc[c * 128:(c + 1) * 128, cols])
        for tch in range(4):
            ti = tb * 4 + tch
            tcs = slice(tch * 128, (tch + 1) * 128)
            lp = (pb[tch % 2][:, 0:16], f"pb{tch % 2}")
            for c in range(NCH):
                tr.mm(lp, sh.zb[c][:, tcs], gwT(c * 16, (c + 1) * 16),
                      start=(c == 0), stop=(c == 7))
            lg = ma(0, 16)
            tr.s_copy(lg, lp)
            nm = mt(0, 1)
            tr.reduce(nm, lg, ALU.max, negate=True)
            ex = ma(16, 32)
            tr.act(ex, lg, AF.Exp, bias=nm)
            sm = mt(1, 2)
            tr.reduce(sm, ex, ALU.add)
            rs = mt(2, 3)
            tr.recip(rs, sm)
            orig = mb(0, 16)
            tr.ts(orig, ex, rs, None, ALU.mult)
            scores = ma(32, 48)
            tr.tt(scores, orig, gb, ALU.add)
            for g in range(4):
                sg = ma(32 + g * 4, 36 + g * 4)
                m1g = mt(3, 4)
                tr.reduce(m1g, sg, ALU.max)
                mkg = mt(16, 20)
                tr.ts(mkg, sg, m1g, None, ALU.is_ge)
                p2 = mt(20, 24)
                tr.stt(p2, mkg, -1e30, sg, ALU.mult, ALU.add)
                m2g = mt(4, 5)
                tr.reduce(m2g, p2, ALU.max)
                tr.tt((sh.moeT[:, 8 + g:9 + g], "moeT"), m1g, m2g, ALU.add)
            gs = mt(8, 12)
            mg1 = mt(5, 6)
            tr.reduce(mg1, gs, ALU.max)
            mkg = mt(16, 20)
            tr.ts(mkg, gs, mg1, None, ALU.is_ge)
            t4 = mt(20, 24)
            tr.stt(t4, mkg, -1e4, big4, ALU.mult, ALU.add)
            ig1 = mt(6, 7)
            tr.reduce(ig1, t4, ALU.min)
            ohg1 = mt(24, 28)
            tr.ts(ohg1, cs.iota4[:], ig1, None, ALU.is_equal)
            gs2 = mt(28, 32)
            tr.stt(gs2, ohg1, -1e30, gs, ALU.mult, ALU.add)
            mg2 = mt(5, 6)
            tr.reduce(mg2, gs2, ALU.max)
            tr.ts(mkg, gs2, mg2, None, ALU.is_ge)
            tr.stt(t4, mkg, -1e4, big4, ALU.mult, ALU.add)
            ig2 = mt(7, 8)
            tr.reduce(ig2, t4, ALU.min)
            ohg2 = mt(32, 36)
            tr.ts(ohg2, cs.iota4[:], ig2, None, ALU.is_equal)
            keep = mt(36, 40)
            tr.tt(keep, ohg1, ohg2, ALU.add)
            for g in range(4):
                kadj = mt(12, 13)
                tr.ts(kadj, (sh.moeT[:, 36 + g:37 + g], "moeT"), -1.0, 1e30,
                      ALU.add, ALU.mult)
                tr.ts(ma(48 + g * 4, 52 + g * 4), ma(32 + g * 4, 36 + g * 4),
                      kadj, None, ALU.add)
            sp16 = ma(48, 64)
            m1 = mt(13, 14)
            tr.reduce(m1, sp16, ALU.max)
            mk16 = mb(16, 32)
            tr.ts(mk16, sp16, m1, None, ALU.is_ge)
            t16_ = mb(32, 48)
            tr.stt(t16_, mk16, -1e4, big16, ALU.mult, ALU.add)
            i1 = mt(14, 15)
            tr.reduce(i1, t16_, ALU.min)
            oh1 = mb(48, 64)
            tr.ts(oh1, cs.iota16[:], i1, None, ALU.is_equal)
            sp2 = mb(64, 80)
            tr.stt(sp2, oh1, -1e30, sp16, ALU.mult, ALU.add)
            m2 = mt(13, 14)
            tr.reduce(m2, sp2, ALU.max)
            tr.ts(mk16, sp2, m2, None, ALU.is_ge)
            tr.stt(t16_, mk16, -1e4, big16, ALU.mult, ALU.add)
            i2 = mt(15, 16)
            tr.reduce(i2, t16_, ALU.min)
            oh2 = mb(80, 96)
            tr.ts(oh2, cs.iota16[:], i2, None, ALU.is_equal)
            wtmp = mb(96, 112)
            tr.tt(wtmp, orig, oh1, ALU.mult)
            w1 = mt(13, 14)
            tr.reduce(w1, wtmp, ALU.add)
            tr.tt(wtmp, orig, oh2, ALU.mult)
            w2 = mt(14, 15)
            tr.reduce(w2, wtmp, ALU.add)
            Gc = mb(112, 128)
            tr.ts(Gc, oh1, w1, None, ALU.mult)
            Gc2 = mb(96, 112)
            tr.ts(Gc2, oh2, w2, None, ALU.mult)
            tr.tt(Gc, Gc, Gc2, ALU.add)
            tr.transpose((pb[7][0:16, 0:128], "pb7"), Gc, cs.ident128[:])
            tr.s_copy((sh.g8[:], "sh_g8"), (pb[7][0:16, 0:128], "pb7"))
            tr.dma(g1_d[:, ti * 128:(ti + 1) * 128], (sh.g8[:], "sh_g8"),
                   queue="scalar")


def emit_l1f_experts(tr, cs, sh, pb, w, wb, zsrc, g1_d, yacc_d):
    for e in range(16):
        be1c = load_bias_cols(tr, cs, w["L1_ffn_be1"][e, :], 512, pb[6], sh, 0)
        be3c = load_bias_cols(tr, cs, w["L1_ffn_be3"][e, :], 512, pb[6], sh, 4)
        be2c = load_bias_cols(tr, cs, w["L1_ffn_be2"][e, :], C, pb[7], sh, 8)
        for c in range(NCH):
            tr.dma((sh.WA[:, c * 512:(c + 1) * 512], "WA"),
                   wb["L1_ffn_We1"][e * 1024 + c * 128:e * 1024 + (c + 1) * 128, :],
                   queue="scalar")
            tr.dma((sh.WA[:, 4096 + c * 512:4096 + (c + 1) * 512], "WA"),
                   wb["L1_ffn_We3"][e * 1024 + c * 128:e * 1024 + (c + 1) * 128, :],
                   queue="scalar")
        for ic in range(4):
            tr.dma((sh.WB[:, ic * 1024:(ic + 1) * 1024], "WB"),
                   wb["L1_ffn_We2"][e * 512 + ic * 128:e * 512 + (ic + 1) * 128, :],
                   queue="scalar")
        for tb in range(NTB):
            cols = slice(tb * 512, (tb + 1) * 512)
            for c in range(NCH):
                tr.dma(sh.zb[c][:], zsrc[c * 128:(c + 1) * 128, cols])
            gate = sh.g[2]
            tr.dma(gate[:], g1_d[e:e + 1, cols].broadcast_to([128, 512]))
            for ic in range(4):
                h1p = (pb[0][:], "pb0")
                h3p = (pb[1][:], "pb1")
                for c in range(NCH):
                    tr.mm(h1p, (sh.WA[:, c * 512 + ic * 128:c * 512 + (ic + 1) * 128], "WA"),
                          sh.zb[c][:], start=(c == 0), stop=(c == 7))
                for c in range(NCH):
                    tr.mm(h3p, (sh.WA[:, 4096 + c * 512 + ic * 128:4096 + c * 512 + (ic + 1) * 128],
                                "WA"), sh.zb[c][:], start=(c == 0), stop=(c == 7))
                tr.act(sh.b16[0][:], h1p, AF.Silu, bias=be1c[:, ic:ic + 1])
                tr.act(sh.b16[1][:], h3p, AF.Identity, bias=be3c[:, ic:ic + 1])
                tr.tt(pt_tile(sh, ic), sh.b16[0][:], sh.b16[1][:], ALU.mult)
            for dc in range(NCH):
                op_ = (pb[2 + dc % 2][:], f"pb{2 + dc % 2}")
                for ic in range(4):
                    tr.mm(op_, (sh.WB[:, ic * 1024 + dc * 128:ic * 1024 + (dc + 1) * 128], "WB"),
                          pt_tile(sh, ic), start=(ic == 0), stop=(ic == 3))
                g0 = sh.g[dc % 2]
                tr.stt(g0[:], op_, be2c[:, dc:dc + 1], gate[:], ALU.add, ALU.mult)
                tr.dma(yacc_d[dc * 128:(dc + 1) * 128, cols], g0[:], queue="gpsimd",
                       accum_op=(ALU.add if e > 0 else None))


def emit_l1f_shared_ab(tr, cs, sh, pb, w, wb, zsrc, ab_d):
    for c in range(NCH):
        tr.dma((sh.WA[:, c * 1024:(c + 1) * 1024], "WA"),
               wb["L1_ffn_ws1"][c * 128:(c + 1) * 128, :], queue="scalar")
        tr.dma((sh.WB[:, c * 1024:(c + 1) * 1024], "WB"),
               wb["L1_ffn_ws3"][c * 128:(c + 1) * 128, :], queue="scalar")
    bs1c = load_bias_cols(tr, cs, w["L1_ffn_bs1"], C, pb[6], sh, 0)
    bs3c = load_bias_cols(tr, cs, w["L1_ffn_bs3"], C, pb[7], sh, 8)
    for tb in range(NTB):
        cols = slice(tb * 512, (tb + 1) * 512)
        for c in range(NCH):
            tr.dma(sh.zb[c][:], zsrc[c * 128:(c + 1) * 128, cols])
        for dc in range(NCH):
            ap_ = (pb[dc % 2][:], f"pb{dc % 2}")
            bp_ = (pb[2 + dc % 2][:], f"pb{2 + dc % 2}")
            for c in range(NCH):
                tr.mm(ap_, (sh.WA[:, c * 1024 + dc * 128:c * 1024 + (dc + 1) * 128], "WA"),
                      sh.zb[c][:], start=(c == 0), stop=(c == 7))
            for c in range(NCH):
                tr.mm(bp_, (sh.WB[:, c * 1024 + dc * 128:c * 1024 + (dc + 1) * 128], "WB"),
                      sh.zb[c][:], start=(c == 0), stop=(c == 7))
            tr.act(sh.b16[0][:], ap_, AF.Silu, bias=bs1c[:, dc:dc + 1])
            tr.act(sh.b16[1][:], bp_, AF.Identity, bias=bs3c[:, dc:dc + 1])
            tr.tt(sh.b16[2][:], sh.b16[0][:], sh.b16[1][:], ALU.mult)
            tr.dma(ab_d[dc * 128:(dc + 1) * 128, cols], sh.b16[2][:])


def emit_l1f_final(tr, cs, sh, pb, w, wb, ab_d, yacc_d, xfm3, xfm4):
    for c in range(NCH):
        tr.dma((sh.WA[:, c * 1024:(c + 1) * 1024], "WA"),
               wb["L1_ffn_ws2"][c * 128:(c + 1) * 128, :], queue="scalar")
    bs2c = load_bias_cols(tr, cs, w["L1_ffn_bs2"], C, pb[6], sh, 0)
    for tb in range(NTB):
        cols = slice(tb * 512, (tb + 1) * 512)
        for c in range(NCH):
            tr.dma(sh.zb[c][:], ab_d[c * 128:(c + 1) * 128, cols])
        for dc in range(NCH):
            op_ = (pb[dc % 2][:], f"pb{dc % 2}")
            for c in range(NCH):
                tr.mm(op_, (sh.WA[:, c * 1024 + dc * 128:c * 1024 + (dc + 1) * 128], "WA"),
                      sh.zb[c][:], start=(c == 0), stop=(c == 7))
            g0 = sh.g[dc % 2]
            tr.act(g0[:], op_, AF.Identity, bias=bs2c[:, dc:dc + 1])
            g1 = sh.g[2 + dc % 2]
            tr.dma(g1[:], yacc_d[dc * 128:(dc + 1) * 128, cols])
            tr.tt(g0[:], g0[:], g1[:], ALU.add)
            tr.dma(g1[:], xfm3[dc * 128:(dc + 1) * 128, cols], queue="scalar")
            tr.tt(g0[:], g0[:], g1[:], ALU.add)
            tr.dma(xfm4[dc * 128:(dc + 1) * 128, cols], g0[:], queue="gpsimd")


WEIGHT_SPECS = {
    "L0_attn_dn_g": [C], "L0_attn_dn_b": [C],
    "L0_attn_gates_w": [C, 3 * C], "L0_attn_gates_b": [3 * C],
    "L0_attn_lr_w": [C, 1], "L0_attn_lr_b": [1],
    "L0_attn_out_w": [C, C], "L0_attn_out_b": [C],
    "L0_attn_mn_g": [C], "L0_attn_mn_b": [C],
    "L0_attn_w1": [C, C], "L0_attn_b1": [C],
    "L0_attn_w2": [C, C], "L0_attn_b2": [C],
    "L0_ffn_Wq": [C, C], "L0_ffn_Wk": [C, C], "L0_ffn_Wv": [C, C],
    "L0_ffn_Wo": [C, C], "L0_ffn_bo": [C], "L0_ffn_Wr": [C, 8],
    "L0_ffn_We1": [8, C, 4096], "L0_ffn_be1": [8, 4096],
    "L0_ffn_We2": [8, 4096, C], "L0_ffn_be2": [8, C],
    "L0_ffn_ln1_g": [C], "L0_ffn_ln1_b": [C],
    "L0_ffn_ln2_g": [C], "L0_ffn_ln2_b": [C],
    "L1_attn_Wq": [C, C], "L1_attn_bq": [C], "L1_attn_Wk": [C, C],
    "L1_attn_bk": [C], "L1_attn_Wv": [C, C], "L1_attn_bv": [C],
    "L1_attn_Wo": [C, C], "L1_attn_bo": [C],
    "L1_ffn_gate_w": [16, C], "L1_ffn_gate_b": [16],
    "L1_ffn_We1": [16, C, 512], "L1_ffn_be1": [16, 512],
    "L1_ffn_We2": [16, 512, C], "L1_ffn_be2": [16, C],
    "L1_ffn_We3": [16, C, 512], "L1_ffn_be3": [16, 512],
    "L1_ffn_ws1": [C, C], "L1_ffn_bs1": [C],
    "L1_ffn_ws2": [C, C], "L1_ffn_bs2": [C],
    "L1_ffn_ws3": [C, C], "L1_ffn_bs3": [C],
}

N_STAGES = 1


def build_program(taps=(), stages=99):
    nc = bass.Bass("TRN2", target_bir_lowering=False)
    tr = TR(nc)
    pbig = nc.alloc_psum_tensor("pbig", [128, 4096], F32)
    pb = [pbig[:, i * 512:(i + 1) * 512] for i in range(8)]
    cs = Consts(tr)
    sh = SH(tr)

    class LazyW(dict):
        def __missing__(self, name):
            ap = nc.dram_tensor(name, list(WEIGHT_SPECS[name]), F32,
                                kind="ExternalInput").ap()
            self[name] = ap
            return ap

    w = LazyW()
    w["x"] = nc.dram_tensor("x", [T, C], F32, kind="ExternalInput").ap()
    wb = {}
    for name, (rows, cols) in [(n, (s[0], s[1])) for n, s in PRECAST]:
        full = WEIGHT_SPECS[name]
        wb[name] = nc.dram_tensor("b16_" + name, [rows, cols], BF16,
                                  kind="Internal").ap()

    out = nc.dram_tensor("out", [T, C], F32, kind="ExternalOutput").ap()

    scratch = {}

    def dscr(name, shape, dtype=F32):
        ap = nc.dram_tensor(name, list(shape), dtype, kind="Internal").ap()
        scratch[name] = ap
        return ap

    xfm0 = dscr("xfm0", [C, T])
    xln = dscr("xln", [C, T])
    zbf = dscr("zbf", [C, T], BF16)
    qfm_d = dscr("qfm_d", [C, T], BF16)
    kfm_d = dscr("kfm_d", [C, T], BF16)
    ktm_d = dscr("ktm_d", [T, C], BF16)
    vtm_d = dscr("vtm_d", [T, C], BF16)
    x1a_d = dscr("x1a_d", [C, T])
    z2bf = dscr("z2bf", [C, T], BF16)
    xfm1 = dscr("xfm1", [C, T])

    emit_precast(tr, sh, w, wb)
    emit_entry_transpose(tr, cs, sh, pb, w["x"], xfm0)
    final = xfm0
    if stages >= 2:
        emit_layernorm(tr, cs, sh, pb, xfm0, [(xln, F32)], 1e-5)
        emit_layernorm(tr, cs, sh, pb, xln, [(zbf, BF16)], 1e-5,
                       gamma=w["L0_attn_dn_g"], beta=w["L0_attn_dn_b"])
        final = xln
    if stages >= 3:
        emit_l0a_gates(tr, cs, sh, pb, w, wb, zbf, qfm_d, kfm_d, ktm_d, vtm_d)
    if stages >= 4:
        emit_l0a_recurrence(tr, cs, sh, pb, kfm_d, ktm_d, vtm_d,
                            rpart=globals().get("_rpart", 9))
    if stages >= 5:
        emit_l0a_output(tr, cs, sh, pb, w, wb, qfm_d, xln, x1a_d)
        final = x1a_d
    if stages >= 6:
        emit_layernorm(tr, cs, sh, pb, x1a_d, [(z2bf, BF16)], 1e-5,
                       gamma=w["L0_attn_mn_g"], beta=w["L0_attn_mn_b"])
        emit_l0a_mlp(tr, cs, sh, pb, w, wb, z2bf, x1a_d, xfm0, xfm1)
        final = xfm1

    if stages >= 7:
        x2ln = dscr("x2ln", [C, T])
        qfm2_d = dscr("qfm2_d", [C, T], BF16)
        kfm2_d = dscr("kfm2_d", [C, T], BF16)
        emit_layernorm(tr, cs, sh, pb, xfm1, [(x2ln, F32), (zbf, BF16)], 1e-5)
        emit_l0f_qkv(tr, cs, sh, pb, w, wb, zbf, qfm2_d, kfm2_d)
    if stages >= 8:
        ctx_d = dscr("ctx_d", [C, T], BF16)
        emit_l0f_attn(tr, cs, sh, pb, qfm2_d, kfm2_d, ctx_d)
    if stages >= 9:
        x2a_d = dscr("x2a_d", [C, T])
        out1f_d = dscr("out1f_d", [C, T])
        out1b_d = dscr("out1b_d", [C, T], BF16)
        emit_fm_proj(tr, cs, sh, pb, wb["L0_ffn_Wo"], w["L0_ffn_bo"], ctx_d,
                     x2a_d, res_fm=x2ln)
        emit_layernorm(tr, cs, sh, pb, x2a_d, [(out1f_d, F32), (out1b_d, BF16)],
                       1e-6, gamma=w["L0_ffn_ln1_g"], beta=w["L0_ffn_ln1_b"])
        final = out1f_d
    if stages >= 10:
        g_d = dscr("g_d", [8, T])
        emit_l0f_router(tr, cs, sh, pb, w, out1b_d, g_d)
    if stages >= 11:
        acc_d = dscr("acc_d", [C, T])
        emit_l0f_moe(tr, cs, sh, pb, w, wb, out1b_d, g_d, acc_d)
        final = acc_d
    if stages >= 12:
        x2c_d = dscr("x2c_d", [C, T])
        xfm2 = dscr("xfm2", [C, T])
        emit_add_fm(tr, cs, sh, pb, out1f_d, acc_d, x2c_d)
        emit_layernorm(tr, cs, sh, pb, x2c_d, [], 1e-6,
                       gamma=w["L0_ffn_ln2_g"], beta=w["L0_ffn_ln2_b"],
                       res_fm=xfm1, res_dst=xfm2)
        final = xfm2

    if stages >= 13:
        nbs = sh.col[:, 34:50]
        statall = sh.A2[:, 1040:1552]
        qfm3_d = dscr("qfm3_d", [C, T], BF16)
        kfm3_d = dscr("kfm3_d", [C, T], BF16)
        emit_layernorm(tr, cs, sh, pb, xfm2, [(zbf, BF16)], 1e-5)
        emit_fm_proj(tr, cs, sh, pb, wb["L1_attn_Wq"], w["L1_attn_bq"], zbf,
                     qfm3_d, dst_dtype=BF16)
        emit_fm_proj(tr, cs, sh, pb, wb["L1_attn_Wk"], w["L1_attn_bk"], zbf,
                     kfm3_d, dst_dtype=BF16, wslab=(sh.WB, "WB"), bslot=8)
        emit_l1a_qkv_v(tr, cs, sh, pb, w, wb, zbf)
        final = qfm3_d
    if stages >= 14:
        mask_td = dscr("mask_td", [512, 128])
        emit_l1a_stats(tr, cs, sh, pb, qfm3_d, kfm3_d, statall, nbs)
        emit_l1a_select(tr, cs, sh, pb, statall, mask_td)
    if stages >= 15:
        ctx3_d = dscr("ctx3_d", [C, T], BF16)
        xfm3 = dscr("xfm3", [C, T])
        emit_l1a_attn(tr, cs, sh, pb, qfm3_d, kfm3_d, mask_td, nbs, ctx3_d)
        emit_fm_proj(tr, cs, sh, pb, wb["L1_attn_Wo"], w["L1_attn_bo"], ctx3_d,
                     xfm3, res_fm=xfm2)
        final = xfm3
    if stages >= 16:
        g1_d = dscr("g1_d", [16, T])
        yacc_d = dscr("yacc_d", [C, T])
        emit_layernorm(tr, cs, sh, pb, xfm3, [(zbf, BF16)], 1e-5)
        emit_l1f_gate(tr, cs, sh, pb, w, zbf, g1_d)
    if stages >= 17:
        emit_l1f_experts(tr, cs, sh, pb, w, wb, zbf, g1_d, yacc_d)
        final = yacc_d
    if stages >= 18:
        ab_d = dscr("ab_d", [C, T], BF16)
        xfm4 = dscr("xfm4", [C, T])
        emit_l1f_shared_ab(tr, cs, sh, pb, w, wb, zbf, ab_d)
        emit_l1f_final(tr, cs, sh, pb, w, wb, ab_d, yacc_d, xfm3, xfm4)
        final = xfm4

    emit_exit_transpose(tr, cs, sh, pb, final, out)

    if taps:
        for tname in taps:
            ap = scratch[tname]
            tout = nc.dram_tensor("tap_" + tname, list(ap.shape), ap.dtype,
                                  kind="ExternalOutput").ap()
            tr.dma(tout, ap)

    tr.finalize_and_emit()
    used = [k for k in w if k != "x"]
    return nc, used


def kernel(**inputs):
    x = np.ascontiguousarray(inputs["L0_attn_x"]).astype(np.float32, copy=False)
    g = globals()
    nc, used = build_program(taps=g.get("_DEBUG_TAPS", ()), stages=g.get("_stages", 99))
    core_ids = g.get("_cores", None) or list(range(8))
    in_maps = []
    for core in core_ids:
        m = {"x": x[core]}
        for name in used:
            m[name] = np.ascontiguousarray(inputs[name]).astype(np.float32, copy=False)
        in_maps.append(m)
    res = run_bass_kernel_spmd(nc, in_maps, core_ids=core_ids)
    taps = g.get("_DEBUG_TAPS", ())
    if taps:
        g["_taps"] = [{t: r["tap_" + t] for t in taps} for r in res.results]
    return np.stack([r["out"] for r in res.results], axis=0)
